# version 07
# baseline (speedup 1.0000x reference)
import math
import numpy as np
from contextlib import ExitStack, contextmanager

import concourse.bass as bass
import concourse.mybir as mybir
from concourse.bass_utils import run_bass_kernel_spmd

F32 = mybir.dt.float32
BF16 = mybir.dt.bfloat16
AF = mybir.ActivationFunctionType
ALU = mybir.AluOpType
AX = mybir.AxisListType

NCORES = 8
S = 4096
D = 1024
NT = S // 128
TB = 512
NB = S // TB
TPB = TB // 128
LN_EPS = 1e-5

ENGS = ("sync", "tensor", "vector", "scalar", "gpsimd")
DEBUG = False
PARANOID = False
_DBG = {}


def dbg_dump(p, name, sap, shape, dt, wait):
    if not DEBUG:
        return
    if "dbgs" not in p._sems:
        p.sem("dbgs")
    t = p.nc.dram_tensor("dbg_" + name, list(shape), dt, kind="ExternalOutput")
    w = list(wait) + [("dbgs", p.cnt("dbgs"))]
    p.emit("gpsimd", lambda E: E.dma_start(t.ap(), sap), wait=w,
           inc="dbgs", inck=16)
    p.emit("gpsimd", lambda E: E.nop(), wait=[("dbgs", p.cnt("dbgs"))])


def host_constants():
    c = {}
    c["ident"] = np.eye(128, dtype=np.float32)
    j = np.arange(32, dtype=np.float64)
    inv = 10000.0 ** (-2.0 * j / 64.0)
    t = np.arange(S, dtype=np.float64)
    ang = np.outer(inv, t)
    cos = np.cos(ang)
    sin = np.sin(ang)
    c["rope_c"] = np.tile(cos, (4, 1)).astype(np.float32)
    c["rope_s"] = np.concatenate([-sin, sin, -sin, sin], 0).astype(np.float32)
    N = D
    e = np.arange(N, dtype=np.float64)
    k = np.arange(N, dtype=np.float64)
    Df = np.empty((N, N), dtype=np.float64)
    Df[:, 0:512] = np.cos(2 * np.pi * np.outer(e, k[0:512]) / N)
    Df[:, 512] = np.cos(np.pi * e)
    Df[:, 513:1024] = -np.sin(2 * np.pi * np.outer(e, k[1:512]) / N)
    c["dft_f"] = Df.astype(np.float32)
    Di = np.empty((N, N), dtype=np.float64)
    n = np.arange(N, dtype=np.float64)
    Di[0, :] = 1.0 / N
    Di[1:512, :] = 2.0 * np.cos(2 * np.pi * np.outer(k[1:512], n) / N) / N
    Di[512, :] = np.cos(np.pi * n) / N
    Di[513:1024, :] = -2.0 * np.sin(2 * np.pi * np.outer(k[1:512], n) / N) / N
    c["dft_i"] = Di.astype(np.float32)
    return c


def rope_perm():
    p = []
    for h in range(16):
        base = h * 64
        p += [base + 2 * i for i in range(32)]
        p += [base + 2 * i + 1 for i in range(32)]
    return np.array(p, dtype=np.int64)


class P:
    def __init__(self, nc, stack):
        self.nc = nc
        self.prog_stack = stack
        self.stack = stack
        self.q = {e: [] for e in ENGS}
        self._sems = {}
        self.sem("bar")

    def sem(self, name):
        h = self.prog_stack.enter_context(self.nc.semaphore(name))
        self._sems[name] = [h, 0]
        return name

    def h(self, name):
        return self._sems[name][0]

    def bump(self, name, k=1):
        self._sems[name][1] += k
        return self._sems[name][1]

    def cnt(self, name):
        return self._sems[name][1]

    def on(self, eng, fn):
        self.q[eng].append(fn)

    def emit(self, eng, fn, wait=(), inc=None, inck=1):
        pairs = [(self.h(s), v) for (s, v) in wait if v > 0]
        hinc = self.h(inc) if inc else None
        if inc:
            self.bump(inc, inck)

        def f(E, pairs=pairs, fn=fn, hinc=hinc, inck=inck):
            for h, v in pairs:
                E.wait_ge(h, v)
            ins = fn(E)
            if hinc is not None:
                ins.then_inc(hinc, inck)
        self.on(eng, f)
        return self.cnt(inc) if inc else None

    def dma_barrier(self, sem):
        self.emit("sync", lambda E: E.nop(), wait=[(sem, self.cnt(sem))])

    def fence(self, eng, sem):
        self.emit(eng, lambda E: E.drain())
        self.emit(eng, lambda E: E.nop(), inc=sem)
        return self.cnt(sem)

    def barrier(self):
        tgt = self.bump("bar", len(ENGS))
        h = self.h("bar")
        for e in ENGS:
            def f(E, h=h, t=tgt, e=e):
                if e in ("vector", "scalar", "gpsimd"):
                    E.drain()
                E.nop().then_inc(h, 1)
                E.wait_ge(h, t)
            self.on(e, f)

    def flush(self):
        with self.nc.Block() as blk:
            for e in ENGS:
                getattr(blk, e)(lambda E, fns=tuple(self.q[e]): [f(E) for f in fns])
        self.q = {e: [] for e in ENGS}

    def sb(self, name, shape, dt, side=None):
        return self.stack.enter_context(
            self.nc.sbuf_tensor(name, shape, dt, side=side))

    def ps(self, name, shape, dt=F32):
        return self.stack.enter_context(self.nc.psum_tensor(name, shape, dt))

    @contextmanager
    def scope(self):
        old = self.stack
        with ExitStack() as st:
            self.stack = st
            try:
                yield
            finally:
                self.stack = old


class Common:

    def __init__(self, p, ident_ap):
        self.ident_f = p.sb("c_ident_f", [128, 128], F32)
        self.ident_b = p.sb("c_ident_b", [128, 128], BF16)
        self.ones11 = p.sb("c_ones11", [1, 1], F32)
        self.ones_col = p.sb("c_ones_col", [128, 1], F32)
        ld = p.sem("c_ld")
        p.emit("sync", lambda E: E.dma_start(self.ident_f[:], ident_ap),
               inc=ld, inck=16)
        p.emit("vector", lambda E: E.tensor_copy(self.ident_b[:], self.ident_f[:]),
               wait=[(ld, 16)])
        p.emit("gpsimd", lambda E: E.memset(self.ones11[:], 1.0))
        p.emit("gpsimd", lambda E: E.memset(self.ones_col[:], 1.0))
        p.barrier()


def load_weight_bf16(p, name, dram_ap, kdim, mdim, s_ld2, s_cv, stg):
    kc = kdim // 128
    wsb = p.sb(name, [128, kc * mdim], BF16)
    base_cv = p.cnt(s_cv)
    for k in range(kc):
        b = k % 2
        if k >= 2:
            wait = [(s_cv, base_cv + k - 1)]
        else:
            wait = [(s_cv, base_cv)]
        src = dram_ap[k * 128:(k + 1) * 128, :]
        p.emit("sync", lambda E, d=stg[:, b * mdim:(b + 1) * mdim], s=src:
               E.dma_start(d, s), wait=wait, inc=s_ld2[b], inck=16)
        p.emit("vector", lambda E, d=wsb[:, k * mdim:(k + 1) * mdim],
               s=stg[:, b * mdim:(b + 1) * mdim]: E.tensor_copy(d, s),
               wait=[(s_ld2[b], p.cnt(s_ld2[b]))], inc=s_cv)
    return wsb


def row_to_cols(p, com, name, row_ap_getter, nchunks, psum, s_mm, s_dv,
                dt=F32):
    out = p.sb(name, [128, nchunks], dt)
    for c in range(nchunks):
        p.emit("tensor", lambda E, o=psum[:, c:c + 1], l=row_ap_getter(c):
               E.matmul(o, l, com.ones11[:], start=True, stop=True), inc=s_mm)
    gv = p.fence("tensor", s_mm)
    p.emit("vector", lambda E, d=out[:, 0:nchunks], s=psum[:, 0:nchunks]:
           E.tensor_copy(d, s), wait=[(s_mm, gv)], inc=s_dv)
    return out


class LNBufs:
    def __init__(self, p, pref, sqd=True, xg=True, xo=True):
        self.xg = [p.sb(f"{pref}_xg{i}", [128, TPB * D], F32)
                   for i in range(2)] if xg else None
        self.xn_tok = p.sb(f"{pref}_xnt", [128, TPB * D], F32)
        self.stats_s = p.sb(f"{pref}_ss", [128, NT], F32)
        self.stats_q = p.sb(f"{pref}_sq", [128, NT], F32)
        self.stats_m = p.sb(f"{pref}_sm", [128, NT], F32)
        self.stats_r = p.sb(f"{pref}_sr", [128, NT], F32)
        self.sqd = p.sb(f"{pref}_sqd", [128, D], BF16) if sqd else None
        self.fstats_s = p.sb(f"{pref}_fs", [128, NT], F32)
        self.fstats_q = p.sb(f"{pref}_fq", [128, NT], F32)
        self.fstats_m = p.sb(f"{pref}_fm", [128, NT], F32)
        self.fstats_r = p.sb(f"{pref}_fr", [128, NT], F32)
        self.xo = [p.sb(f"{pref}_xo{i}", [128, TPB * D], F32)
                   for i in range(2)] if xo else None


def ln_finalize(p, ss, sq, sm, sr, c0, c1, wait, s_dv, s_ac):
    sl = lambda t: t[:, c0:c1]
    p.emit("vector", lambda E: E.tensor_scalar_mul(sl(sm), sl(ss), 1.0 / D),
           wait=wait, inc=s_dv)
    p.emit("vector", lambda E: E.drain())
    p.emit("vector", lambda E: E.tensor_mul(sl(sr), sl(sm), sl(sm)), inc=s_dv)
    p.emit("vector", lambda E: E.drain())
    p.emit("vector", lambda E: E.scalar_tensor_tensor(
        sl(sr), sl(sq), 1.0 / D, sl(sr), ALU.mult, ALU.subtract), inc=s_dv)
    p.emit("vector", lambda E: E.drain())
    p.emit("vector", lambda E: E.tensor_scalar_add(sl(sr), sl(sr), LN_EPS),
           inc=s_dv)
    p.emit("scalar", lambda E: E.activation(sl(sr), sl(sr), AF.Sqrt),
           wait=[(s_dv, p.cnt(s_dv))], inc=s_ac)
    p.emit("vector", lambda E: E.reciprocal(sl(sr), sl(sr)),
           wait=[(s_ac, p.cnt(s_ac))], inc=s_dv)
    p.emit("vector", lambda E: E.drain())
    return p.cnt(s_dv)


class TRing:

    def __init__(self, p, pref):
        self.banks = [p.ps(pref + f"_ptr{i}", [128, 512], F32) for i in range(2)]
        self.free = [0, 0]
        self.idx = 0

    def next(self):
        i = self.idx % 2
        self.idx += 1
        return i, self.banks[i], self.free[i]


def transpose_group(p, com, tring, srcs, dst_ap, s_mm, s_dv, dwait=(),
                    fp32=True, fence=False):
    i, bank, freev = tring.next()
    idn = com.ident_f[:] if fp32 else com.ident_b[:]
    for j, sap in enumerate(srcs):
        w = []
        if j == 0:
            w = list(dwait) + ([("__dv__", freev)] if freev > 0 else [])
            w = [(s_dv if s == "__dv__" else s, v) for (s, v) in w]
        p.emit("tensor", lambda E, o=bank[:, j * 128:(j + 1) * 128], s=sap,
               idn=idn: E.transpose(o, s, idn), wait=w, inc=s_mm)
    n = len(srcs)
    gv = p.fence("tensor", s_mm)
    p.emit("vector", lambda E, d=dst_ap, s=bank[:, 0:n * 128]:
           E.tensor_copy(d, s), wait=[(s_mm, gv)], inc=s_dv)
    tring.free[i] = p.cnt(s_dv)
    if fence:
        return p.fence("vector", s_dv)
    return p.cnt(s_dv)


def build_l0_ffn(p, com, x_ap, y_ap, w, pref="f0"):
    RANK, DFF = 256, 4096

    s_ld = [p.sem(pref + "_ld0"), p.sem(pref + "_ld1")]
    s_st = [p.sem(pref + "_st0"), p.sem(pref + "_st1")]
    s_dv = p.sem(pref + "_dv")
    s_ac = p.sem(pref + "_ac")
    s_mm = p.sem(pref + "_mm")
    s_wld = p.sem(pref + "_wld")
    s_wl2 = [p.sem(pref + "_wl0"), p.sem(pref + "_wl1")]
    s_wcv = p.sem(pref + "_wcv")

    tring = TRing(p, pref)
    p_a = p.ps(pref + "_pa", [128, 512], F32)
    p_b = p.ps(pref + "_pb", [128, 512], F32)
    p_h = [p.ps(pref + f"_ph{i}", [128, 512], F32) for i in range(2)]
    p_q = [p.ps(pref + f"_pq{i}", [128, 512], F32) for i in range(2)]
    ring = [p_a, p_b]

    wstack = ExitStack()
    stg = wstack.enter_context(p.nc.sbuf_tensor(pref + "_stg", [128, 2 * DFF],
                                                F32, side="right"))
    brow = wstack.enter_context(p.nc.sbuf_tensor(pref + "_brow", [1, DFF + D],
                                                 F32, side="right"))
    p.emit("sync", lambda E: E.dma_start(brow[0:1, 0:DFF], w["b1"]),
           inc=s_wld, inck=16)
    p.emit("sync", lambda E: E.dma_start(brow[0:1, DFF:DFF + D], w["b2"]),
           inc=s_wld, inck=16)
    p.dma_barrier(s_wld)
    p.barrier()
    u1 = load_weight_bf16(p, pref + "_u1", w["U1"], D, RANK, s_wl2, s_wcv, stg)
    v1 = load_weight_bf16(p, pref + "_v1", w["V1"], RANK, DFF, s_wl2, s_wcv, stg)
    u2 = load_weight_bf16(p, pref + "_u2", w["U2"], DFF, RANK, s_wl2, s_wcv, stg)
    v2 = load_weight_bf16(p, pref + "_v2", w["V2"], RANK, D, s_wl2, s_wcv, stg)
    p.barrier()
    b1c = row_to_cols(p, com, pref + "_b1c",
                      lambda c: brow[0:1, c * 128:(c + 1) * 128], 32,
                      tring.banks[0][:, 0:32], s_mm, s_dv)
    p.barrier()
    b2c = row_to_cols(p, com, pref + "_b2c",
                      lambda c: brow[0:1, DFF + c * 128:DFF + (c + 1) * 128], 8,
                      tring.banks[0][:, 0:8], s_mm, s_dv)
    p.barrier()
    wstack.close()
    dbg_dump(p, "v2e", v2[:], [128, 2 * D], BF16, [])
    p.barrier()

    L = LNBufs(p, pref)
    xnT = [p.sb(pref + f"_xnT{i}", [128, 8 * TB], BF16) for i in range(2)]
    ptb = p.sb(pref + "_ptb", [128, 2 * TB], BF16)
    htb = p.sb(pref + "_htb", [128, 32 * TB], BF16)
    qtb = p.sb(pref + "_qtb", [128, 2 * TB], BF16)
    otf = p.sb(pref + "_otf", [128, 8 * TB], F32)
    ft = L.xn_tok

    xnT_consumed = {}
    xn_tr_mm = {}
    gelu_done = {}
    ht_read = {}
    ot_ac = {}
    ot_ac_last = {}
    q2_evac = {}
    bapply_done = {}
    store_done = {}

    pending_store = [None]

    def flush_store(p):
        if pending_store[0] is not None:
            fn = pending_store[0]
            pending_store[0] = None
            fn()

    for b in range(NB):
        bb = b % 2
        for t in range(TPB):
            gi = b * TPB + t
            wait = []
            if b >= 2:
                wait.append((s_dv, bapply_done[b - 2]))
            p.emit("sync", lambda E, d=L.xg[bb][:, t * D:(t + 1) * D],
                   s=x_ap[gi * 128:(gi + 1) * 128, :]: E.dma_start(d, s),
                   wait=wait, inc=s_ld[bb], inck=16)
        ldv = p.cnt(s_ld[bb])
        for t in range(TPB):
            gi = b * TPB + t
            xt = L.xg[bb][:, t * D:(t + 1) * D]
            ldw = [(s_ld[bb], ldv)]
            p.emit("vector", lambda E, x=xt, c=L.stats_s[:, gi:gi + 1]:
                   E.tensor_reduce(c, x, AX.X, ALU.add), wait=ldw, inc=s_dv)
            p.emit("scalar", lambda E, x=xt, c=L.stats_q[:, gi:gi + 1]:
                   E.activation(L.sqd[:], x, AF.Square, accum_out=c),
                   wait=ldw + [(s_ac, p.cnt(s_ac))], inc=s_ac)
        p.fence("vector", s_dv)
        p.fence("scalar", s_ac)
        finv = ln_finalize(p, L.stats_s, L.stats_q, L.stats_m, L.stats_r,
                           b * TPB, (b + 1) * TPB,
                           [(s_dv, p.cnt(s_dv)), (s_ac, p.cnt(s_ac))],
                           s_dv, s_ac)
        for t in range(TPB):
            gi = b * TPB + t
            wl = [(s_dv, finv)]
            if t == 0 and b >= 1:
                wl.append((s_mm, xn_tr_mm[b - 1]))
            p.emit("vector", lambda E, d=L.xn_tok[:, t * D:(t + 1) * D],
                   x=L.xg[bb][:, t * D:(t + 1) * D],
                   m=L.stats_m[:, gi:gi + 1], r=L.stats_r[:, gi:gi + 1]:
                   E.tensor_scalar(d, x, m, r, ALU.subtract, ALU.mult),
                   wait=wl, inc=s_dv)
        apply_done = p.fence("vector", s_dv)
        first = True
        for c in range(8):
            srcs = [L.xn_tok[:, t * D + c * 128:t * D + (c + 1) * 128]
                    for t in range(TPB)]
            dw = [(s_dv, apply_done)] if first else ()
            if first and b >= 2:
                dw = list(dw) + [(s_mm, xnT_consumed[b - 2])]
            transpose_group(p, com, tring, srcs,
                            xnT[bb][:, c * TB:(c + 1) * TB], s_mm, s_dv,
                            dwait=dw, fence=(c == 7))
            first = False
        xn_tr_mm[b] = p.cnt(s_mm)
        evxnT = p.cnt(s_dv)
        flush_store(p)
        if PARANOID:
            p.barrier()
        if b == 0:
            dbg_dump(p, "xnt", L.xn_tok[:], [128, TPB * D], F32,
                     [(s_dv, apply_done)])
            dbg_dump(p, "sts", L.stats_s[:], [128, NT], F32,
                     [(s_dv, apply_done)])
            dbg_dump(p, "stq", L.stats_q[:], [128, NT], F32,
                     [(s_dv, apply_done)])
            dbg_dump(p, "stm", L.stats_m[:], [128, NT], F32,
                     [(s_dv, apply_done)])
            dbg_dump(p, "str", L.stats_r[:], [128, NT], F32,
                     [(s_dv, apply_done)])
            dbg_dump(p, "xg0", L.xg[0][:], [128, TPB * D], F32,
                     [(s_dv, apply_done)])
            dbg_dump(p, "xnT", xnT[0][:], [128, 8 * TB], BF16,
                     [(s_dv, evxnT)])

        for rc in range(2):
            pr = ring[rc % 2]
            for k in range(8):
                wl = []
                if rc == 0 and k == 0:
                    wl = [(s_dv, evxnT)]
                    if b >= 1:
                        wl.append((s_ac, ot_ac_last[b - 1]))
                p.emit("tensor", lambda E, o=pr[:, 0:TB],
                       l=u1[:, k * RANK + rc * 128:k * RANK + (rc + 1) * 128],
                       r=xnT[bb][:, k * TB:(k + 1) * TB],
                       st=(k == 0), sp=(k == 7):
                       E.matmul(o, l, r, start=st, stop=sp), wait=wl, inc=s_mm)
            gv = p.fence("tensor", s_mm)
            p.emit("vector", lambda E, d=ptb[:, rc * TB:(rc + 1) * TB],
                   s=pr[:, 0:TB]: E.tensor_copy(d, s),
                   wait=[(s_mm, gv)], inc=s_dv)
        xnT_consumed[b] = p.cnt(s_mm)
        pt_ready = p.fence("vector", s_dv)
        if PARANOID:
            p.barrier()
        if b == 0:
            dbg_dump(p, "ptb", ptb[:], [128, 2 * TB], BF16, [(s_dv, pt_ready)])

        for hc in range(32):
            ph = p_h[hc % 2]
            wl = [(s_dv, pt_ready)] if hc < 2 else []
            if hc >= 2:
                wl.append((s_ac, gelu_done[(b, hc - 2)]))
            elif b >= 1:
                wl.append((s_ac, gelu_done[(b - 1, 30 + hc)]))
            for rc in range(2):
                p.emit("tensor", lambda E, o=ph[:, 0:TB],
                       l=v1[:, rc * DFF + hc * 128:rc * DFF + (hc + 1) * 128],
                       r=ptb[:, rc * TB:(rc + 1) * TB],
                       st=(rc == 0), sp=(rc == 1):
                       E.matmul(o, l, r, start=st, stop=sp),
                       wait=wl if rc == 0 else (), inc=s_mm)
            gv = p.fence("tensor", s_mm)
            p.emit("scalar", lambda E, d=htb[:, hc * TB:(hc + 1) * TB],
                   s=ph[:, 0:TB], bc=b1c[:, hc:hc + 1]:
                   E.activation(d, s, AF.Gelu_apprx_tanh, bias=bc),
                   wait=[(s_mm, gv)], inc=s_ac)
            gelu_done[(b, hc)] = p.fence("scalar", s_ac)
        for hc in range(32):
            for rc2 in range(2):
                wl = []
                if rc2 == 0:
                    wl.append((s_ac, gelu_done[(b, hc)]))
                if hc == 0 and rc2 == 0 and b >= 1:
                    wl.append((s_dv, q2_evac[b - 1]))
                p.emit("tensor", lambda E, o=p_q[rc2][:, 0:TB],
                       l=u2[:, hc * RANK + rc2 * 128:hc * RANK + (rc2 + 1) * 128],
                       r=htb[:, hc * TB:(hc + 1) * TB],
                       st=(hc == 0), sp=(hc == 31):
                       E.matmul(o, l, r, start=st, stop=sp), wait=wl, inc=s_mm)
            ht_read[(b, hc)] = p.cnt(s_mm)
        gv = p.fence("tensor", s_mm)
        for rc2 in range(2):
            p.emit("vector", lambda E, d=qtb[:, rc2 * TB:(rc2 + 1) * TB],
                   s=p_q[rc2][:, 0:TB]: E.tensor_copy(d, s),
                   wait=[(s_mm, gv)], inc=s_dv)
        q2_evac[b] = p.fence("vector", s_dv)
        if PARANOID:
            p.barrier()
        if b == 0:
            dbg_dump(p, "htb", htb[:], [128, 32 * TB], BF16,
                     [(s_ac, gelu_done[(b, 31)])])
            dbg_dump(p, "qtb", qtb[:], [128, 2 * TB], BF16,
                     [(s_dv, q2_evac[b])])

        for oc in range(8):
            pr = ring[oc % 2]
            for rc2 in range(2):
                wl = []
                if rc2 == 0:
                    wl = [(s_dv, q2_evac[b])]
                    if oc >= 2:
                        wl.append((s_ac, ot_ac[(b, oc - 2)]))
                p.emit("tensor", lambda E, o=pr[:, 0:TB],
                       l=v2[:, rc2 * D + oc * 128:rc2 * D + (oc + 1) * 128],
                       r=qtb[:, rc2 * TB:(rc2 + 1) * TB],
                       st=(rc2 == 0), sp=(rc2 == 1):
                       E.matmul(o, l, r, start=st, stop=sp), wait=wl, inc=s_mm)
            gv = p.fence("tensor", s_mm)
            p.emit("vector", lambda E, d=otf[:, oc * TB:(oc + 1) * TB],
                   s=pr[:, 0:TB], bc=b2c[:, oc:oc + 1]:
                   E.tensor_scalar_add(d, s, bc),
                   wait=[(s_mm, gv)], inc=s_dv)
            fv = p.fence("vector", s_dv)
            p.emit("scalar", lambda E: E.nop(), wait=[(s_dv, fv)], inc=s_ac)
            ot_ac[(b, oc)] = p.cnt(s_ac)
        ot_ac_last[b] = p.cnt(s_ac)
        if PARANOID:
            p.barrier()

        for t in range(TPB):
            for half in range(2):
                srcs = [otf[:, (half * 4 + cc) * TB + t * 128:
                        (half * 4 + cc) * TB + (t + 1) * 128] for cc in range(4)]
                transpose_group(p, com, tring, srcs,
                                ft[:, t * D + half * 512:t * D + (half + 1) * 512],
                                s_mm, s_dv,
                                dwait=[(s_ac, ot_ac[(b, half * 4 + 3)])],
                                fence=(t == TPB - 1 and half == 1))
        tr_done = p.cnt(s_dv)
        if b == 0:
            dbg_dump(p, "otf", otf[:], [128, 8 * TB], F32,
                     [(s_ac, ot_ac_last[b])])
            dbg_dump(p, "b2c", b2c[:], [128, 8], F32, [(s_ac, ot_ac_last[b])])
            dbg_dump(p, "v2w", v2[:], [128, 2 * D], BF16, [(s_ac, ot_ac_last[b])])
            dbg_dump(p, "qtb2", qtb[:], [128, 2 * TB], BF16, [(s_ac, ot_ac_last[b])])
            dbg_dump(p, "b1c", b1c[:], [128, 32], F32, [(s_ac, ot_ac_last[b])])
            dbg_dump(p, "ft", ft[:], [128, TPB * D], F32, [(s_dv, tr_done)])

        for t in range(TPB):
            gi = b * TPB + t
            ftt = ft[:, t * D:(t + 1) * D]
            p.emit("vector", lambda E, x=ftt, c=L.fstats_s[:, gi:gi + 1]:
                   E.tensor_reduce(c, x, AX.X, ALU.add),
                   wait=[(s_dv, tr_done)], inc=s_dv)
            p.emit("scalar", lambda E, x=ftt, c=L.fstats_q[:, gi:gi + 1]:
                   E.activation(L.sqd[:], x, AF.Square, accum_out=c),
                   wait=[(s_dv, tr_done), (s_ac, p.cnt(s_ac))], inc=s_ac)
        p.fence("vector", s_dv)
        p.fence("scalar", s_ac)
        ffv = ln_finalize(p, L.fstats_s, L.fstats_q, L.fstats_m, L.fstats_r,
                          b * TPB, (b + 1) * TPB,
                          [(s_dv, p.cnt(s_dv)), (s_ac, p.cnt(s_ac))],
                          s_dv, s_ac)
        for t in range(TPB):
            gi = b * TPB + t
            wl = [(s_dv, ffv)]
            if t == 0 and b >= 2:
                wl.append((s_st[bb], store_done[b - 2]))
            p.emit("vector", lambda E, d=L.xo[bb][:, t * D:(t + 1) * D],
                   f=ft[:, t * D:(t + 1) * D],
                   m=L.fstats_m[:, gi:gi + 1], r=L.fstats_r[:, gi:gi + 1]:
                   E.tensor_scalar(d, f, m, r, ALU.subtract, ALU.mult),
                   wait=wl, inc=s_dv)
            p.emit("vector", lambda E, d=L.xo[bb][:, t * D:(t + 1) * D],
                   x=L.xg[bb][:, t * D:(t + 1) * D]:
                   E.tensor_add(d, d, x), wait=[(s_dv, p.cnt(s_dv))], inc=s_dv)
        bapply_done[b] = p.fence("vector", s_dv)

        def mk_store(bv=bapply_done[b], b=b, bb=bb):
            settle = p.cnt(s_dv)
            for t in range(TPB):
                gi = b * TPB + t
                p.emit("sync", lambda E, d=y_ap[gi * 128:(gi + 1) * 128, :],
                       s=L.xo[bb][:, t * D:(t + 1) * D]:
                       E.dma_start(d, s),
                       wait=[(s_dv, max(bv, settle))] if t == 0 else (),
                       inc=s_st[bb], inck=16)
            store_done[b] = p.cnt(s_st[bb])
        pending_store[0] = mk_store
        store_done[b] = p.cnt(s_st[bb]) + 16 * TPB

    flush_store(p)
    p.emit("sync", lambda E: E.nop(), wait=[(s_st[0], p.cnt(s_st[0]))])
    p.emit("sync", lambda E: E.nop(), wait=[(s_st[1], p.cnt(s_st[1]))])
    p.barrier()


TWIST_SRC = [4, 5, 6, 7, 0, 1, 2, 3]
TWIST_SGN = [1.0, 1.0, -1.0, -1.0, -1.0, -1.0, 1.0, 1.0]


def build_l1_ffn(p, com, x_ap, y_ap, w, pref="g1"):
    s_ld = [p.sem(pref + "_ld0"), p.sem(pref + "_ld1")]
    s_st = [p.sem(pref + "_st0"), p.sem(pref + "_st1")]
    s_dv = p.sem(pref + "_dv")
    s_ac = p.sem(pref + "_ac")
    s_mm = p.sem(pref + "_mm")
    s_wld = p.sem(pref + "_wld")
    s_wl2 = [p.sem(pref + "_wl0"), p.sem(pref + "_wl1")]
    s_wcv = p.sem(pref + "_wcv")

    tring = TRing(p, pref)
    ring = [p.ps(pref + "_pa", [128, 512], F32),
            p.ps(pref + "_pb", [128, 512], F32)]
    p_tf = p.ps(pref + "_ptf", [128, 512], F32)
    p_cp = p.ps(pref + "_pcp", [128, 512], F32)

    OB1, OB2, OB3, OG1, OG2 = 0, 1024, 1536, 1792, 2304
    OB4, OG3, OAR = 2560, 2561, 2562
    brow = p.sb(pref + "_brow", [1, 2564], F32)
    for off, key, ln in ((OB1, "cn_b1", D), (OB2, "cn_b2", 512),
                         (OB3, "cn_b3", 256), (OG1, "gc_b1", 512),
                         (OG2, "gc_b2", 256), (OB4, "cn_b4", 1),
                         (OG3, "gc_b3", 1), (OAR, "ar", 1)):
        p.emit("sync", lambda E, d=brow[0:1, off:off + ln], s=w[key]:
               E.dma_start(d, s), inc=s_wld, inck=16)
    p.dma_barrier(s_wld)
    p.barrier()

    wstack = ExitStack()
    stg = wstack.enter_context(p.nc.sbuf_tensor(pref + "_stg", [128, 2048],
                                                F32, side="right"))
    w4sb = p.sb(pref + "_w4", [128, 2], BF16)
    g3sb = p.sb(pref + "_g3", [128, 2], BF16)
    for j, (key, col) in enumerate((("cn_w4", 0), ("cn_w4", 1),
                                    ("gc_w3", 2), ("gc_w3", 3))):
        ch = j % 2
        p.emit("sync", lambda E, d=stg[:, j:j + 1],
               sr=w[key][ch * 128:(ch + 1) * 128, :]:
               E.dma_start(d, sr), inc=s_wld, inck=16)
    p.dma_barrier(s_wld)
    p.barrier()
    p.emit("vector", lambda E: E.tensor_copy(w4sb[:, 0:2], stg[:, 0:2]),
           inc=s_wcv)
    p.emit("vector", lambda E: E.tensor_copy(g3sb[:, 0:2], stg[:, 2:4]),
           inc=s_wcv)
    p.fence("vector", s_wcv)
    w1sb = load_weight_bf16(p, pref + "_w1", w["cn_w1"], D, D, s_wl2, s_wcv, stg)
    w2sb = load_weight_bf16(p, pref + "_w2", w["cn_w2"], D, 512, s_wl2, s_wcv, stg)
    w3sb = load_weight_bf16(p, pref + "_w3", w["cn_w3"], 512, 256, s_wl2, s_wcv, stg)
    g1sb = load_weight_bf16(p, pref + "_g1", w["gc_w1"], D, 512, s_wl2, s_wcv, stg)
    g2sb = load_weight_bf16(p, pref + "_g2", w["gc_w2"], 512, 256, s_wl2, s_wcv, stg)
    p.barrier()
    b1c = row_to_cols(p, com, pref + "_b1c",
                      lambda c: brow[0:1, OB1 + c * 128:OB1 + (c + 1) * 128], 8,
                      tring.banks[0][:, 0:8], s_mm, s_dv)
    p.barrier()
    b2c = row_to_cols(p, com, pref + "_b2c",
                      lambda c: brow[0:1, OB2 + c * 128:OB2 + (c + 1) * 128], 4,
                      tring.banks[0][:, 0:4], s_mm, s_dv)
    p.barrier()
    b3c = row_to_cols(p, com, pref + "_b3c",
                      lambda c: brow[0:1, OB3 + c * 128:OB3 + (c + 1) * 128], 2,
                      tring.banks[0][:, 0:2], s_mm, s_dv)
    p.barrier()
    wstack.close()
    b4ap = brow[0:1, OB4:OB4 + 1]
    g3ap = brow[0:1, OG3:OG3 + 1]

    ones128b = p.sb(pref + "_ones128b", [1, 128], BF16)
    scr = p.sb(pref + "_scr", [1, 8], F32)
    p.emit("gpsimd", lambda E: E.memset(ones128b[:], 1.0))
    p.emit("vector", lambda E: E.tensor_scalar_mul(scr[0:1, 0:1],
                                                   brow[0:1, OAR:OAR + 1], 2.0))
    p.emit("vector", lambda E: E.drain())
    p.emit("vector", lambda E: E.scalar_tensor_tensor(
        scr[0:1, 1:2], brow[0:1, OAR:OAR + 1], -1.0, com.ones11[:],
        ALU.mult, ALU.bypass))
    p.emit("vector", lambda E: E.drain())
    p.emit("vector", lambda E: E.tensor_scalar_add(scr[0:1, 1:2],
                                                   scr[0:1, 1:2], 0.1))
    p.barrier()

    L = LNBufs(p, pref)
    xpre = [L.xg[0][:, 0:D], L.xg[0][:, D:2 * D]]
    outT = [p.sb(pref + f"_outT{i}", [128, 8 * TB], F32) for i in range(2)]
    outB = [p.sb(pref + f"_outB{i}", [128, 8 * TB], BF16) for i in range(2)]
    h1b = p.sb(pref + "_h1b", [128, 8 * TB], BF16)
    h2b = p.sb(pref + "_h2b", [128, 4 * TB], BF16)
    h3b = p.sb(pref + "_h3b", [128, 2 * TB], BF16)
    trow = p.sb(pref + "_trow", [1, 2 * TB], F32)
    crow_b = p.sb(pref + "_crowb", [1, 2 * TB], BF16)
    crow_f = p.sb(pref + "_crowf", [1, TB], F32)
    tw_scr = p.sb(pref + "_twscr", [128, TB], F32)
    gcols = p.sb(pref + "_gcols", [128, 14], F32)
    gmean = L.xo[0][0:1, 0:D + 8]
    grow = L.xo[1][0:1, 0:512 + 256 + 4]
    ft = L.xn_tok

    red_dv = {}
    sq_ac = {}
    for gi in range(NT):
        rb = gi % 2
        wl = []
        if gi >= 2:
            wl = [(s_dv, red_dv[gi - 2]), (s_ac, sq_ac[gi - 2])]
        p.emit("sync", lambda E, d=xpre[rb][:], s=x_ap[gi * 128:(gi + 1) * 128, :]:
               E.dma_start(d, s), wait=wl, inc=s_ld[rb], inck=16)
        ldw = [(s_ld[rb], p.cnt(s_ld[rb]))]
        p.emit("vector", lambda E, x=xpre[rb][:], c=L.stats_s[:, gi:gi + 1]:
               E.tensor_reduce(c, x, AX.X, ALU.add), wait=ldw, inc=s_dv)
        red_dv[gi] = p.cnt(s_dv)
        p.emit("scalar", lambda E, x=xpre[rb][:], c=L.stats_q[:, gi:gi + 1]:
               E.activation(L.sqd[:], x, AF.Square, accum_out=c),
               wait=ldw + [(s_ac, p.cnt(s_ac))], inc=s_ac)
        sq_ac[gi] = p.cnt(s_ac)
    p.fence("vector", s_dv)
    p.fence("scalar", s_ac)
    finv_pre = ln_finalize(p, L.stats_s, L.stats_q, L.stats_m, L.stats_r,
                           0, NT, [(s_dv, p.cnt(s_dv)), (s_ac, p.cnt(s_ac))],
                           s_dv, s_ac)
    p.barrier()
    dbg_dump(p, "l1_sts", L.stats_s[:], [128, NT], F32, [])
    dbg_dump(p, "l1_stq", L.stats_q[:], [128, NT], F32, [])
    dbg_dump(p, "l1_str", L.stats_r[:], [128, NT], F32, [])
    p.barrier()
    for gi in range(NT):
        rb = gi % 2
        wl = [(s_mm, p.cnt(s_mm))] if gi >= 2 else []
        p.emit("sync", lambda E, d=xpre[rb][:], s=x_ap[gi * 128:(gi + 1) * 128, :]:
               E.dma_start(d, s), wait=wl, inc=s_ld[rb], inck=16)
        for hh in range(2):
            p.emit("tensor", lambda E, o=ring[hh][0:1, 0:512],
                   l=L.stats_r[:, gi:gi + 1], r=xpre[rb][:, hh * 512:(hh + 1) * 512],
                   st=(gi == 0), sp=(gi == NT - 1):
                   E.matmul(o, l, r, start=st, stop=sp),
                   wait=[(s_ld[rb], p.cnt(s_ld[rb]))] if hh == 0 else (), inc=s_mm)
    p.emit("vector", lambda E: E.tensor_mul(gcols[:, 8:9], L.stats_m[:, 0:NT],
                                            L.stats_r[:, 0:NT])
           if False else E.tensor_mul(L.fstats_s[:, 0:NT], L.stats_m[:, 0:NT],
                                      L.stats_r[:, 0:NT]), inc=s_dv)
    p.emit("vector", lambda E: E.drain())
    p.emit("vector", lambda E: E.tensor_reduce(gcols[:, 8:9],
                                               L.fstats_s[:, 0:NT], AX.X, ALU.add),
           inc=s_dv)
    mrv = p.fence("vector", s_dv)
    p.emit("tensor", lambda E: E.matmul(p_tf[0:1, 0:1], com.ones_col[:],
                                        gcols[:, 8:9], start=True, stop=True),
           wait=[(s_dv, mrv)], inc=s_mm)
    gv = p.fence("tensor", s_mm)
    p.emit("vector", lambda E: E.tensor_copy(gmean[0:1, D:D + 1],
                                             p_tf[0:1, 0:1]),
           wait=[(s_mm, gv)], inc=s_dv)
    p.emit("vector", lambda E: E.drain())
    gv = p.fence("tensor", s_mm)
    for hh in range(2):
        p.emit("vector", lambda E, d=gmean[0:1, hh * 512:(hh + 1) * 512],
               sp=ring[hh][0:1, 0:512]:
               E.tensor_scalar(d, sp, gmean[0:1, D:D + 1], 1.0 / S,
                               ALU.subtract, ALU.mult),
               wait=[(s_mm, gv)], inc=s_dv)
    p.fence("vector", s_dv)
    p.barrier()

    gc8 = row_to_cols(p, com, pref + "_gc8",
                      lambda c: gmean[0:1, c * 128:(c + 1) * 128], 8,
                      tring.banks[0][:, 0:8], s_mm, s_dv, dt=BF16)
    p.barrier()
    for k in range(8):
        p.emit("tensor", lambda E, o=p_tf[0:1, 0:512],
               l=gc8[:, k:k + 1], r=g1sb[:, k * 512:(k + 1) * 512],
               st=(k == 0), sp=(k == 7):
               E.matmul(o, l, r, start=st, stop=sp), inc=s_mm)
    gv = p.fence("tensor", s_mm)
    p.emit("vector", lambda E: E.tensor_add(grow[0:1, 0:512], p_tf[0:1, 0:512],
                                            brow[0:1, OG1:OG1 + 512]),
           wait=[(s_mm, gv)], inc=s_dv)
    fv = p.fence("vector", s_dv)
    p.emit("scalar", lambda E: E.activation(grow[0:1, 0:512], grow[0:1, 0:512],
                                            AF.Gelu), wait=[(s_dv, fv)], inc=s_ac)
    p.fence("scalar", s_ac)
    p.barrier()
    gc4 = row_to_cols(p, com, pref + "_gc4",
                      lambda c: grow[0:1, c * 128:(c + 1) * 128], 4,
                      tring.banks[0][:, 0:4], s_mm, s_dv, dt=BF16)
    p.barrier()
    for k in range(4):
        p.emit("tensor", lambda E, o=p_tf[0:1, 0:256],
               l=gc4[:, k:k + 1], r=g2sb[:, k * 256:(k + 1) * 256],
               st=(k == 0), sp=(k == 3):
               E.matmul(o, l, r, start=st, stop=sp), inc=s_mm)
    gv = p.fence("tensor", s_mm)
    p.emit("vector", lambda E: E.tensor_add(grow[0:1, 512:768], p_tf[0:1, 0:256],
                                            brow[0:1, OG2:OG2 + 256]),
           wait=[(s_mm, gv)], inc=s_dv)
    fv = p.fence("vector", s_dv)
    p.emit("scalar", lambda E: E.activation(grow[0:1, 512:768], grow[0:1, 512:768],
                                            AF.Gelu), wait=[(s_dv, fv)], inc=s_ac)
    p.fence("scalar", s_ac)
    p.barrier()
    gc2 = row_to_cols(p, com, pref + "_gc2",
                      lambda c: grow[0:1, 512 + c * 128:512 + (c + 1) * 128], 2,
                      tring.banks[0][:, 0:2], s_mm, s_dv, dt=BF16)
    p.barrier()
    for k in range(2):
        p.emit("tensor", lambda E, o=p_tf[0:1, 0:1],
               l=gc2[:, k:k + 1], r=g3sb[:, k:k + 1],
               st=(k == 0), sp=(k == 1):
               E.matmul(o, l, r, start=st, stop=sp), inc=s_mm)
    gv = p.fence("tensor", s_mm)
    p.emit("scalar", lambda E: E.activation(grow[0:1, 768:769], p_tf[0:1, 0:1],
                                            AF.Sigmoid, bias=g3ap),
           wait=[(s_mm, gv)], inc=s_ac)
    fv = p.fence("scalar", s_ac)
    p.emit("vector", lambda E: E.tensor_scalar_mul(scr[0:1, 2:3],
                                                   grow[0:1, 768:769], 0.7),
           wait=[(s_ac, fv)], inc=s_dv)
    p.fence("vector", s_dv)
    p.barrier()
    dbg_dump(p, "l1_gmean", gmean[:], [1, D + 8], F32, [])
    dbg_dump(p, "l1_grow", grow[:], [1, 772], F32, [])
    dbg_dump(p, "l1_w1", w1sb[:], [128, 8 * D], BF16, [])
    dbg_dump(p, "l1_w4", w4sb[:], [128, 2], BF16, [])
    dbg_dump(p, "l1_g3", g3sb[:], [128, 2], BF16, [])
    dbg_dump(p, "l1_b1c", b1c[:], [128, 8], F32, [])
    p.barrier()

    pending_store = [None]

    def flush_store():
        if pending_store[0] is not None:
            fn = pending_store[0]
            pending_store[0] = None
            fn()

    bapply_done = {}
    store_done = {}
    h1mm_last = {}
    twist_dv = {}
    xn_tr_mm = {}
    tf_read = [0]
    cp_read = [0]
    ring_free = {0: 0, 1: 0}

    for b in range(NB):
        bb = b % 2
        for t in range(TPB):
            gi = b * TPB + t
            wl = []
            if b >= 2:
                wl.append((s_dv, bapply_done[b - 2]))
            p.emit("sync", lambda E, d=L.xg[bb][:, t * D:(t + 1) * D],
                   s=x_ap[gi * 128:(gi + 1) * 128, :]: E.dma_start(d, s),
                   wait=wl, inc=s_ld[bb], inck=16)
        ldv = p.cnt(s_ld[bb])
        for t in range(TPB):
            gi = b * TPB + t
            wl = [(s_ld[bb], ldv)]
            if t == 0 and b >= 1:
                wl.append((s_mm, xn_tr_mm[b - 1]))
            p.emit("vector", lambda E, d=L.xn_tok[:, t * D:(t + 1) * D],
                   x=L.xg[bb][:, t * D:(t + 1) * D],
                   m=L.stats_m[:, gi:gi + 1], r=L.stats_r[:, gi:gi + 1]:
                   E.tensor_scalar(d, x, m, r, ALU.subtract, ALU.mult),
                   wait=wl, inc=s_dv)
        apply_done = p.fence("vector", s_dv)
        first = True
        for c in range(8):
            i, bank, freev = tring.next()
            for j in range(TPB):
                wlt = []
                if j == 0:
                    wlt = [(s_dv, max(freev, apply_done))]
                    if first and b >= 1:
                        wlt.append((s_mm, h1mm_last[(b - 1, 2)]))
                        wlt.append((s_dv, twist_dv[(b - 1, 2)]))
                p.emit("tensor", lambda E, o=bank[:, j * 128:(j + 1) * 128],
                       i_=L.xn_tok[:, j * D + c * 128:j * D + (c + 1) * 128],
                       idn=com.ident_f[:]: E.transpose(o, i_, idn),
                       wait=wlt, inc=s_mm)
            mmv = p.fence("tensor", s_mm)
            p.emit("vector", lambda E, d=outT[0][:, c * TB:(c + 1) * TB],
                   sp=bank[:, 0:TB]: E.tensor_copy(d, sp),
                   wait=[(s_mm, mmv)], inc=s_dv)
            p.emit("vector", lambda E, d=outB[0][:, c * TB:(c + 1) * TB],
                   sp=bank[:, 0:TB]: E.tensor_copy(d, sp), inc=s_dv)
            tring.free[i] = p.cnt(s_dv)
            first = False
        xn_tr_mm[b] = p.cnt(s_mm)
        y0v = p.fence("vector", s_dv)
        flush_store()

        for cyc in range(3):
            cur = cyc % 2
            nxt = 1 - cur
            curT, curB = outT[cur], outB[cur]
            nxtT, nxtB = outT[nxt], outB[nxt]
            data_ready = y0v if cyc == 0 else twist_dv[(b, cyc - 1)]

            def hchain(wsb, rhs, rhs_ev, kc, oc_n, bias_cols, dst,
                       data_wait):
                evs = {}
                for oc in range(oc_n):
                    i = oc % 2
                    pr = ring[i]
                    for k in range(kc):
                        wl2 = []
                        if oc == 0 and rhs_ev is not None:
                            wl2.append((s_ac, rhs_ev[k]))
                        if k == 0:
                            if oc == 0 and data_wait:
                                wl2.append((s_dv, data_wait))
                            if ring_free[i] > 0:
                                wl2.append((s_ac, ring_free[i]))
                        p.emit("tensor", lambda E, o=pr[:, 0:TB],
                               l=wsb[:, k * (oc_n * 128) + oc * 128:
                                     k * (oc_n * 128) + (oc + 1) * 128],
                               r=rhs[:, k * TB:(k + 1) * TB],
                               st=(k == 0), sp=(k == kc - 1):
                               E.matmul(o, l, r, start=st, stop=sp),
                               wait=wl2, inc=s_mm)
                    gv = p.fence("tensor", s_mm)
                    p.emit("scalar", lambda E, d=dst[:, oc * TB:(oc + 1) * TB],
                           sp=pr[:, 0:TB], bc=bias_cols[:, oc:oc + 1]:
                           E.activation(d, sp, AF.Gelu, bias=bc),
                           wait=[(s_mm, gv)], inc=s_ac)
                    ring_free[i] = p.fence("scalar", s_ac)
                    evs[oc] = ring_free[i]
                return evs

            h1ev = hchain(w1sb, curB, None, 8, 8, b1c, h1b, data_ready)
            h1mm_last[(b, cyc)] = p.cnt(s_mm)
            h2ev = hchain(w2sb, h1b, h1ev, 8, 4, b2c, h2b, 0)
            h3ev = hchain(w3sb, h2b, h2ev, 4, 2, b3c, h3b, 0)
            for k in range(2):
                wl2 = [(s_ac, h3ev[k])] if True else []
                if k == 0 and (b > 0 or cyc > 0):
                    wl2.append((s_ac, tf_read[0]))
                p.emit("tensor", lambda E, o=p_tf[0:1, 0:TB],
                       l=w4sb[:, k:k + 1], r=h3b[:, k * TB:(k + 1) * TB],
                       st=(k == 0), sp=(k == 1):
                       E.matmul(o, l, r, start=st, stop=sp), wait=wl2, inc=s_mm)
            gv = p.fence("tensor", s_mm)
            p.emit("scalar", lambda E: E.activation(trow[0:1, 0:TB],
                                                    p_tf[0:1, 0:TB],
                                                    AF.Sigmoid, bias=b4ap),
                   wait=[(s_mm, gv)], inc=s_ac)
            tf_read[0] = p.fence("scalar", s_ac)
            p.emit("vector", lambda E: E.tensor_scalar(
                trow[0:1, TB:2 * TB], trow[0:1, 0:TB], 0.3, scr[0:1, 2:3],
                ALU.mult, ALU.add), wait=[(s_ac, tf_read[0])], inc=s_dv)
            p.emit("vector", lambda E: E.drain())
            p.emit("vector", lambda E: E.tensor_scalar(
                trow[0:1, 0:TB], trow[0:1, TB:2 * TB],
                scr[0:1, 0:1], scr[0:1, 1:2], ALU.mult, ALU.add), inc=s_dv)
            p.emit("vector", lambda E: E.drain())
            p.emit("vector", lambda E: E.tensor_copy(crow_b[0:1, 0:TB],
                                                     trow[0:1, 0:TB]),
                   inc=s_dv)
            p.emit("vector", lambda E: E.drain())
            p.emit("vector", lambda E: E.tensor_copy(crow_f[0:1, 0:TB],
                                                     crow_b[0:1, 0:TB]), inc=s_dv)
            p.emit("vector", lambda E: E.drain())
            p.emit("vector", lambda E: E.tensor_sub(crow_f[0:1, 0:TB],
                                                    trow[0:1, 0:TB],
                                                    crow_f[0:1, 0:TB]), inc=s_dv)
            p.emit("vector", lambda E: E.drain())
            p.emit("vector", lambda E: E.tensor_copy(crow_b[0:1, TB:2 * TB],
                                                     crow_f[0:1, 0:TB]), inc=s_dv)
            cbv = p.fence("vector", s_dv)
            wl2 = [(s_dv, cbv)]
            if (b, cyc) != (0, 0):
                wl2.append((s_dv, cp_read[0]))
            p.emit("tensor", lambda E: E.matmul(p_cp[:, 0:TB], ones128b[:],
                                                crow_b[0:1, 0:TB],
                                                start=True, stop=False),
                   wait=wl2, inc=s_mm)
            p.emit("tensor", lambda E: E.matmul(p_cp[:, 0:TB], ones128b[:],
                                                crow_b[0:1, TB:2 * TB],
                                                start=False, stop=True), inc=s_mm)
            bcv = p.fence("tensor", s_mm)
            for c8 in range(8):
                sc_, sg = TWIST_SRC[c8], TWIST_SGN[c8]
                p.emit("vector", lambda E, i0=curT[:, sc_ * TB:(sc_ + 1) * TB],
                       sg=sg: E.scalar_tensor_tensor(
                           tw_scr[:], i0, sg, p_cp[:, 0:TB], ALU.mult, ALU.mult),
                       wait=[(s_mm, bcv)] if c8 == 0 else (), inc=s_dv)
                p.emit("vector", lambda E, d=nxtT[:, c8 * TB:(c8 + 1) * TB],
                       i0=curT[:, c8 * TB:(c8 + 1) * TB]:
                       E.tensor_add(d, i0, tw_scr[:]),
                       wait=[(s_dv, p.cnt(s_dv))], inc=s_dv)
                p.emit("vector", lambda E, d=nxtB[:, c8 * TB:(c8 + 1) * TB],
                       sp=nxtT[:, c8 * TB:(c8 + 1) * TB]:
                       E.tensor_copy(d, sp),
                       wait=[(s_dv, p.cnt(s_dv))], inc=s_dv)
            twist_dv[(b, cyc)] = p.fence("vector", s_dv)
            cp_read[0] = twist_dv[(b, cyc)]

        if b == 0:
            dbg_dump(p, "l1_y0", outT[0][:], [128, 8 * TB], F32,
                     [(s_dv, y0v)])
            dbg_dump(p, "l1_y0b", outB[0][:], [128, 8 * TB], BF16,
                     [(s_dv, y0v)])
            dbg_dump(p, "l1_h1", h1b[:], [128, 8 * TB], BF16,
                     [(s_ac, ring_free[1])])
            dbg_dump(p, "l1_trow", trow[:], [1, 2 * TB], F32,
                     [(s_dv, twist_dv[(b, 0)])])
            dbg_dump(p, "l1_o1", outT[1][:], [128, 8 * TB], F32,
                     [(s_dv, twist_dv[(b, 0)])])
        fT = outT[1]
        for t in range(TPB):
            for half in range(2):
                srcs = [fT[:, (half * 4 + cc) * TB + t * 128:
                           (half * 4 + cc) * TB + (t + 1) * 128]
                        for cc in range(4)]
                transpose_group(p, com, tring, srcs,
                                ft[:, t * D + half * 512:t * D + (half + 1) * 512],
                                s_mm, s_dv,
                                dwait=[(s_dv, twist_dv[(b, 2)])],
                                fence=(t == TPB - 1 and half == 1))
        tr_done = p.cnt(s_dv)

        for t in range(TPB):
            gi = b * TPB + t
            ftt = ft[:, t * D:(t + 1) * D]
            p.emit("vector", lambda E, x=ftt, c=L.fstats_q[:, gi:gi + 1]
                   if False else L.fstats_s[:, gi:gi + 1]:
                   E.tensor_reduce(c, x, AX.X, ALU.add),
                   wait=[(s_dv, tr_done)], inc=s_dv)
            p.emit("scalar", lambda E, x=ftt, c=L.fstats_q[:, gi:gi + 1]:
                   E.activation(L.sqd[:], x, AF.Square, accum_out=c),
                   wait=[(s_dv, tr_done), (s_ac, p.cnt(s_ac))], inc=s_ac)
        p.fence("vector", s_dv)
        p.fence("scalar", s_ac)
        ffv = ln_finalize(p, L.fstats_s, L.fstats_q, L.fstats_m, L.fstats_r,
                          b * TPB, (b + 1) * TPB,
                          [(s_dv, p.cnt(s_dv)), (s_ac, p.cnt(s_ac))],
                          s_dv, s_ac)
        for t in range(TPB):
            gi = b * TPB + t
            wl = [(s_dv, ffv)]
            if t == 0 and b >= 2:
                wl.append((s_st[bb], store_done[b - 2]))
            p.emit("vector", lambda E, d=L.xo[bb][:, t * D:(t + 1) * D],
                   f=ft[:, t * D:(t + 1) * D],
                   m=L.fstats_m[:, gi:gi + 1], r=L.fstats_r[:, gi:gi + 1]:
                   E.tensor_scalar(d, f, m, r, ALU.subtract, ALU.mult),
                   wait=wl, inc=s_dv)
            p.emit("vector", lambda E, d=L.xo[bb][:, t * D:(t + 1) * D],
                   x=L.xg[bb][:, t * D:(t + 1) * D]:
                   E.tensor_add(d, d, x), wait=[(s_dv, p.cnt(s_dv))], inc=s_dv)
        bapply_done[b] = p.fence("vector", s_dv)

        def mk_store(bv=bapply_done[b], b=b, bb=bb):
            settle = p.cnt(s_dv)
            for t in range(TPB):
                gi = b * TPB + t
                p.emit("sync", lambda E, d=y_ap[gi * 128:(gi + 1) * 128, :],
                       s=L.xo[bb][:, t * D:(t + 1) * D]:
                       E.dma_start(d, s),
                       wait=[(s_dv, max(bv, settle))] if t == 0 else (),
                       inc=s_st[bb], inck=16)
            store_done[b] = p.cnt(s_st[bb])
        pending_store[0] = mk_store
        store_done[b] = p.cnt(s_st[bb]) + 16 * TPB

    flush_store()
    p.emit("sync", lambda E: E.nop(), wait=[(s_st[0], p.cnt(s_st[0]))])
    p.emit("sync", lambda E: E.nop(), wait=[(s_st[1], p.cnt(s_st[1]))])
    p.barrier()


def build_l1_ffn_program():
    nc = bass.Bass("TRN2", target_bir_lowering=False)
    x = nc.dram_tensor("x", [S, D], F32, kind="ExternalInput").ap()
    y = nc.dram_tensor("y", [S, D], F32, kind="ExternalOutput").ap()
    shapes = {"cn_w1": [D, D], "cn_w2": [D, 512], "cn_w3": [512, 256],
              "cn_w4": [256, 1], "gc_w1": [D, 512], "gc_w2": [512, 256],
              "gc_w3": [256, 1], "cn_b1": [1, D], "cn_b2": [1, 512],
              "cn_b3": [1, 256], "cn_b4": [1, 1], "gc_b1": [1, 512],
              "gc_b2": [1, 256], "gc_b3": [1, 1], "ar": [1, 1]}
    w = {k: nc.dram_tensor(k, v, F32, kind="ExternalInput").ap()
         for k, v in shapes.items()}
    capz = _declare_const_inputs(nc, ["ident"])
    with ExitStack() as stack:
        p = P(nc, stack)
        com = Common(p, capz["ident"])
        with p.scope():
            build_l1_ffn(p, com, x, y, w)
        p.flush()
    return nc


def run_l1_ffn(x_full, w_np):
    nc = build_l1_ffn_program()
    in_maps = []
    for c in range(NCORES):
        m = {"x": np.ascontiguousarray(x_full[c])}
        for k, v in w_np.items():
            m[k] = np.ascontiguousarray(v.reshape(1, -1) if v.ndim == 1 else v)
        m.update(_const_in_map(["ident"]))
        in_maps.append(m)
    res = run_bass_kernel_spmd(nc, in_maps, core_ids=list(range(NCORES)))
    return np.stack([r["y"] for r in res.results])


def build_l1_attn(p, com, x_ap, y_ap, w, pref="a1"):
    s_ld = [p.sem(pref + "_ld0"), p.sem(pref + "_ld1")]
    s_st = [p.sem(pref + "_st0"), p.sem(pref + "_st1")]
    s_dv = p.sem(pref + "_dv")
    s_ac = p.sem(pref + "_ac")
    s_mm = p.sem(pref + "_mm")
    s_wld = p.sem(pref + "_wld")
    s_wl2 = [p.sem(pref + "_wl0"), p.sem(pref + "_wl1")]
    s_wcv = p.sem(pref + "_wcv")
    EPS = 1e-8

    tring = TRing(p, pref)
    ring = [p.ps(pref + "_pa", [128, 512], F32),
            p.ps(pref + "_pb", [128, 512], F32)]

    def dvc(fn, wait=(), inc=s_dv):
        p.emit("vector", fn, wait=list(wait) + [(s_dv, p.cnt(s_dv))], inc=inc)

    def acc(fn, wait=()):
        p.emit("scalar", fn, wait=list(wait) + [(s_ac, p.cnt(s_ac))], inc=s_ac)

    wstack = ExitStack()
    brow = wstack.enter_context(p.nc.sbuf_tensor(pref + "_brow", [1, 3 * D],
                                                 F32, side="right"))
    for j, key in enumerate(("bq", "bk", "bv")):
        p.emit("sync", lambda E, d=brow[0:1, j * D:(j + 1) * D], sr=w[key]:
               E.dma_start(d, sr), inc=s_wld, inck=16)
    p.dma_barrier(s_wld)
    p.barrier()

    AB = {}
    dcols = {}
    stg = wstack.enter_context(p.nc.sbuf_tensor(pref + "_stg", [128, 2 * D],
                                                F32, side="right"))
    dfb = wstack.enter_context(p.nc.sbuf_tensor(pref + "_dfb", [128, 8 * D],
                                                BF16, side="right"))
    wb = wstack.enter_context(p.nc.sbuf_tensor(pref + "_wb", [128, 8 * D],
                                               BF16, side="right"))
    base_cv = p.cnt(s_wcv)
    for k in range(8):
        bpar = k % 2
        wait = [(s_wcv, base_cv + k - 1)] if k >= 2 else [(s_wcv, base_cv)]
        p.emit("sync", lambda E, d=stg[:, bpar * D:(bpar + 1) * D],
               sr=w["dft_f"][k * 128:(k + 1) * 128, :]: E.dma_start(d, sr),
               wait=wait, inc=s_wl2[bpar], inck=16)
        p.emit("vector", lambda E, d=dfb[:, k * D:(k + 1) * D],
               sr=stg[:, bpar * D:(bpar + 1) * D]: E.tensor_copy(d, sr),
               wait=[(s_wl2[bpar], p.cnt(s_wl2[bpar]))], inc=s_wcv)
    dinv = load_weight_bf16(p, pref + "_dinv", w["dft_i"], D, D, s_wl2,
                            s_wcv, stg)
    p.barrier()
    bqc = row_to_cols(p, com, pref + "_bqc",
                      lambda c: brow[0:1, c * 128:(c + 1) * 128], 8,
                      tring.banks[0][:, 0:8], s_mm, s_dv, dt=BF16)
    p.barrier()
    bkc = row_to_cols(p, com, pref + "_bkc",
                      lambda c: brow[0:1, D + c * 128:D + (c + 1) * 128], 8,
                      tring.banks[0][:, 0:8], s_mm, s_dv, dt=BF16)
    p.barrier()
    bvc = row_to_cols(p, com, pref + "_bvc",
                      lambda c: brow[0:1, 2 * D + c * 128:2 * D + (c + 1) * 128],
                      8, tring.banks[0][:, 0:8], s_mm, s_dv, dt=BF16)
    p.barrier()
    for nm, bc in (("q", bqc), ("k", bkc), ("v", bvc)):
        base_cv = p.cnt(s_wcv)
        for k in range(8):
            bpar = k % 2
            wait = [(s_wcv, base_cv + k - 1)] if k >= 2 else [(s_wcv, base_cv)]
            p.emit("sync", lambda E, d=stg[:, bpar * D:(bpar + 1) * D],
                   sr=w["W" + nm][k * 128:(k + 1) * 128, :]: E.dma_start(d, sr),
                   wait=wait, inc=s_wl2[bpar], inck=16)
            p.emit("vector", lambda E, d=wb[:, k * D:(k + 1) * D],
                   sr=stg[:, bpar * D:(bpar + 1) * D]: E.tensor_copy(d, sr),
                   wait=[(s_wl2[bpar], p.cnt(s_wl2[bpar]))], inc=s_wcv)
        p.fence("vector", s_wcv)
        p.barrier()
        ab = p.sb(pref + "_A" + nm, [128, 8 * D], BF16)
        dc = p.sb(pref + "_d" + nm, [128, 8], F32)
        for frc in range(8):
            for half in range(2):
                pr = ring[half]
                for ec in range(8):
                    p.emit("tensor", lambda E, o=pr[:, 0:512],
                           l=wb[:, ec * D + frc * 128:ec * D + (frc + 1) * 128],
                           r=dfb[:, ec * D + half * 512:ec * D + (half + 1) * 512],
                           st=(ec == 0), sp=(ec == 7):
                           E.matmul(o, l, r, start=st, stop=sp),
                           wait=[(s_dv, p.cnt(s_dv))] if ec == 0 else (),
                           inc=s_mm)
                gv = p.fence("tensor", s_mm)
                p.emit("vector", lambda E,
                       d=ab[:, frc * D + half * 512:frc * D + (half + 1) * 512],
                       sp=pr[:, 0:512]: E.tensor_copy(d, sp),
                       wait=[(s_mm, gv)], inc=s_dv)
            for ec in range(8):
                p.emit("tensor", lambda E, o=tring.banks[0][:, 0:1],
                       l=dfb[:, ec * D + frc * 128:ec * D + (frc + 1) * 128],
                       r=bc[:, ec:ec + 1], st=(ec == 0), sp=(ec == 7):
                       E.matmul(o, l, r, start=st, stop=sp), inc=s_mm)
            gv = p.fence("tensor", s_mm)
            p.emit("vector", lambda E, d=dc[:, frc:frc + 1],
                   sp=tring.banks[0][:, 0:1]: E.tensor_copy(d, sp),
                   wait=[(s_mm, gv)], inc=s_dv)
        p.fence("vector", s_dv)
        p.barrier()
        AB[nm] = ab
        dcols[nm] = dc
    wstack.close()

    L = LNBufs(p, pref, sqd=False)
    xnT = p.sb(pref + "_xnT", [128, 8 * TB], BF16)
    bufA = p.sb(pref + "_bufA", [128, 8 * TB], F32)
    bufB = p.sb(pref + "_bufB", [128, 8 * TB], F32)
    nsc1 = p.sb(pref + "_n1", [128, 4 * TB], F32)
    nsc2 = p.sb(pref + "_n2", [128, 4 * TB], F32)
    zb = xnT
    ccar = p.sb(pref + "_ccar", [128, 8], F32)
    ft = L.xn_tok
    otf = bufA
    HT_ = 4 * TB

    p.emit("gpsimd", lambda E: E.memset(ccar[:], 0.0))
    p.barrier()

    def project(dst, ab, dc, evw):
        for frc in range(8):
            pr = ring[frc % 2]
            for ec in range(8):
                wl = []
                if ec == 0:
                    wl = [(s_dv, evxnT)]
                    if frc < 2:
                        wl += list(evw)
                        wl.append((s_dv, proj_ev.get(6 + frc, 0)))
                    else:
                        wl.append((s_dv, proj_ev[frc - 2]))
                p.emit("tensor", lambda E, o=pr[:, 0:TB],
                       l=ab[:, ec * D + frc * 128:ec * D + (frc + 1) * 128],
                       r=xnT[:, ec * TB:(ec + 1) * TB],
                       st=(ec == 0), sp=(ec == 7):
                       E.matmul(o, l, r, start=st, stop=sp), wait=wl, inc=s_mm)
            gv = p.fence("tensor", s_mm)
            p.emit("vector", lambda E, d=dst[:, frc * TB:(frc + 1) * TB],
                   sp=pr[:, 0:TB], bcl=dc[:, frc:frc + 1]:
                   E.tensor_scalar_add(d, sp, bcl),
                   wait=[(s_mm, gv), (s_dv, p.cnt(s_dv))], inc=s_dv)
            proj_ev[frc] = p.cnt(s_dv)
        return p.fence("vector", s_dv)

    def normalize(buf):
        top = buf[:, 0:HT_]
        bot = buf[:, HT_:2 * HT_]
        acc(lambda E: E.activation(nsc1[:], top, AF.Square),
            wait=[(s_dv, p.cnt(s_dv))])
        acc(lambda E: E.activation(nsc2[:], bot, AF.Square))
        af = p.fence("scalar", s_ac)
        dvc(lambda E: E.tensor_add(nsc1[:], nsc1[:], nsc2[:]), wait=[(s_ac, af)])
        df_ = p.fence("vector", s_dv)
        acc(lambda E: E.activation(nsc1[:], nsc1[:], AF.Sqrt), wait=[(s_dv, df_)])
        acc(lambda E: E.activation(nsc1[:], nsc1[:], AF.Ln, bias=epsc[:]))
        acc(lambda E: E.activation(nsc1[:], nsc1[:], AF.Exp, scale=-1.0))
        af = p.fence("scalar", s_ac)
        for j, off in ((0, 0), (1, HT_)):
            acc(lambda E, o=off, j=j: E.activation(
                nsc2[0:1, j * TB:(j + 1) * TB], buf[0:1, o:o + TB], AF.Abs),
                wait=[(s_dv, p.cnt(s_dv))] if j == 0 else ())
            acc(lambda E, j=j: E.activation(
                nsc2[0:1, j * TB:(j + 1) * TB], nsc2[0:1, j * TB:(j + 1) * TB],
                AF.Ln, bias=epsc[0:1, :]))
            acc(lambda E, j=j: E.activation(
                nsc2[0:1, j * TB:(j + 1) * TB], nsc2[0:1, j * TB:(j + 1) * TB],
                AF.Exp, scale=-1.0))
        af2 = p.fence("scalar", s_ac)
        dvc(lambda E: E.tensor_mul(nsc2[0:1, 2 * TB:3 * TB], buf[0:1, 0:TB],
                                   nsc2[0:1, 0:TB]), wait=[(s_ac, af2)])
        dvc(lambda E: E.tensor_mul(nsc2[0:1, 3 * TB:4 * TB],
                                   buf[0:1, HT_:HT_ + TB], nsc2[0:1, TB:2 * TB]))
        dvc(lambda E: E.tensor_mul(top, top, nsc1[:]))
        dvc(lambda E: E.tensor_mul(bot, bot, nsc1[:]))
        dvc(lambda E: E.tensor_copy(buf[0:1, 0:TB], nsc2[0:1, 2 * TB:3 * TB]))
        dvc(lambda E: E.tensor_copy(buf[0:1, HT_:HT_ + TB],
                                    nsc2[0:1, 3 * TB:4 * TB]))
        return p.fence("vector", s_dv)

    epsc = p.sb(pref + "_epsc", [128, 1], F32)
    p.emit("gpsimd", lambda E: E.memset(epsc[:], EPS))
    p.barrier()

    pending_store = [None]

    def flush_store():
        if pending_store[0] is not None:
            fn = pending_store[0]
            pending_store[0] = None
            fn()

    bapply_done = {}
    store_done = {}
    xn_tr_mm = {}
    proj_ev = {}
    invread = [0]
    prev_bufB_free = [0]
    xn_consumed = 0
    rstash = p.sb(pref + "_rstash", [1, TB], F32)

    for b in range(NB):
        bb = b % 2
        for t in range(TPB):
            gi = b * TPB + t
            wl = []
            if b >= 2:
                wl.append((s_dv, bapply_done[b - 2]))
            p.emit("sync", lambda E, d=L.xg[bb][:, t * D:(t + 1) * D],
                   sr=x_ap[gi * 128:(gi + 1) * 128, :]: E.dma_start(d, sr),
                   wait=wl, inc=s_ld[bb], inck=16)
        ldv = p.cnt(s_ld[bb])
        for t in range(TPB):
            gi = b * TPB + t
            xt = L.xg[bb][:, t * D:(t + 1) * D]
            p.emit("vector", lambda E, x=xt, c=L.stats_s[:, gi:gi + 1]:
                   E.tensor_reduce(c, x, AX.X, ALU.add),
                   wait=[(s_ld[bb], ldv)], inc=s_dv)
            p.emit("scalar", lambda E, x=xt, c=L.stats_q[:, gi:gi + 1]:
                   E.activation(zb[:, 0:D], x, AF.Square, accum_out=c),
                   wait=[(s_ld[bb], ldv), (s_ac, p.cnt(s_ac))], inc=s_ac)
        p.fence("vector", s_dv)
        p.fence("scalar", s_ac)
        finv = ln_finalize(p, L.stats_s, L.stats_q, L.stats_m, L.stats_r,
                           b * TPB, (b + 1) * TPB,
                           [(s_dv, p.cnt(s_dv)), (s_ac, p.cnt(s_ac))],
                           s_dv, s_ac)
        for t in range(TPB):
            gi = b * TPB + t
            wl = [(s_dv, finv)]
            if t == 0 and b >= 1:
                wl.append((s_mm, xn_tr_mm[b - 1]))
            p.emit("vector", lambda E, d=L.xn_tok[:, t * D:(t + 1) * D],
                   x=L.xg[bb][:, t * D:(t + 1) * D],
                   m=L.stats_m[:, gi:gi + 1], r=L.stats_r[:, gi:gi + 1]:
                   E.tensor_scalar(d, x, m, r, ALU.subtract, ALU.mult),
                   wait=wl, inc=s_dv)
        apply_done = p.fence("vector", s_dv)
        first = True
        for c in range(8):
            srcs = [L.xn_tok[:, t * D + c * 128:t * D + (c + 1) * 128]
                    for t in range(TPB)]
            dw = [(s_dv, apply_done)] if first else ()
            if first and b >= 1:
                dw = list(dw) + [(s_mm, xn_consumed)]
            transpose_group(p, com, tring, srcs,
                            xnT[:, c * TB:(c + 1) * TB], s_mm, s_dv,
                            dwait=dw, fence=(c == 7))
            first = False
        xn_tr_mm[b] = p.cnt(s_mm)
        evxnT = p.cnt(s_dv)
        flush_store()

        project(bufA, AB["k"], dcols["k"], [(s_mm, invread[0])])
        project(bufB, AB["v"], dcols["v"], [(s_dv, prev_bufB_free[0])])
        normalize(bufA)
        normalize(bufB)
        top = lambda buf: buf[:, 0:HT_]
        bot = lambda buf: buf[:, HT_:2 * HT_]
        dvc(lambda E: E.tensor_mul(nsc1[:], top(bufA), top(bufB)))
        dvc(lambda E: E.tensor_mul(nsc2[:], bot(bufA), bot(bufB)))
        dvc(lambda E: E.tensor_copy(rstash[0:1, 0:TB], nsc1[0:1, 0:TB]))
        dvc(lambda E: E.tensor_sub(nsc1[:], nsc1[:], nsc2[:]))
        dvc(lambda E: E.tensor_copy(nsc1[0:1, 0:TB], rstash[0:1, 0:TB]))
        dvc(lambda E: E.tensor_mul(nsc2[:], top(bufA), bot(bufB)))
        dvc(lambda E: E.tensor_mul(top(bufB), bot(bufA), top(bufB)))
        dvc(lambda E: E.tensor_add(top(bufB), nsc2[:], top(bufB)))
        dvc(lambda E: E.tensor_mul(top(bufB)[0:1, 0:TB],
                                   bot(bufA)[0:1, 0:TB],
                                   bot(bufB)[0:1, 0:TB]))
        for c in range(8):
            src = nsc1[:, c * TB:(c + 1) * TB] if c < 4 else                 bufB[:, (c - 4) * TB:(c - 3) * TB]
            dst = bufA[:, c * TB:(c + 1) * TB]
            dvc(lambda E, s_=src, d_=dst, cc=ccar[:, c:c + 1]:
                E.tensor_tensor_scan(d_, s_, s_, cc, ALU.add, ALU.bypass))
            dvc(lambda E, d_=ccar[:, c:c + 1],
                s_=bufA[:, c * TB + TB - 1:c * TB + TB]:
                E.tensor_copy(d_, s_))
        scan_done = p.fence("vector", s_dv)
        evq = project(bufB, AB["q"], dcols["q"], [(s_dv, scan_done)])
        xn_consumed = p.cnt(s_mm)
        normalize(bufB)
        dvc(lambda E: E.tensor_mul(nsc1[:], top(bufA), top(bufB)),
            wait=[(s_mm, xn_consumed)])
        dvc(lambda E: E.tensor_mul(nsc2[:], bot(bufA), bot(bufB)))
        dvc(lambda E: E.tensor_add(nsc1[:], nsc1[:], nsc2[:]))
        dvc(lambda E: E.tensor_copy(zb[:, 0:HT_], nsc1[:]))
        dvc(lambda E: E.tensor_mul(nsc1[:], bot(bufA), top(bufB)))
        dvc(lambda E: E.tensor_mul(nsc2[:], top(bufA), bot(bufB)))
        dvc(lambda E: E.tensor_sub(nsc1[:], nsc1[:], nsc2[:]))
        dvc(lambda E: E.tensor_copy(zb[:, HT_:2 * HT_], nsc1[:]))
        dvc(lambda E: E.tensor_mul(nsc1[0:1, 0:TB], top(bufA)[0:1, 0:TB],
                                   top(bufB)[0:1, 0:TB]))
        dvc(lambda E: E.tensor_copy(zb[0:1, 0:TB], nsc1[0:1, 0:TB]))
        dvc(lambda E: E.tensor_mul(nsc1[0:1, 0:TB], bot(bufA)[0:1, 0:TB],
                                   bot(bufB)[0:1, 0:TB]))
        dvc(lambda E: E.tensor_copy(zb[0:1, HT_:HT_ + TB], nsc1[0:1, 0:TB]))
        zdone = p.fence("vector", s_dv)
        prev_bufB_free = [p.cnt(s_dv)]
        inv_ev = {}
        for tc in range(8):
            pr = ring[tc % 2]
            for frc in range(8):
                wl = []
                if frc == 0:
                    wl = [(s_dv, zdone)]
                    if tc >= 2:
                        wl.append((s_dv, inv_ev[tc - 2]))
                p.emit("tensor", lambda E, o=pr[:, 0:TB],
                       l=dinv[:, frc * D + tc * 128:frc * D + (tc + 1) * 128],
                       r=zb[:, frc * TB:(frc + 1) * TB],
                       st=(frc == 0), sp=(frc == 7):
                       E.matmul(o, l, r, start=st, stop=sp), wait=wl, inc=s_mm)
            gv = p.fence("tensor", s_mm)
            p.emit("vector", lambda E, d=otf[:, tc * TB:(tc + 1) * TB],
                   sp=pr[:, 0:TB]: E.tensor_copy(d, sp),
                   wait=[(s_mm, gv), (s_dv, p.cnt(s_dv))], inc=s_dv)
            inv_ev[tc] = p.cnt(s_dv)
        inv_done = p.fence("vector", s_dv)

        for t in range(TPB):
            for half in range(2):
                srcs = [otf[:, (half * 4 + cc) * TB + t * 128:
                           (half * 4 + cc) * TB + (t + 1) * 128]
                        for cc in range(4)]
                transpose_group(p, com, tring, srcs,
                                ft[:, t * D + half * 512:t * D + (half + 1) * 512],
                                s_mm, s_dv, dwait=[(s_dv, inv_done)],
                                fence=(t == TPB - 1 and half == 1))
        tr_done = p.cnt(s_dv)
        invread[0] = p.cnt(s_mm)

        for t in range(TPB):
            gi = b * TPB + t
            ftt = ft[:, t * D:(t + 1) * D]
            p.emit("vector", lambda E, x=ftt, c=L.fstats_s[:, gi:gi + 1]:
                   E.tensor_reduce(c, x, AX.X, ALU.add),
                   wait=[(s_dv, tr_done)], inc=s_dv)
            p.emit("scalar", lambda E, x=ftt, c=L.fstats_q[:, gi:gi + 1]:
                   E.activation(zb[:, 0:D], x, AF.Square, accum_out=c),
                   wait=[(s_dv, tr_done), (s_ac, p.cnt(s_ac))], inc=s_ac)
        p.fence("vector", s_dv)
        p.fence("scalar", s_ac)
        ffv = ln_finalize(p, L.fstats_s, L.fstats_q, L.fstats_m, L.fstats_r,
                          b * TPB, (b + 1) * TPB,
                          [(s_dv, p.cnt(s_dv)), (s_ac, p.cnt(s_ac))],
                          s_dv, s_ac)
        for t in range(TPB):
            gi = b * TPB + t
            wl = [(s_dv, ffv)]
            if t == 0 and b >= 2:
                wl.append((s_st[bb], store_done[b - 2]))
            p.emit("vector", lambda E, d=L.xo[bb][:, t * D:(t + 1) * D],
                   f=ft[:, t * D:(t + 1) * D],
                   m=L.fstats_m[:, gi:gi + 1], r=L.fstats_r[:, gi:gi + 1]:
                   E.tensor_scalar(d, f, m, r, ALU.subtract, ALU.mult),
                   wait=wl, inc=s_dv)
            p.emit("vector", lambda E, d=L.xo[bb][:, t * D:(t + 1) * D],
                   x=L.xg[bb][:, t * D:(t + 1) * D]:
                   E.tensor_add(d, d, x), wait=[(s_dv, p.cnt(s_dv))], inc=s_dv)
        bapply_done[b] = p.fence("vector", s_dv)

        def mk_store(bv=bapply_done[b], b=b, bb=bb):
            settle = p.cnt(s_dv)
            for t in range(TPB):
                gi = b * TPB + t
                p.emit("sync", lambda E, d=y_ap[gi * 128:(gi + 1) * 128, :],
                       sr=L.xo[bb][:, t * D:(t + 1) * D]:
                       E.dma_start(d, sr),
                       wait=[(s_dv, max(bv, settle))] if t == 0 else (),
                       inc=s_st[bb], inck=16)
            store_done[b] = p.cnt(s_st[bb])
        pending_store[0] = mk_store
        store_done[b] = p.cnt(s_st[bb]) + 16 * TPB

    flush_store()
    p.emit("sync", lambda E: E.nop(), wait=[(s_st[0], p.cnt(s_st[0]))])
    p.emit("sync", lambda E: E.nop(), wait=[(s_st[1], p.cnt(s_st[1]))])
    p.barrier()


def build_l1_attn_program():
    nc = bass.Bass("TRN2", target_bir_lowering=False)
    x = nc.dram_tensor("x", [S, D], F32, kind="ExternalInput").ap()
    y = nc.dram_tensor("y", [S, D], F32, kind="ExternalOutput").ap()
    shapes = {"Wq": [D, D], "Wk": [D, D], "Wv": [D, D],
              "bq": [1, D], "bk": [1, D], "bv": [1, D]}
    w = {k: nc.dram_tensor(k, v, F32, kind="ExternalInput").ap()
         for k, v in shapes.items()}
    capz = _declare_const_inputs(nc, ["ident", "dft_f", "dft_i"])
    w["dft_f"] = capz["dft_f"]
    w["dft_i"] = capz["dft_i"]
    with ExitStack() as stack:
        p = P(nc, stack)
        com = Common(p, capz["ident"])
        with p.scope():
            build_l1_attn(p, com, x, y, w)
        p.flush()
    return nc


def run_l1_attn(x_full, w_np):
    nc = build_l1_attn_program()
    in_maps = []
    for c in range(NCORES):
        m = {"x": np.ascontiguousarray(x_full[c])}
        for k in ("Wq", "Wk", "Wv"):
            m[k] = np.ascontiguousarray(w_np[k])
        for k in ("bq", "bk", "bv"):
            m[k] = np.ascontiguousarray(w_np[k].reshape(1, -1))
        m.update(_const_in_map(["ident", "dft_f", "dft_i"]))
        in_maps.append(m)
    res = run_bass_kernel_spmd(nc, in_maps, core_ids=list(range(NCORES)))
    return np.stack([r["y"] for r in res.results])


def build_l0_attn(p, com, x_ap, y_ap, w, xnq_dram, ctx_dram, pref="t0"):
    s_ld = [p.sem(pref + "_ld0"), p.sem(pref + "_ld1")]
    s_st = [p.sem(pref + "_st0"), p.sem(pref + "_st1")]
    s_cx = p.sem(pref + "_cx")
    s_xq = [p.sem(pref + "_xq0"), p.sem(pref + "_xq1")]
    s_xw = [p.sem(pref + "_xw0"), p.sem(pref + "_xw1")]
    s_dv = p.sem(pref + "_dv")
    s_ac = p.sem(pref + "_ac")
    s_mm = p.sem(pref + "_mm")
    s_wld = p.sem(pref + "_wld")
    s_wl2 = [p.sem(pref + "_wl0"), p.sem(pref + "_wl1")]
    s_wcv = p.sem(pref + "_wcv")

    def dvc(fn, wait=(), inc=s_dv):
        p.emit("vector", fn, wait=list(wait) + [(s_dv, p.cnt(s_dv))], inc=inc)

    tring = TRing(p, pref)
    ps_sc2 = [p.ps(pref + f"_sc{i}", [128, 1024], F32) for i in range(2)]
    ps_sc = ps_sc2[0]
    ps_cxA = p.ps(pref + "_cxA", [128, 512], F32)
    ps_cxB = p.ps(pref + "_cxB", [128, 512], F32)

    wstack = ExitStack()
    brow = wstack.enter_context(p.nc.sbuf_tensor(pref + "_brow", [1, 3 * D],
                                                 F32, side="right"))
    for j, key in enumerate(("bq", "bk", "bv")):
        p.emit("sync", lambda E, d=brow[0:1, j * D:(j + 1) * D], sr=w[key]:
               E.dma_start(d, sr), inc=s_wld, inck=16)
    p.dma_barrier(s_wld)
    p.barrier()
    bqc = row_to_cols(p, com, pref + "_bqc",
                      lambda c: brow[0:1, c * 128:(c + 1) * 128], 8,
                      tring.banks[0][:, 0:8], s_mm, s_dv)
    p.barrier()
    bkc = row_to_cols(p, com, pref + "_bkc",
                      lambda c: brow[0:1, D + c * 128:D + (c + 1) * 128], 8,
                      tring.banks[0][:, 0:8], s_mm, s_dv)
    p.barrier()
    bvc = row_to_cols(p, com, pref + "_bvc",
                      lambda c: brow[0:1, 2 * D + c * 128:2 * D + (c + 1) * 128],
                      8, tring.banks[0][:, 0:8], s_mm, s_dv)
    p.barrier()
    wstack.close()

    L = LNBufs(p, pref, sqd=False, xg=False, xo=False)
    sqdum = p.sb(pref + "_sqdum", [128, D], BF16)
    HW_ = 65

    with p.scope():
        L.xg = [p.sb(pref + f"_xga{i}", [128, TPB * D], F32) for i in range(2)]
        blkw = [p.sb(pref + f"_blkw{i}", [128, 8 * TB], BF16) for i in range(2)]
        grp_apply = {}
        blk_st = {}
        for g in range(8):
            gb = g % 2
            for t in range(TPB):
                gi = g * TPB + t
                wl = []
                if g >= 2:
                    wl.append((s_dv, grp_apply[g - 2]))
                p.emit("sync", lambda E, d=L.xg[gb][:, t * D:(t + 1) * D],
                       sr=x_ap[gi * 128:(gi + 1) * 128, :]: E.dma_start(d, sr),
                       wait=wl, inc=s_ld[gb], inck=16)
            ldv = p.cnt(s_ld[gb])
            for t in range(TPB):
                gi = g * TPB + t
                xt = L.xg[gb][:, t * D:(t + 1) * D]
                p.emit("vector", lambda E, x=xt, c=L.stats_s[:, gi:gi + 1]:
                       E.tensor_reduce(c, x, AX.X, ALU.add),
                       wait=[(s_ld[gb], ldv)], inc=s_dv)
                p.emit("scalar", lambda E, x=xt, c=L.stats_q[:, gi:gi + 1]:
                       E.activation(sqdum[:], x, AF.Square, accum_out=c),
                       wait=[(s_ld[gb], ldv), (s_ac, p.cnt(s_ac))], inc=s_ac)
            p.fence("vector", s_dv)
            p.fence("scalar", s_ac)
            finv = ln_finalize(p, L.stats_s, L.stats_q, L.stats_m, L.stats_r,
                               g * TPB, (g + 1) * TPB,
                               [(s_dv, p.cnt(s_dv)), (s_ac, p.cnt(s_ac))],
                               s_dv, s_ac)
            for t in range(TPB):
                gi = g * TPB + t
                dvc(lambda E, xt=L.xg[gb][:, t * D:(t + 1) * D],
                    m=L.stats_m[:, gi:gi + 1], r=L.stats_r[:, gi:gi + 1]:
                    E.tensor_scalar(xt, xt, m, r, ALU.subtract, ALU.mult),
                    wait=[(s_dv, finv)])
            apply_done = p.fence("vector", s_dv)
            for c in range(8):
                srcs = [L.xg[gb][:, t * D + c * 128:t * D + (c + 1) * 128]
                        for t in range(TPB)]
                dw = [(s_dv, apply_done)] if c == 0 else []
                if c == 0 and g >= 2:
                    dw.append((s_xw[gb], blk_st.get(g - 2, 0)))
                transpose_group(p, com, tring, srcs,
                                blkw[gb][:, c * TB:(c + 1) * TB], s_mm, s_dv,
                                dwait=dw, fence=(c == 7))
            grp_apply[g] = p.cnt(s_dv)
            p.emit("sync", lambda E, g=g, gb=gb: E.dma_start(
                xnq_dram[:, :, g * TB:(g + 1) * TB].rearrange(
                    "c q w -> q c w"),
                blkw[gb].ap().rearrange("q (c w) -> q c w", w=TB)),
                   wait=[(s_dv, grp_apply[g])], inc=s_xw[gb], inck=16)
            blk_st[g] = p.cnt(s_xw[gb])
        p.emit("sync", lambda E: E.nop(), wait=[(s_xw[0], p.cnt(s_xw[0]))])
        p.emit("sync", lambda E: E.nop(), wait=[(s_xw[1], p.cnt(s_xw[1]))])
        p.barrier()

    with p.scope():
        blk = [p.sb(pref + f"_blk{i}", [128, 8 * TB], BF16) for i in range(2)]
        vw = p.sb(pref + "_vw", [128, 32 * 16 * HW_], BF16)
        vtc = p.sb(pref + "_vtc", [128, TB], F32)
        wstack = ExitStack()
        stg = wstack.enter_context(p.nc.sbuf_tensor(pref + "_stg", [128, 2 * D],
                                                    F32, side="right"))
        wvb = wstack.enter_context(p.nc.sbuf_tensor(pref + "_wvb", [128, 8 * D],
                                                    BF16, side="right"))
        base_cv = p.cnt(s_wcv)
        for k in range(8):
            bpar = k % 2
            wait = [(s_wcv, base_cv + k - 1)] if k >= 2 else [(s_wcv, base_cv)]
            p.emit("sync", lambda E, d=stg[:, bpar * D:(bpar + 1) * D],
                   sr=w["wv"][k * 128:(k + 1) * 128, :]: E.dma_start(d, sr),
                   wait=wait, inc=s_wl2[bpar], inck=16)
            p.emit("vector", lambda E, d=wvb[:, k * D:(k + 1) * D],
                   sr=stg[:, bpar * D:(bpar + 1) * D]: E.tensor_copy(d, sr),
                   wait=[(s_wl2[bpar], p.cnt(s_wl2[bpar]))], inc=s_wcv)
        p.fence("vector", s_wcv)
        p.emit("gpsimd", lambda E: E.memset(vw[:], 1.0))
        p.barrier()
        vt_mm = [0]
        vtc_ac = [0]
        for tb in range(8):
            gb = tb % 2
            wl = [(s_mm, vt_mm[0])] if tb >= 2 else []
            p.emit("sync", lambda E, gb=gb, tb=tb: E.dma_start(
                blk[gb].ap().rearrange("q (c w) -> q c w", w=TB),
                xnq_dram[:, :, tb * TB:(tb + 1) * TB].rearrange(
                    "c q w -> q c w")),
                   wait=wl, inc=s_xq[gb], inck=16)
            ldq = p.cnt(s_xq[gb])
            for dc_ in range(8):
                psm = ps_sc2[dc_ % 2][:, 0:512]
                for c in range(8):
                    wl = []
                    if c == 0:
                        wl = [(s_xq[gb], ldq), (s_ac, vtc_ac[0])]
                    p.emit("tensor", lambda E, o=psm,
                           l=wvb[:, c * D + dc_ * 128:c * D + (dc_ + 1) * 128],
                           r=blk[gb][:, c * TB:(c + 1) * TB],
                           st=(c == 0), sp=(c == 7):
                           E.matmul(o, l, r, start=st, stop=sp),
                           wait=wl, inc=s_mm)
                gv = p.fence("tensor", s_mm)
                p.emit("scalar", lambda E, psm=psm, dc_=dc_:
                       E.activation(vtc[:], psm, AF.Identity,
                                    bias=bvc[:, dc_:dc_ + 1]),
                       wait=[(s_mm, gv), (s_ac, p.cnt(s_ac))], inc=s_ac)
                vtc_ac[0] = p.fence("scalar", s_ac)
                for t4 in range(4):
                    i, bank, freev = tring.next()
                    p.emit("tensor", lambda E, bank=bank, t4=t4:
                           E.transpose(bank[:, 0:128],
                                       vtc[:, t4 * 128:(t4 + 1) * 128],
                                       com.ident_f[:]),
                           wait=[(s_ac, vtc_ac[0]), (s_dv, freev)], inc=s_mm)
                    gv2 = p.fence("tensor", s_mm)
                    kt = tb * 4 + t4
                    p.emit("vector", lambda E, kt=kt, dc_=dc_, bank=bank:
                           E.tensor_copy(
                               vw.ap().rearrange("q (t h e) -> q t h e",
                                                 h=16, e=HW_)
                               [:, kt, dc_ * 2:dc_ * 2 + 2, 0:64],
                               bank[:, 0:128].rearrange("q (h e) -> q h e",
                                                        e=64)),
                           wait=[(s_mm, gv2), (s_dv, p.cnt(s_dv))], inc=s_dv)
                    tring.free[i] = p.cnt(s_dv)
            vt_mm[0] = p.cnt(s_mm)
        p.fence("vector", s_dv)
        p.barrier()
        wstack.close()

        ropeC = p.sb(pref + "_ropec", [128, S], BF16)
        ropeS = p.sb(pref + "_ropes", [128, S], BF16)
        wstack = ExitStack()
        stg = wstack.enter_context(p.nc.sbuf_tensor(pref + "_stg3",
                                                    [128, S], F32,
                                                    side="right"))
        for j, (key, dst) in enumerate((("rope_c", ropeC), ("rope_s", ropeS))):
            wl = [] if j == 0 else [(s_wcv, p.cnt(s_wcv))]
            p.emit("sync", lambda E, d=stg[:], sr=w[key]:
                   E.dma_start(d, sr), wait=wl, inc=s_wl2[j], inck=16)
            p.emit("vector", lambda E, d=dst[:], sr=stg[:]:
                   E.tensor_copy(d, sr), wait=[(s_wl2[j], p.cnt(s_wl2[j]))],
                   inc=s_wcv)
            p.fence("vector", s_wcv)
        p.barrier()
        wstack.close()

        wqp = p.sb(pref + "_wqp", [128, 8 * 128], BF16)
        wkp = p.sb(pref + "_wkp", [128, 8 * 128], BF16)
        qt = p.sb(pref + "_qt", [128, S], BF16)
        kt_ = p.sb(pref + "_kt", [128, S], BF16)
        swb = p.sb(pref + "_swb", [128, S], BF16)
        pt = p.sb(pref + "_pt", [128, 2048], BF16)
        cstage = p.sb(pref + "_cstage", [128, 1024], F32)
        ctout = p.sb(pref + "_ctout", [128, TPB * 128], F32)
        rcl = p.sb(pref + "_rcl", [128, 2], F32)
        stgp = p.sb(pref + "_stgp", [128, 256], F32)

        last_exp = [0]
        last_ctev = [0]
        last_projmm = [vt_mm[0]]
        last_ctx_st = [0]
        rope_dv = [0]

        for pair in range(8):
            base_cv = p.cnt(s_wcv)
            for j, (wkey, dst) in enumerate((("wq", wqp), ("wk", wkp))):
                for k in range(8):
                    bpar = k % 2
                    idx = j * 8 + k
                    wait = [(s_wcv, base_cv + idx - 1)] if idx >= 2 else \
                        [(s_wcv, base_cv)]
                    p.emit("sync", lambda E,
                           d=stgp[:, bpar * 128:(bpar + 1) * 128],
                           sr=w[wkey][k * 128:(k + 1) * 128,
                                      pair * 128:(pair + 1) * 128]:
                           E.dma_start(d, sr), wait=wait, inc=s_wl2[bpar],
                           inck=16)
                    p.emit("vector", lambda E,
                           d=dst[:, k * 128:(k + 1) * 128],
                           sr=stgp[:, bpar * 128:(bpar + 1) * 128]:
                           E.tensor_copy(d, sr),
                           wait=[(s_wl2[bpar], p.cnt(s_wl2[bpar])),
                                 (s_mm, last_projmm[0]),
                                 (s_dv, p.cnt(s_dv))], inc=s_wcv)
            wpair_done = p.fence("vector", s_wcv)
            qk_ev = {}
            for tb in range(8):
                gb = tb % 2
                wl = [(s_mm, qk_ev.get(("mm", tb - 2), last_projmm[0]))]
                p.emit("sync", lambda E, gb=gb, tb=tb: E.dma_start(
                    blk[gb].ap().rearrange("q (c w) -> q c w", w=TB),
                    xnq_dram[:, :, tb * TB:(tb + 1) * TB].rearrange(
                        "c q w -> q c w")),
                       wait=wl, inc=s_xq[gb], inck=16)
                ldq = p.cnt(s_xq[gb])
                for j, (wsb_, dst, bcl) in enumerate(((wqp, qt, bqc),
                                                      (wkp, kt_, bkc))):
                    pr = tring.banks[j]
                    for c in range(8):
                        wl = []
                        if c == 0:
                            wl = [(s_xq[gb], ldq), (s_wcv, wpair_done),
                                  (s_dv, qk_ev.get((j, tb - 1),
                                                   tring.free[j]))]
                        p.emit("tensor", lambda E, o=pr[:, 0:TB],
                               l=wsb_[:, c * 128:(c + 1) * 128],
                               r=blk[gb][:, c * TB:(c + 1) * TB],
                               st=(c == 0), sp=(c == 7):
                               E.matmul(o, l, r, start=st, stop=sp),
                               wait=wl, inc=s_mm)
                    gv = p.fence("tensor", s_mm)
                    p.emit("vector", lambda E, d=dst[:, tb * TB:(tb + 1) * TB],
                           sp=pr[:, 0:TB], bcl=bcl[:, pair:pair + 1]:
                           E.tensor_scalar_add(d, sp, bcl),
                           wait=[(s_mm, gv), (s_dv, p.cnt(s_dv))], inc=s_dv)
                    qk_ev[(j, tb)] = p.cnt(s_dv)
                qk_ev[("mm", tb)] = p.cnt(s_mm)
            last_projmm[0] = p.cnt(s_mm)
            tring.free[0] = qk_ev[(0, 7)]
            tring.free[1] = qk_ev[(1, 7)]
            for dst in (qt, kt_):
                for (o_, i_) in ((0, 32), (32, 0), (64, 96), (96, 64)):
                    dvc(lambda E, o_=o_, i_=i_, dst=dst:
                        E.tensor_copy(swb[o_:o_ + 32, :], dst[i_:i_ + 32, :]))
                dvc(lambda E: E.tensor_mul(swb[:], swb[:], ropeS[:]))
                dvc(lambda E, dst=dst: E.tensor_mul(dst[:], dst[:], ropeC[:]))
                dvc(lambda E, dst=dst: E.tensor_add(dst[:], dst[:], swb[:]))
            rope_dv[0] = p.fence("vector", s_dv)

            exp_slot = [0, 0]
            for qb in range(8):
                for kc in range(32):
                    sl = kc % 2
                    psl = ps_sc2[sl]
                    for hh, base in enumerate((0, 64)):
                        wl = []
                        if hh == 0:
                            wl = [(s_ac, exp_slot[sl]), (s_dv, rope_dv[0])]
                        p.emit("tensor", lambda E,
                               o=psl[:, hh * 512:(hh + 1) * 512],
                               l=kt_[base:base + 64,
                                     kc * 128:(kc + 1) * 128],
                               r=qt[base:base + 64, qb * TB:(qb + 1) * TB],
                               tp=(base, 0):
                               E.matmul(o, l, r, start=True, stop=True,
                                        tile_position=tp),
                               wait=wl, inc=s_mm)
                    gv = p.cnt(s_mm)
                    p.emit("scalar", lambda E, sl=sl, psl=psl: E.activation(
                        pt[:, sl * 1024:(sl + 1) * 1024], psl[:, 0:1024],
                        AF.Exp, scale=0.125),
                           wait=[(s_mm, gv), (s_ac, p.cnt(s_ac))], inc=s_ac)
                    exp_slot[sl] = p.cnt(s_ac)
                    last_exp[0] = exp_slot[sl]
                    for hh, ps_cx in enumerate((ps_cxA, ps_cxB)):
                        h = pair * 2 + hh
                        wl = []
                        if hh == 0:
                            wl = [(s_ac, exp_slot[sl])]
                            if kc == 0:
                                wl.append((s_dv, last_ctev[0]))
                        p.emit("tensor", lambda E, o=ps_cx[0:65, 0:TB],
                               l=vw[:, kc * 1040 + h * HW_:
                                    kc * 1040 + (h + 1) * HW_],
                               r=pt[:, sl * 1024 + hh * 512:
                                    sl * 1024 + (hh + 1) * 512],
                               st=(kc == 0), sp=(kc == 31):
                               E.matmul(o, l, r, start=st, stop=sp),
                               wait=wl, inc=s_mm)
                pv_gv = p.fence("tensor", s_mm)
                for hh, ps_cx in enumerate((ps_cxA, ps_cxB)):
                    p.emit("vector", lambda E, hh=hh, ps_cx=ps_cx:
                           E.tensor_copy(cstage[0:65, hh * 512:hh * 512 + 512],
                                         ps_cx[0:65, 0:TB]),
                           wait=[(s_mm, pv_gv), (s_dv, p.cnt(s_dv)),
                                 (s_cx, last_ctx_st[0])], inc=s_dv)
                cs_dv = p.fence("vector", s_dv)
                last_ctev[0] = cs_dv
                for t4 in range(TPB):
                    i, bank, freev = tring.next()
                    for hh in range(2):
                        p.emit("tensor", lambda E, hh=hh, t4=t4, bank=bank:
                               E.transpose(bank[:, hh * 65:hh * 65 + 65],
                                           cstage[0:65,
                                                  hh * 512 + t4 * 128:
                                                  hh * 512 + (t4 + 1) * 128],
                                           com.ident_f[0:65, 0:65]),
                               wait=[(s_dv, max(cs_dv, freev))] if hh == 0
                               else [], inc=s_mm)
                    gv2 = p.fence("tensor", s_mm)
                    dvc(lambda E, bank=bank: E.reciprocal(
                        rcl[:, 0:1], bank[:, 64:65]), wait=[(s_mm, gv2)])
                    dvc(lambda E, bank=bank: E.reciprocal(
                        rcl[:, 1:2], bank[:, 129:130]))
                    for hh in range(2):
                        dvc(lambda E, hh=hh, t4=t4, bank=bank:
                            E.tensor_scalar_mul(
                                ctout[:, t4 * 128 + hh * 64:
                                      t4 * 128 + (hh + 1) * 64],
                                bank[:, hh * 65:hh * 65 + 64],
                                rcl[:, hh:hh + 1]))
                    tring.free[i] = p.cnt(s_dv)
                ct_fence = p.fence("vector", s_dv)
                dst = ctx_dram[qb * TB:(qb + 1) * TB,
                               pair * 128:(pair + 1) * 128]
                p.emit("sync", lambda E, d=dst: E.dma_start(
                    d.rearrange("(t q) c -> q t c", q=128),
                    ctout.ap().rearrange("q (t c) -> q t c", c=128)),
                       wait=[(s_dv, ct_fence)], inc=s_cx, inck=16)
                last_ctx_st[0] = p.cnt(s_cx)
        p.emit("sync", lambda E: E.nop(), wait=[(s_cx, p.cnt(s_cx))])
        p.barrier()

    L.xg = [p.sb(pref + f"_xgb{i}", [128, TPB * D], F32) for i in range(2)]
    cg = [p.sb(pref + f"_cg{i}", [128, TPB * D], F32) for i in range(2)]
    xo = [p.sb(pref + f"_xo{i}", [128, TPB * D], F32) for i in range(2)]
    bapply = {}
    st_done = {}
    for g in range(8):
        gb = g % 2
        for t in range(TPB):
            gi = g * TPB + t
            wl = []
            if g >= 2:
                wl.append((s_dv, bapply[g - 2]))
            p.emit("sync", lambda E, d=L.xg[gb][:, t * D:(t + 1) * D],
                   sr=x_ap[gi * 128:(gi + 1) * 128, :]: E.dma_start(d, sr),
                   wait=wl, inc=s_ld[gb], inck=16)
            p.emit("sync", lambda E, d=cg[gb][:, t * D:(t + 1) * D],
                   sr=ctx_dram[gi * 128:(gi + 1) * 128, :]: E.dma_start(d, sr),
                   wait=[], inc=s_ld[gb], inck=16)
        ldv = p.cnt(s_ld[gb])
        for t in range(TPB):
            gi = g * TPB + t
            ct = cg[gb][:, t * D:(t + 1) * D]
            p.emit("vector", lambda E, x=ct, c=L.fstats_s[:, gi:gi + 1]:
                   E.tensor_reduce(c, x, AX.X, ALU.add),
                   wait=[(s_ld[gb], ldv)], inc=s_dv)
            p.emit("scalar", lambda E, x=ct, c=L.fstats_q[:, gi:gi + 1]:
                   E.activation(sqdum[:], x, AF.Square, accum_out=c),
                   wait=[(s_ld[gb], ldv), (s_ac, p.cnt(s_ac))], inc=s_ac)
        p.fence("vector", s_dv)
        p.fence("scalar", s_ac)
        ffv = ln_finalize(p, L.fstats_s, L.fstats_q, L.fstats_m, L.fstats_r,
                          g * TPB, (g + 1) * TPB,
                          [(s_dv, p.cnt(s_dv)), (s_ac, p.cnt(s_ac))],
                          s_dv, s_ac)
        for t in range(TPB):
            gi = g * TPB + t
            wl = [(s_dv, ffv)]
            if t == 0 and g >= 2:
                wl.append((s_st[gb], st_done[g - 2]))
            p.emit("vector", lambda E, d=xo[gb][:, t * D:(t + 1) * D],
                   f=cg[gb][:, t * D:(t + 1) * D],
                   m=L.fstats_m[:, gi:gi + 1], r=L.fstats_r[:, gi:gi + 1]:
                   E.tensor_scalar(d, f, m, r, ALU.subtract, ALU.mult),
                   wait=wl, inc=s_dv)
            p.emit("vector", lambda E, d=xo[gb][:, t * D:(t + 1) * D],
                   x=L.xg[gb][:, t * D:(t + 1) * D]:
                   E.tensor_add(d, d, x), wait=[(s_dv, p.cnt(s_dv))], inc=s_dv)
        bapply[g] = p.fence("vector", s_dv)
        for t in range(TPB):
            gi = g * TPB + t
            p.emit("sync", lambda E, d=y_ap[gi * 128:(gi + 1) * 128, :],
                   sr=xo[gb][:, t * D:(t + 1) * D]: E.dma_start(d, sr),
                   wait=[(s_dv, bapply[g])] if t == 0 else (),
                   inc=s_st[gb], inck=16)
        st_done[g] = p.cnt(s_st[gb])
    p.emit("sync", lambda E: E.nop(), wait=[(s_st[0], p.cnt(s_st[0]))])
    p.emit("sync", lambda E: E.nop(), wait=[(s_st[1], p.cnt(s_st[1]))])
    p.barrier()


def build_l0_attn_program():
    nc = bass.Bass("TRN2", target_bir_lowering=False)
    x = nc.dram_tensor("x", [S, D], F32, kind="ExternalInput").ap()
    y = nc.dram_tensor("y", [S, D], F32, kind="ExternalOutput").ap()
    shapes = {"wq": [D, D], "wk": [D, D], "wv": [D, D],
              "bq": [1, D], "bk": [1, D], "bv": [1, D]}
    w = {k: nc.dram_tensor(k, v, F32, kind="ExternalInput").ap()
         for k, v in shapes.items()}
    capz = _declare_const_inputs(nc, ["ident", "rope_c", "rope_s"])
    w["rope_c"] = capz["rope_c"]
    w["rope_s"] = capz["rope_s"]
    xnq = nc.dram_tensor("xnq", [8, 128, S], BF16, kind="Internal").ap()
    ctxd = nc.dram_tensor("ctxd", [S, D], F32, kind="Internal").ap()
    with ExitStack() as stack:
        p = P(nc, stack)
        com = Common(p, capz["ident"])
        with p.scope():
            build_l0_attn(p, com, x, y, w, xnq, ctxd)
        p.flush()
    return nc


def run_l0_attn(x_full, w_np):
    nc = build_l0_attn_program()
    perm = rope_perm()
    in_maps = []
    for c in range(NCORES):
        m = {"x": np.ascontiguousarray(x_full[c]),
             "wq": np.ascontiguousarray(w_np["Wq"][:, perm]),
             "wk": np.ascontiguousarray(w_np["Wk"][:, perm]),
             "wv": np.ascontiguousarray(w_np["Wv"]),
             "bq": np.ascontiguousarray(w_np["bq"][perm].reshape(1, -1)),
             "bk": np.ascontiguousarray(w_np["bk"][perm].reshape(1, -1)),
             "bv": np.ascontiguousarray(w_np["bv"].reshape(1, -1))}
        m.update(_const_in_map(["ident", "rope_c", "rope_s"]))
        in_maps.append(m)
    res = run_bass_kernel_spmd(nc, in_maps, core_ids=list(range(NCORES)))
    return np.stack([r["y"] for r in res.results])


_CONSTS = host_constants()


def _declare_const_inputs(nc, names):
    aps = {}
    for n in names:
        arr = _CONSTS[n]
        aps[n] = nc.dram_tensor("c_" + n, list(arr.shape), F32,
                                kind="ExternalInput").ap()
    return aps


def _const_in_map(names):
    return {"c_" + n: _CONSTS[n] for n in names}


def build_l0_ffn_program():
    nc = bass.Bass("TRN2", target_bir_lowering=False)
    x = nc.dram_tensor("x", [S, D], F32, kind="ExternalInput").ap()
    y = nc.dram_tensor("y", [S, D], F32, kind="ExternalOutput").ap()
    w = {
        "U1": nc.dram_tensor("u1", [D, 256], F32, kind="ExternalInput").ap(),
        "V1": nc.dram_tensor("v1", [256, 4096], F32, kind="ExternalInput").ap(),
        "b1": nc.dram_tensor("b1", [1, 4096], F32, kind="ExternalInput").ap(),
        "U2": nc.dram_tensor("u2", [4096, 256], F32, kind="ExternalInput").ap(),
        "V2": nc.dram_tensor("v2", [256, 1024], F32, kind="ExternalInput").ap(),
        "b2": nc.dram_tensor("b2", [1, 1024], F32, kind="ExternalInput").ap(),
    }
    capz = _declare_const_inputs(nc, ["ident"])
    with ExitStack() as stack:
        p = P(nc, stack)
        com = Common(p, capz["ident"])
        with p.scope():
            build_l0_ffn(p, com, x, y, w)
        p.flush()
    return nc


def run_l0_ffn(x_full, w_np):
    nc = build_l0_ffn_program()
    in_maps = []
    for c in range(NCORES):
        m = {"x": np.ascontiguousarray(x_full[c]),
             "u1": w_np["U1"], "v1": w_np["V1"],
             "b1": w_np["b1"].reshape(1, -1),
             "u2": w_np["U2"], "v2": w_np["V2"],
             "b2": w_np["b2"].reshape(1, -1)}
        m.update(_const_in_map(["ident"]))
        in_maps.append(m)
    res = run_bass_kernel_spmd(nc, in_maps, core_ids=list(range(NCORES)))
    return np.stack([r["y"] for r in res.results])




def build_full_program():
    nc = bass.Bass("TRN2", target_bir_lowering=False)
    x = nc.dram_tensor("x", [S, D], F32, kind="ExternalInput").ap()
    y = nc.dram_tensor("y", [S, D], F32, kind="ExternalOutput").ap()
    sc1 = nc.dram_tensor("sc1", [S, D], F32, kind="Internal").ap()
    sc2 = nc.dram_tensor("sc2", [S, D], F32, kind="Internal").ap()
    sc3 = nc.dram_tensor("sc3", [S, D], F32, kind="Internal").ap()
    xnq = nc.dram_tensor("xnq", [8, 128, S], BF16, kind="Internal").ap()
    ctxd = nc.dram_tensor("ctxd", [S, D], F32, kind="Internal").ap()

    wA = {k: nc.dram_tensor("A" + k, v, F32, kind="ExternalInput").ap()
          for k, v in (("wq", [D, D]), ("wk", [D, D]), ("wv", [D, D]),
                       ("bq", [1, D]), ("bk", [1, D]), ("bv", [1, D]))}
    wF = {k: nc.dram_tensor("F" + k, v, F32, kind="ExternalInput").ap()
          for k, v in (("U1", [D, 256]), ("V1", [256, 4096]),
                       ("b1", [1, 4096]), ("U2", [4096, 256]),
                       ("V2", [256, D]), ("b2", [1, D]))}
    wB = {k: nc.dram_tensor("B" + k, v, F32, kind="ExternalInput").ap()
          for k, v in (("Wq", [D, D]), ("Wk", [D, D]), ("Wv", [D, D]),
                       ("bq", [1, D]), ("bk", [1, D]), ("bv", [1, D]))}
    wG = {k: nc.dram_tensor("G" + k, v, F32, kind="ExternalInput").ap()
          for k, v in (("cn_w1", [D, D]), ("cn_w2", [D, 512]),
                       ("cn_w3", [512, 256]), ("cn_w4", [256, 1]),
                       ("gc_w1", [D, 512]), ("gc_w2", [512, 256]),
                       ("gc_w3", [256, 1]), ("cn_b1", [1, D]),
                       ("cn_b2", [1, 512]), ("cn_b3", [1, 256]),
                       ("cn_b4", [1, 1]), ("gc_b1", [1, 512]),
                       ("gc_b2", [1, 256]), ("gc_b3", [1, 1]),
                       ("ar", [1, 1]))}
    capz = _declare_const_inputs(nc, ["ident", "rope_c", "rope_s",
                                      "dft_f", "dft_i"])
    wA["rope_c"] = capz["rope_c"]
    wA["rope_s"] = capz["rope_s"]
    wB["dft_f"] = capz["dft_f"]
    wB["dft_i"] = capz["dft_i"]

    with ExitStack() as stack:
        p = P(nc, stack)
        com = Common(p, capz["ident"])
        with p.scope():
            build_l0_attn(p, com, x, sc1, wA, xnq, ctxd)
        with p.scope():
            build_l0_ffn(p, com, sc1, sc2, wF)
        with p.scope():
            build_l1_attn(p, com, sc2, sc3, wB)
        with p.scope():
            build_l1_ffn(p, com, sc3, y, wG)
        p.flush()
    return nc


_FUSED_NC = [None]


def kernel_fused(**inputs):
    if _FUSED_NC[0] is None:
        _FUSED_NC[0] = build_full_program()
    nc = _FUSED_NC[0]
    perm = rope_perm()
    f32 = lambda a: np.ascontiguousarray(np.asarray(a, np.float32))
    base = {}
    base["Awq"] = f32(inputs["L0_attn_Wq"])[:, perm]
    base["Awk"] = f32(inputs["L0_attn_Wk"])[:, perm]
    base["Awv"] = f32(inputs["L0_attn_Wv"])
    base["Abq"] = f32(inputs["L0_attn_bq"])[perm].reshape(1, -1)
    base["Abk"] = f32(inputs["L0_attn_bk"])[perm].reshape(1, -1)
    base["Abv"] = f32(inputs["L0_attn_bv"]).reshape(1, -1)
    for k in ("U1", "V1", "b1", "U2", "V2", "b2"):
        v = f32(inputs["L0_ffn_" + k])
        base["F" + k] = v.reshape(1, -1) if v.ndim == 1 else v
    for k in ("Wq", "Wk", "Wv", "bq", "bk", "bv"):
        v = f32(inputs["L1_attn_" + k])
        base["B" + k] = v.reshape(1, -1) if v.ndim == 1 else v
    for k in ("cn_w1", "cn_w2", "cn_w3", "cn_w4", "gc_w1", "gc_w2", "gc_w3",
              "cn_b1", "cn_b2", "cn_b3", "cn_b4", "gc_b1", "gc_b2", "gc_b3"):
        v = f32(inputs["L1_ffn_" + k])
        base["G" + k] = v.reshape(1, -1) if v.ndim == 1 else v
    base["Gar"] = f32(inputs["L1_ffn_adaptive_range"]).reshape(1, 1)
    for k, v in _CONSTS.items():
        base["c_" + k] = v
    xin = f32(inputs["L0_attn_hidden_states"])
    in_maps = []
    for c in range(NCORES):
        m = dict(base)
        m["x"] = np.ascontiguousarray(xin[c])
        in_maps.append(m)
    res = run_bass_kernel_spmd(nc, in_maps, core_ids=list(range(NCORES)))
    return np.ascontiguousarray(
        np.stack([r["y"] for r in res.results]).astype(np.float32))


def kernel(**inputs):
    return kernel_fused(**inputs)


def kernel_multi(**inputs):
    x = np.ascontiguousarray(np.asarray(inputs["L0_attn_hidden_states"],
                                        np.float32))
    wa = {"Wq": inputs["L0_attn_Wq"], "Wk": inputs["L0_attn_Wk"],
          "Wv": inputs["L0_attn_Wv"], "bq": inputs["L0_attn_bq"],
          "bk": inputs["L0_attn_bk"], "bv": inputs["L0_attn_bv"]}
    wa = {k: np.asarray(v, np.float32) for k, v in wa.items()}
    x = run_l0_attn(x, wa)

    wf = {k: np.asarray(inputs["L0_ffn_" + k], np.float32)
          for k in ("U1", "V1", "b1", "U2", "V2", "b2")}
    x = run_l0_ffn(x, wf)

    wb = {k: np.asarray(inputs["L1_attn_" + k], np.float32)
          for k in ("Wq", "Wk", "Wv", "bq", "bk", "bv")}
    x = run_l1_attn(x, wb)

    wg = {k: np.asarray(inputs["L1_ffn_" + k], np.float32)
          for k in ("cn_w1", "cn_b1", "cn_w2", "cn_b2", "cn_w3", "cn_b3",
                    "cn_w4", "cn_b4", "gc_w1", "gc_b1", "gc_w2", "gc_b2",
                    "gc_w3", "gc_b3")}
    wg["ar"] = np.asarray(inputs["L1_ffn_adaptive_range"], np.float32)
    x = run_l1_ffn(x, wg)
    return np.ascontiguousarray(x.astype(np.float32))


# version 08
# speedup vs baseline: 1.1444x; 1.1444x over previous
import math
import numpy as np
from contextlib import ExitStack, contextmanager

import concourse.bass as bass
import concourse.mybir as mybir
from concourse.bass_utils import run_bass_kernel_spmd

F32 = mybir.dt.float32
BF16 = mybir.dt.bfloat16
AF = mybir.ActivationFunctionType
ALU = mybir.AluOpType
AX = mybir.AxisListType

NCORES = 8
S = 4096
D = 1024
NT = S // 128
TB = 512
NB = S // TB
TPB = TB // 128
LN_EPS = 1e-5

ENGS = ("sync", "tensor", "vector", "scalar", "gpsimd")
DEBUG = False
PARANOID = False
_DBG = {}


def dbg_dump(p, name, sap, shape, dt, wait):
    if not DEBUG:
        return
    if "dbgs" not in p._sems:
        p.sem("dbgs")
    t = p.nc.dram_tensor("dbg_" + name, list(shape), dt, kind="ExternalOutput")
    w = list(wait) + [("dbgs", p.cnt("dbgs"))]
    p.emit("gpsimd", lambda E: E.dma_start(t.ap(), sap), wait=w,
           inc="dbgs", inck=16)
    p.emit("gpsimd", lambda E: E.nop(), wait=[("dbgs", p.cnt("dbgs"))])


def host_constants():
    c = {}
    c["ident"] = np.eye(128, dtype=np.float32)
    j = np.arange(32, dtype=np.float64)
    inv = 10000.0 ** (-2.0 * j / 64.0)
    t = np.arange(S, dtype=np.float64)
    ang = np.outer(inv, t)
    cos = np.cos(ang)
    sin = np.sin(ang)
    c["rope_c"] = np.tile(cos, (4, 1)).astype(np.float32)
    c["rope_s"] = np.concatenate([-sin, sin, -sin, sin], 0).astype(np.float32)
    N = D
    e = np.arange(N, dtype=np.float64)
    k = np.arange(N, dtype=np.float64)
    Df = np.empty((N, N), dtype=np.float64)
    Df[:, 0:512] = np.cos(2 * np.pi * np.outer(e, k[0:512]) / N)
    Df[:, 512] = np.cos(np.pi * e)
    Df[:, 513:1024] = -np.sin(2 * np.pi * np.outer(e, k[1:512]) / N)
    c["dft_f"] = Df.astype(np.float32)
    Di = np.empty((N, N), dtype=np.float64)
    n = np.arange(N, dtype=np.float64)
    Di[0, :] = 1.0 / N
    Di[1:512, :] = 2.0 * np.cos(2 * np.pi * np.outer(k[1:512], n) / N) / N
    Di[512, :] = np.cos(np.pi * n) / N
    Di[513:1024, :] = -2.0 * np.sin(2 * np.pi * np.outer(k[1:512], n) / N) / N
    c["dft_i"] = Di.astype(np.float32)
    return c


def rope_perm():
    p = []
    for h in range(16):
        base = h * 64
        p += [base + 2 * i for i in range(32)]
        p += [base + 2 * i + 1 for i in range(32)]
    return np.array(p, dtype=np.int64)


class P:
    def __init__(self, nc, stack):
        self.nc = nc
        self.prog_stack = stack
        self.stack = stack
        self.q = {e: [] for e in ENGS}
        self._sems = {}
        self.sem("bar")

    def sem(self, name):
        h = self.prog_stack.enter_context(self.nc.semaphore(name))
        self._sems[name] = [h, 0]
        return name

    def h(self, name):
        return self._sems[name][0]

    def bump(self, name, k=1):
        self._sems[name][1] += k
        return self._sems[name][1]

    def cnt(self, name):
        return self._sems[name][1]

    def on(self, eng, fn):
        self.q[eng].append(fn)

    def emit(self, eng, fn, wait=(), inc=None, inck=1):
        pairs = [(self.h(s), v) for (s, v) in wait if v > 0]
        hinc = self.h(inc) if inc else None
        if inc:
            self.bump(inc, inck)

        def f(E, pairs=pairs, fn=fn, hinc=hinc, inck=inck):
            for h, v in pairs:
                E.wait_ge(h, v)
            ins = fn(E)
            if hinc is not None:
                ins.then_inc(hinc, inck)
        self.on(eng, f)
        return self.cnt(inc) if inc else None

    def dma_barrier(self, sem):
        self.emit("sync", lambda E: E.nop(), wait=[(sem, self.cnt(sem))])

    def fence(self, eng, sem):
        self.emit(eng, lambda E: E.drain())
        self.emit(eng, lambda E: E.nop(), inc=sem)
        return self.cnt(sem)

    def barrier(self):
        tgt = self.bump("bar", len(ENGS))
        h = self.h("bar")
        for e in ENGS:
            def f(E, h=h, t=tgt, e=e):
                if e in ("vector", "scalar", "gpsimd"):
                    E.drain()
                E.nop().then_inc(h, 1)
                E.wait_ge(h, t)
            self.on(e, f)

    def flush(self):
        with self.nc.Block() as blk:
            for e in ENGS:
                getattr(blk, e)(lambda E, fns=tuple(self.q[e]): [f(E) for f in fns])
        self.q = {e: [] for e in ENGS}

    def sb(self, name, shape, dt, side=None):
        return self.stack.enter_context(
            self.nc.sbuf_tensor(name, shape, dt, side=side))

    def ps(self, name, shape, dt=F32):
        return self.stack.enter_context(self.nc.psum_tensor(name, shape, dt))

    @contextmanager
    def scope(self):
        old = self.stack
        with ExitStack() as st:
            self.stack = st
            try:
                yield
            finally:
                self.stack = old


class Common:

    def __init__(self, p, ident_ap):
        self.ident_f = p.sb("c_ident_f", [128, 128], F32)
        self.ident_b = p.sb("c_ident_b", [128, 128], BF16)
        self.ones11 = p.sb("c_ones11", [1, 1], F32)
        self.ones_col = p.sb("c_ones_col", [128, 1], F32)
        ld = p.sem("c_ld")
        p.emit("sync", lambda E: E.dma_start(self.ident_f[:], ident_ap),
               inc=ld, inck=16)
        p.emit("vector", lambda E: E.tensor_copy(self.ident_b[:], self.ident_f[:]),
               wait=[(ld, 16)])
        p.emit("gpsimd", lambda E: E.memset(self.ones11[:], 1.0))
        p.emit("gpsimd", lambda E: E.memset(self.ones_col[:], 1.0))
        p.barrier()


def load_weight_bf16(p, name, dram_ap, kdim, mdim, s_ld2, s_cv, stg):
    kc = kdim // 128
    wsb = p.sb(name, [128, kc * mdim], BF16)
    base_cv = p.cnt(s_cv)
    for k in range(kc):
        b = k % 2
        if k >= 2:
            wait = [(s_cv, base_cv + k - 1)]
        else:
            wait = [(s_cv, base_cv)]
        src = dram_ap[k * 128:(k + 1) * 128, :]
        p.emit("sync", lambda E, d=stg[:, b * mdim:(b + 1) * mdim], s=src:
               E.dma_start(d, s), wait=wait, inc=s_ld2[b], inck=16)
        p.emit("vector", lambda E, d=wsb[:, k * mdim:(k + 1) * mdim],
               s=stg[:, b * mdim:(b + 1) * mdim]: E.tensor_copy(d, s),
               wait=[(s_ld2[b], p.cnt(s_ld2[b]))], inc=s_cv)
    return wsb


def row_to_cols(p, com, name, row_ap_getter, nchunks, psum, s_mm, s_dv,
                dt=F32):
    out = p.sb(name, [128, nchunks], dt)
    for c in range(nchunks):
        p.emit("tensor", lambda E, o=psum[:, c:c + 1], l=row_ap_getter(c):
               E.matmul(o, l, com.ones11[:], start=True, stop=True), inc=s_mm)
    gv = p.cnt(s_mm)
    p.emit("vector", lambda E, d=out[:, 0:nchunks], s=psum[:, 0:nchunks]:
           E.tensor_copy(d, s), wait=[(s_mm, gv)], inc=s_dv)
    return out


class LNBufs:
    def __init__(self, p, pref, sqd=True, xg=True, xo=True):
        self.xg = [p.sb(f"{pref}_xg{i}", [128, TPB * D], F32)
                   for i in range(2)] if xg else None
        self.xn_tok = p.sb(f"{pref}_xnt", [128, TPB * D], F32)
        self.stats_s = p.sb(f"{pref}_ss", [128, NT], F32)
        self.stats_q = p.sb(f"{pref}_sq", [128, NT], F32)
        self.stats_m = p.sb(f"{pref}_sm", [128, NT], F32)
        self.stats_r = p.sb(f"{pref}_sr", [128, NT], F32)
        self.sqd = p.sb(f"{pref}_sqd", [128, D], BF16) if sqd else None
        self.fstats_s = p.sb(f"{pref}_fs", [128, NT], F32)
        self.fstats_q = p.sb(f"{pref}_fq", [128, NT], F32)
        self.fstats_m = p.sb(f"{pref}_fm", [128, NT], F32)
        self.fstats_r = p.sb(f"{pref}_fr", [128, NT], F32)
        self.xo = [p.sb(f"{pref}_xo{i}", [128, TPB * D], F32)
                   for i in range(2)] if xo else None


def ln_finalize(p, ss, sq, sm, sr, c0, c1, wait, s_dv, s_ac):
    sl = lambda t: t[:, c0:c1]
    p.emit("vector", lambda E: E.tensor_scalar_mul(sl(sm), sl(ss), 1.0 / D),
           wait=wait, inc=s_dv)
    p.emit("vector", lambda E: E.drain())
    p.emit("vector", lambda E: E.tensor_mul(sl(sr), sl(sm), sl(sm)), inc=s_dv)
    p.emit("vector", lambda E: E.drain())
    p.emit("vector", lambda E: E.scalar_tensor_tensor(
        sl(sr), sl(sq), 1.0 / D, sl(sr), ALU.mult, ALU.subtract), inc=s_dv)
    p.emit("vector", lambda E: E.drain())
    p.emit("vector", lambda E: E.tensor_scalar_add(sl(sr), sl(sr), LN_EPS),
           inc=s_dv)
    p.emit("scalar", lambda E: E.activation(sl(sr), sl(sr), AF.Sqrt),
           wait=[(s_dv, p.cnt(s_dv))], inc=s_ac)
    p.emit("vector", lambda E: E.reciprocal(sl(sr), sl(sr)),
           wait=[(s_ac, p.cnt(s_ac))], inc=s_dv)
    p.emit("vector", lambda E: E.drain())
    return p.cnt(s_dv)


class TRing:

    def __init__(self, p, pref):
        self.banks = [p.ps(pref + f"_ptr{i}", [128, 512], F32) for i in range(2)]
        self.free = [0, 0]
        self.idx = 0

    def next(self):
        i = self.idx % 2
        self.idx += 1
        return i, self.banks[i], self.free[i]


def transpose_group(p, com, tring, srcs, dst_ap, s_mm, s_dv, dwait=(),
                    fp32=True, fence=False):
    i, bank, freev = tring.next()
    idn = com.ident_f[:] if fp32 else com.ident_b[:]
    for j, sap in enumerate(srcs):
        w = []
        if j == 0:
            w = list(dwait) + ([("__dv__", freev)] if freev > 0 else [])
            w = [(s_dv if s == "__dv__" else s, v) for (s, v) in w]
        p.emit("tensor", lambda E, o=bank[:, j * 128:(j + 1) * 128], s=sap,
               idn=idn: E.transpose(o, s, idn), wait=w, inc=s_mm)
    n = len(srcs)
    gv = p.cnt(s_mm)
    p.emit("vector", lambda E, d=dst_ap, s=bank[:, 0:n * 128]:
           E.tensor_copy(d, s), wait=[(s_mm, gv)], inc=s_dv)
    tring.free[i] = p.cnt(s_dv)
    if fence:
        return p.fence("vector", s_dv)
    return p.cnt(s_dv)


def build_l0_ffn(p, com, x_ap, y_ap, w, pref="f0"):
    RANK, DFF = 256, 4096

    s_ld = [p.sem(pref + "_ld0"), p.sem(pref + "_ld1")]
    s_st = [p.sem(pref + "_st0"), p.sem(pref + "_st1")]
    s_dv = p.sem(pref + "_dv")
    s_ac = p.sem(pref + "_ac")
    s_mm = p.sem(pref + "_mm")
    s_wld = p.sem(pref + "_wld")
    s_wl2 = [p.sem(pref + "_wl0"), p.sem(pref + "_wl1")]
    s_wcv = p.sem(pref + "_wcv")

    tring = TRing(p, pref)
    p_a = p.ps(pref + "_pa", [128, 512], F32)
    p_b = p.ps(pref + "_pb", [128, 512], F32)
    p_h = [p.ps(pref + f"_ph{i}", [128, 512], F32) for i in range(2)]
    p_q = [p.ps(pref + f"_pq{i}", [128, 512], F32) for i in range(2)]
    ring = [p_a, p_b]

    wstack = ExitStack()
    stg = wstack.enter_context(p.nc.sbuf_tensor(pref + "_stg", [128, 2 * DFF],
                                                F32, side="right"))
    brow = wstack.enter_context(p.nc.sbuf_tensor(pref + "_brow", [1, DFF + D],
                                                 F32, side="right"))
    p.emit("sync", lambda E: E.dma_start(brow[0:1, 0:DFF], w["b1"]),
           inc=s_wld, inck=16)
    p.emit("sync", lambda E: E.dma_start(brow[0:1, DFF:DFF + D], w["b2"]),
           inc=s_wld, inck=16)
    p.dma_barrier(s_wld)
    p.barrier()
    u1 = load_weight_bf16(p, pref + "_u1", w["U1"], D, RANK, s_wl2, s_wcv, stg)
    v1 = load_weight_bf16(p, pref + "_v1", w["V1"], RANK, DFF, s_wl2, s_wcv, stg)
    u2 = load_weight_bf16(p, pref + "_u2", w["U2"], DFF, RANK, s_wl2, s_wcv, stg)
    v2 = load_weight_bf16(p, pref + "_v2", w["V2"], RANK, D, s_wl2, s_wcv, stg)
    p.barrier()
    b1c = row_to_cols(p, com, pref + "_b1c",
                      lambda c: brow[0:1, c * 128:(c + 1) * 128], 32,
                      tring.banks[0][:, 0:32], s_mm, s_dv)
    p.barrier()
    b2c = row_to_cols(p, com, pref + "_b2c",
                      lambda c: brow[0:1, DFF + c * 128:DFF + (c + 1) * 128], 8,
                      tring.banks[0][:, 0:8], s_mm, s_dv)
    p.barrier()
    wstack.close()
    dbg_dump(p, "v2e", v2[:], [128, 2 * D], BF16, [])
    p.barrier()

    L = LNBufs(p, pref)
    xnT = [p.sb(pref + f"_xnT{i}", [128, 8 * TB], BF16) for i in range(2)]
    ptb = p.sb(pref + "_ptb", [128, 2 * TB], BF16)
    htb = p.sb(pref + "_htb", [128, 32 * TB], BF16)
    qtb = p.sb(pref + "_qtb", [128, 2 * TB], BF16)
    otf = p.sb(pref + "_otf", [128, 8 * TB], F32)
    ft = L.xn_tok

    xnT_consumed = {}
    xn_tr_mm = {}
    gelu_done = {}
    ht_read = {}
    ot_ac = {}
    ot_ac_last = {}
    q2_evac = {}
    bapply_done = {}
    store_done = {}

    pending_store = [None]

    def flush_store(p):
        if pending_store[0] is not None:
            fn = pending_store[0]
            pending_store[0] = None
            fn()

    for b in range(NB):
        bb = b % 2
        for t in range(TPB):
            gi = b * TPB + t
            wait = []
            if b >= 2:
                wait.append((s_dv, bapply_done[b - 2]))
            p.emit("sync", lambda E, d=L.xg[bb][:, t * D:(t + 1) * D],
                   s=x_ap[gi * 128:(gi + 1) * 128, :]: E.dma_start(d, s),
                   wait=wait, inc=s_ld[bb], inck=16)
        ldv = p.cnt(s_ld[bb])
        for t in range(TPB):
            gi = b * TPB + t
            xt = L.xg[bb][:, t * D:(t + 1) * D]
            ldw = [(s_ld[bb], ldv)]
            p.emit("vector", lambda E, x=xt, c=L.stats_s[:, gi:gi + 1]:
                   E.tensor_reduce(c, x, AX.X, ALU.add), wait=ldw, inc=s_dv)
            p.emit("scalar", lambda E, x=xt, c=L.stats_q[:, gi:gi + 1]:
                   E.activation(L.sqd[:], x, AF.Square, accum_out=c),
                   wait=ldw + [(s_ac, p.cnt(s_ac))], inc=s_ac)
        p.fence("vector", s_dv)
        p.fence("scalar", s_ac)
        finv = ln_finalize(p, L.stats_s, L.stats_q, L.stats_m, L.stats_r,
                           b * TPB, (b + 1) * TPB,
                           [(s_dv, p.cnt(s_dv)), (s_ac, p.cnt(s_ac))],
                           s_dv, s_ac)
        for t in range(TPB):
            gi = b * TPB + t
            wl = [(s_dv, finv)]
            if t == 0 and b >= 1:
                wl.append((s_mm, xn_tr_mm[b - 1]))
            p.emit("vector", lambda E, d=L.xn_tok[:, t * D:(t + 1) * D],
                   x=L.xg[bb][:, t * D:(t + 1) * D],
                   m=L.stats_m[:, gi:gi + 1], r=L.stats_r[:, gi:gi + 1]:
                   E.tensor_scalar(d, x, m, r, ALU.subtract, ALU.mult),
                   wait=wl, inc=s_dv)
        apply_done = p.fence("vector", s_dv)
        first = True
        for c in range(8):
            srcs = [L.xn_tok[:, t * D + c * 128:t * D + (c + 1) * 128]
                    for t in range(TPB)]
            dw = [(s_dv, apply_done)] if first else ()
            if first and b >= 2:
                dw = list(dw) + [(s_mm, xnT_consumed[b - 2])]
            transpose_group(p, com, tring, srcs,
                            xnT[bb][:, c * TB:(c + 1) * TB], s_mm, s_dv,
                            dwait=dw, fence=(c == 7))
            first = False
        xn_tr_mm[b] = p.cnt(s_mm)
        evxnT = p.cnt(s_dv)
        flush_store(p)
        if PARANOID:
            p.barrier()
        if b == 0:
            dbg_dump(p, "xnt", L.xn_tok[:], [128, TPB * D], F32,
                     [(s_dv, apply_done)])
            dbg_dump(p, "sts", L.stats_s[:], [128, NT], F32,
                     [(s_dv, apply_done)])
            dbg_dump(p, "stq", L.stats_q[:], [128, NT], F32,
                     [(s_dv, apply_done)])
            dbg_dump(p, "stm", L.stats_m[:], [128, NT], F32,
                     [(s_dv, apply_done)])
            dbg_dump(p, "str", L.stats_r[:], [128, NT], F32,
                     [(s_dv, apply_done)])
            dbg_dump(p, "xg0", L.xg[0][:], [128, TPB * D], F32,
                     [(s_dv, apply_done)])
            dbg_dump(p, "xnT", xnT[0][:], [128, 8 * TB], BF16,
                     [(s_dv, evxnT)])

        for rc in range(2):
            pr = ring[rc % 2]
            for k in range(8):
                wl = []
                if rc == 0 and k == 0:
                    wl = [(s_dv, evxnT)]
                    if b >= 1:
                        wl.append((s_ac, ot_ac_last[b - 1]))
                p.emit("tensor", lambda E, o=pr[:, 0:TB],
                       l=u1[:, k * RANK + rc * 128:k * RANK + (rc + 1) * 128],
                       r=xnT[bb][:, k * TB:(k + 1) * TB],
                       st=(k == 0), sp=(k == 7):
                       E.matmul(o, l, r, start=st, stop=sp), wait=wl, inc=s_mm)
            gv = p.cnt(s_mm)
            p.emit("vector", lambda E, d=ptb[:, rc * TB:(rc + 1) * TB],
                   s=pr[:, 0:TB]: E.tensor_copy(d, s),
                   wait=[(s_mm, gv)], inc=s_dv)
        xnT_consumed[b] = p.cnt(s_mm)
        pt_ready = p.fence("vector", s_dv)
        if PARANOID:
            p.barrier()
        if b == 0:
            dbg_dump(p, "ptb", ptb[:], [128, 2 * TB], BF16, [(s_dv, pt_ready)])

        for hc in range(32):
            ph = p_h[hc % 2]
            wl = [(s_dv, pt_ready)] if hc < 2 else []
            if hc >= 2:
                wl.append((s_ac, gelu_done[(b, hc - 2)]))
            elif b >= 1:
                wl.append((s_ac, gelu_done[(b - 1, 30 + hc)]))
            for rc in range(2):
                p.emit("tensor", lambda E, o=ph[:, 0:TB],
                       l=v1[:, rc * DFF + hc * 128:rc * DFF + (hc + 1) * 128],
                       r=ptb[:, rc * TB:(rc + 1) * TB],
                       st=(rc == 0), sp=(rc == 1):
                       E.matmul(o, l, r, start=st, stop=sp),
                       wait=wl if rc == 0 else (), inc=s_mm)
            gv = p.cnt(s_mm)
            p.emit("scalar", lambda E, d=htb[:, hc * TB:(hc + 1) * TB],
                   s=ph[:, 0:TB], bc=b1c[:, hc:hc + 1]:
                   E.activation(d, s, AF.Gelu_apprx_tanh, bias=bc),
                   wait=[(s_mm, gv)], inc=s_ac)
            gelu_done[(b, hc)] = p.fence("scalar", s_ac)
        for hc in range(32):
            for rc2 in range(2):
                wl = []
                if rc2 == 0:
                    wl.append((s_ac, gelu_done[(b, hc)]))
                if hc == 0 and rc2 == 0 and b >= 1:
                    wl.append((s_dv, q2_evac[b - 1]))
                p.emit("tensor", lambda E, o=p_q[rc2][:, 0:TB],
                       l=u2[:, hc * RANK + rc2 * 128:hc * RANK + (rc2 + 1) * 128],
                       r=htb[:, hc * TB:(hc + 1) * TB],
                       st=(hc == 0), sp=(hc == 31):
                       E.matmul(o, l, r, start=st, stop=sp), wait=wl, inc=s_mm)
            ht_read[(b, hc)] = p.cnt(s_mm)
        gv = p.cnt(s_mm)
        for rc2 in range(2):
            p.emit("vector", lambda E, d=qtb[:, rc2 * TB:(rc2 + 1) * TB],
                   s=p_q[rc2][:, 0:TB]: E.tensor_copy(d, s),
                   wait=[(s_mm, gv)], inc=s_dv)
        q2_evac[b] = p.fence("vector", s_dv)
        if PARANOID:
            p.barrier()
        if b == 0:
            dbg_dump(p, "htb", htb[:], [128, 32 * TB], BF16,
                     [(s_ac, gelu_done[(b, 31)])])
            dbg_dump(p, "qtb", qtb[:], [128, 2 * TB], BF16,
                     [(s_dv, q2_evac[b])])

        for oc in range(8):
            pr = ring[oc % 2]
            for rc2 in range(2):
                wl = []
                if rc2 == 0:
                    wl = [(s_dv, q2_evac[b])]
                    if oc >= 2:
                        wl.append((s_ac, ot_ac[(b, oc - 2)]))
                p.emit("tensor", lambda E, o=pr[:, 0:TB],
                       l=v2[:, rc2 * D + oc * 128:rc2 * D + (oc + 1) * 128],
                       r=qtb[:, rc2 * TB:(rc2 + 1) * TB],
                       st=(rc2 == 0), sp=(rc2 == 1):
                       E.matmul(o, l, r, start=st, stop=sp), wait=wl, inc=s_mm)
            gv = p.cnt(s_mm)
            p.emit("vector", lambda E, d=otf[:, oc * TB:(oc + 1) * TB],
                   s=pr[:, 0:TB], bc=b2c[:, oc:oc + 1]:
                   E.tensor_scalar_add(d, s, bc),
                   wait=[(s_mm, gv)], inc=s_dv)
            fv = p.fence("vector", s_dv)
            p.emit("scalar", lambda E: E.nop(), wait=[(s_dv, fv)], inc=s_ac)
            ot_ac[(b, oc)] = p.cnt(s_ac)
        ot_ac_last[b] = p.cnt(s_ac)
        if PARANOID:
            p.barrier()

        for t in range(TPB):
            for half in range(2):
                srcs = [otf[:, (half * 4 + cc) * TB + t * 128:
                        (half * 4 + cc) * TB + (t + 1) * 128] for cc in range(4)]
                transpose_group(p, com, tring, srcs,
                                ft[:, t * D + half * 512:t * D + (half + 1) * 512],
                                s_mm, s_dv,
                                dwait=[(s_ac, ot_ac[(b, half * 4 + 3)])],
                                fence=(t == TPB - 1 and half == 1))
        tr_done = p.cnt(s_dv)
        if b == 0:
            dbg_dump(p, "otf", otf[:], [128, 8 * TB], F32,
                     [(s_ac, ot_ac_last[b])])
            dbg_dump(p, "b2c", b2c[:], [128, 8], F32, [(s_ac, ot_ac_last[b])])
            dbg_dump(p, "v2w", v2[:], [128, 2 * D], BF16, [(s_ac, ot_ac_last[b])])
            dbg_dump(p, "qtb2", qtb[:], [128, 2 * TB], BF16, [(s_ac, ot_ac_last[b])])
            dbg_dump(p, "b1c", b1c[:], [128, 32], F32, [(s_ac, ot_ac_last[b])])
            dbg_dump(p, "ft", ft[:], [128, TPB * D], F32, [(s_dv, tr_done)])

        for t in range(TPB):
            gi = b * TPB + t
            ftt = ft[:, t * D:(t + 1) * D]
            p.emit("vector", lambda E, x=ftt, c=L.fstats_s[:, gi:gi + 1]:
                   E.tensor_reduce(c, x, AX.X, ALU.add),
                   wait=[(s_dv, tr_done)], inc=s_dv)
            p.emit("scalar", lambda E, x=ftt, c=L.fstats_q[:, gi:gi + 1]:
                   E.activation(L.sqd[:], x, AF.Square, accum_out=c),
                   wait=[(s_dv, tr_done), (s_ac, p.cnt(s_ac))], inc=s_ac)
        p.fence("vector", s_dv)
        p.fence("scalar", s_ac)
        ffv = ln_finalize(p, L.fstats_s, L.fstats_q, L.fstats_m, L.fstats_r,
                          b * TPB, (b + 1) * TPB,
                          [(s_dv, p.cnt(s_dv)), (s_ac, p.cnt(s_ac))],
                          s_dv, s_ac)
        for t in range(TPB):
            gi = b * TPB + t
            wl = [(s_dv, ffv)]
            if t == 0 and b >= 2:
                wl.append((s_st[bb], store_done[b - 2]))
            p.emit("vector", lambda E, d=L.xo[bb][:, t * D:(t + 1) * D],
                   f=ft[:, t * D:(t + 1) * D],
                   m=L.fstats_m[:, gi:gi + 1], r=L.fstats_r[:, gi:gi + 1]:
                   E.tensor_scalar(d, f, m, r, ALU.subtract, ALU.mult),
                   wait=wl, inc=s_dv)
            p.emit("vector", lambda E, d=L.xo[bb][:, t * D:(t + 1) * D],
                   x=L.xg[bb][:, t * D:(t + 1) * D]:
                   E.tensor_add(d, d, x), wait=[(s_dv, p.cnt(s_dv))], inc=s_dv)
        bapply_done[b] = p.fence("vector", s_dv)

        def mk_store(bv=bapply_done[b], b=b, bb=bb):
            settle = p.cnt(s_dv)
            for t in range(TPB):
                gi = b * TPB + t
                p.emit("sync", lambda E, d=y_ap[gi * 128:(gi + 1) * 128, :],
                       s=L.xo[bb][:, t * D:(t + 1) * D]:
                       E.dma_start(d, s),
                       wait=[(s_dv, max(bv, settle))] if t == 0 else (),
                       inc=s_st[bb], inck=16)
            store_done[b] = p.cnt(s_st[bb])
        pending_store[0] = mk_store
        store_done[b] = p.cnt(s_st[bb]) + 16 * TPB

    flush_store(p)
    p.emit("sync", lambda E: E.nop(), wait=[(s_st[0], p.cnt(s_st[0]))])
    p.emit("sync", lambda E: E.nop(), wait=[(s_st[1], p.cnt(s_st[1]))])
    p.barrier()


TWIST_SRC = [4, 5, 6, 7, 0, 1, 2, 3]
TWIST_SGN = [1.0, 1.0, -1.0, -1.0, -1.0, -1.0, 1.0, 1.0]


def build_l1_ffn(p, com, x_ap, y_ap, w, pref="g1"):
    s_ld = [p.sem(pref + "_ld0"), p.sem(pref + "_ld1")]
    s_st = [p.sem(pref + "_st0"), p.sem(pref + "_st1")]
    s_dv = p.sem(pref + "_dv")
    s_ac = p.sem(pref + "_ac")
    s_mm = p.sem(pref + "_mm")
    s_wld = p.sem(pref + "_wld")
    s_wl2 = [p.sem(pref + "_wl0"), p.sem(pref + "_wl1")]
    s_wcv = p.sem(pref + "_wcv")

    tring = TRing(p, pref)
    ring = [p.ps(pref + "_pa", [128, 512], F32),
            p.ps(pref + "_pb", [128, 512], F32)]
    p_tf = p.ps(pref + "_ptf", [128, 512], F32)
    p_cp = p.ps(pref + "_pcp", [128, 512], F32)

    OB1, OB2, OB3, OG1, OG2 = 0, 1024, 1536, 1792, 2304
    OB4, OG3, OAR = 2560, 2561, 2562
    brow = p.sb(pref + "_brow", [1, 2564], F32)
    for off, key, ln in ((OB1, "cn_b1", D), (OB2, "cn_b2", 512),
                         (OB3, "cn_b3", 256), (OG1, "gc_b1", 512),
                         (OG2, "gc_b2", 256), (OB4, "cn_b4", 1),
                         (OG3, "gc_b3", 1), (OAR, "ar", 1)):
        p.emit("sync", lambda E, d=brow[0:1, off:off + ln], s=w[key]:
               E.dma_start(d, s), inc=s_wld, inck=16)
    p.dma_barrier(s_wld)
    p.barrier()

    wstack = ExitStack()
    stg = wstack.enter_context(p.nc.sbuf_tensor(pref + "_stg", [128, 2048],
                                                F32, side="right"))
    w4sb = p.sb(pref + "_w4", [128, 2], BF16)
    g3sb = p.sb(pref + "_g3", [128, 2], BF16)
    for j, (key, col) in enumerate((("cn_w4", 0), ("cn_w4", 1),
                                    ("gc_w3", 2), ("gc_w3", 3))):
        ch = j % 2
        p.emit("sync", lambda E, d=stg[:, j:j + 1],
               sr=w[key][ch * 128:(ch + 1) * 128, :]:
               E.dma_start(d, sr), inc=s_wld, inck=16)
    p.dma_barrier(s_wld)
    p.barrier()
    p.emit("vector", lambda E: E.tensor_copy(w4sb[:, 0:2], stg[:, 0:2]),
           inc=s_wcv)
    p.emit("vector", lambda E: E.tensor_copy(g3sb[:, 0:2], stg[:, 2:4]),
           inc=s_wcv)
    p.fence("vector", s_wcv)
    w1sb = load_weight_bf16(p, pref + "_w1", w["cn_w1"], D, D, s_wl2, s_wcv, stg)
    w2sb = load_weight_bf16(p, pref + "_w2", w["cn_w2"], D, 512, s_wl2, s_wcv, stg)
    w3sb = load_weight_bf16(p, pref + "_w3", w["cn_w3"], 512, 256, s_wl2, s_wcv, stg)
    g1sb = load_weight_bf16(p, pref + "_g1", w["gc_w1"], D, 512, s_wl2, s_wcv, stg)
    g2sb = load_weight_bf16(p, pref + "_g2", w["gc_w2"], 512, 256, s_wl2, s_wcv, stg)
    p.barrier()
    b1c = row_to_cols(p, com, pref + "_b1c",
                      lambda c: brow[0:1, OB1 + c * 128:OB1 + (c + 1) * 128], 8,
                      tring.banks[0][:, 0:8], s_mm, s_dv)
    p.barrier()
    b2c = row_to_cols(p, com, pref + "_b2c",
                      lambda c: brow[0:1, OB2 + c * 128:OB2 + (c + 1) * 128], 4,
                      tring.banks[0][:, 0:4], s_mm, s_dv)
    p.barrier()
    b3c = row_to_cols(p, com, pref + "_b3c",
                      lambda c: brow[0:1, OB3 + c * 128:OB3 + (c + 1) * 128], 2,
                      tring.banks[0][:, 0:2], s_mm, s_dv)
    p.barrier()
    wstack.close()
    b4ap = brow[0:1, OB4:OB4 + 1]
    g3ap = brow[0:1, OG3:OG3 + 1]

    ones128b = p.sb(pref + "_ones128b", [1, 128], BF16)
    scr = p.sb(pref + "_scr", [1, 8], F32)
    p.emit("gpsimd", lambda E: E.memset(ones128b[:], 1.0))
    p.emit("vector", lambda E: E.tensor_scalar_mul(scr[0:1, 0:1],
                                                   brow[0:1, OAR:OAR + 1], 2.0))
    p.emit("vector", lambda E: E.drain())
    p.emit("vector", lambda E: E.scalar_tensor_tensor(
        scr[0:1, 1:2], brow[0:1, OAR:OAR + 1], -1.0, com.ones11[:],
        ALU.mult, ALU.bypass))
    p.emit("vector", lambda E: E.drain())
    p.emit("vector", lambda E: E.tensor_scalar_add(scr[0:1, 1:2],
                                                   scr[0:1, 1:2], 0.1))
    p.barrier()

    L = LNBufs(p, pref)
    xpre = [L.xg[0][:, 0:D], L.xg[0][:, D:2 * D]]
    outT = [p.sb(pref + f"_outT{i}", [128, 8 * TB], F32) for i in range(2)]
    outB = [p.sb(pref + f"_outB{i}", [128, 8 * TB], BF16) for i in range(2)]
    h1b = p.sb(pref + "_h1b", [128, 8 * TB], BF16)
    h2b = p.sb(pref + "_h2b", [128, 4 * TB], BF16)
    h3b = p.sb(pref + "_h3b", [128, 2 * TB], BF16)
    trow = p.sb(pref + "_trow", [1, 2 * TB], F32)
    crow_b = p.sb(pref + "_crowb", [1, 2 * TB], BF16)
    crow_f = p.sb(pref + "_crowf", [1, TB], F32)
    tw_scr = p.sb(pref + "_twscr", [128, TB], F32)
    gcols = p.sb(pref + "_gcols", [128, 14], F32)
    gmean = L.xo[0][0:1, 0:D + 8]
    grow = L.xo[1][0:1, 0:512 + 256 + 4]
    ft = L.xn_tok

    red_dv = {}
    sq_ac = {}
    for gi in range(NT):
        rb = gi % 2
        wl = []
        if gi >= 2:
            wl = [(s_dv, red_dv[gi - 2]), (s_ac, sq_ac[gi - 2])]
        p.emit("sync", lambda E, d=xpre[rb][:], s=x_ap[gi * 128:(gi + 1) * 128, :]:
               E.dma_start(d, s), wait=wl, inc=s_ld[rb], inck=16)
        ldw = [(s_ld[rb], p.cnt(s_ld[rb]))]
        p.emit("vector", lambda E, x=xpre[rb][:], c=L.stats_s[:, gi:gi + 1]:
               E.tensor_reduce(c, x, AX.X, ALU.add), wait=ldw, inc=s_dv)
        red_dv[gi] = p.cnt(s_dv)
        p.emit("scalar", lambda E, x=xpre[rb][:], c=L.stats_q[:, gi:gi + 1]:
               E.activation(L.sqd[:], x, AF.Square, accum_out=c),
               wait=ldw + [(s_ac, p.cnt(s_ac))], inc=s_ac)
        sq_ac[gi] = p.cnt(s_ac)
    p.fence("vector", s_dv)
    p.fence("scalar", s_ac)
    finv_pre = ln_finalize(p, L.stats_s, L.stats_q, L.stats_m, L.stats_r,
                           0, NT, [(s_dv, p.cnt(s_dv)), (s_ac, p.cnt(s_ac))],
                           s_dv, s_ac)
    p.barrier()
    dbg_dump(p, "l1_sts", L.stats_s[:], [128, NT], F32, [])
    dbg_dump(p, "l1_stq", L.stats_q[:], [128, NT], F32, [])
    dbg_dump(p, "l1_str", L.stats_r[:], [128, NT], F32, [])
    p.barrier()
    for gi in range(NT):
        rb = gi % 2
        wl = [(s_mm, p.cnt(s_mm))] if gi >= 2 else []
        p.emit("sync", lambda E, d=xpre[rb][:], s=x_ap[gi * 128:(gi + 1) * 128, :]:
               E.dma_start(d, s), wait=wl, inc=s_ld[rb], inck=16)
        for hh in range(2):
            p.emit("tensor", lambda E, o=ring[hh][0:1, 0:512],
                   l=L.stats_r[:, gi:gi + 1], r=xpre[rb][:, hh * 512:(hh + 1) * 512],
                   st=(gi == 0), sp=(gi == NT - 1):
                   E.matmul(o, l, r, start=st, stop=sp),
                   wait=[(s_ld[rb], p.cnt(s_ld[rb]))] if hh == 0 else (), inc=s_mm)
    p.emit("vector", lambda E: E.tensor_mul(gcols[:, 8:9], L.stats_m[:, 0:NT],
                                            L.stats_r[:, 0:NT])
           if False else E.tensor_mul(L.fstats_s[:, 0:NT], L.stats_m[:, 0:NT],
                                      L.stats_r[:, 0:NT]), inc=s_dv)
    p.emit("vector", lambda E: E.drain())
    p.emit("vector", lambda E: E.tensor_reduce(gcols[:, 8:9],
                                               L.fstats_s[:, 0:NT], AX.X, ALU.add),
           inc=s_dv)
    mrv = p.fence("vector", s_dv)
    p.emit("tensor", lambda E: E.matmul(p_tf[0:1, 0:1], com.ones_col[:],
                                        gcols[:, 8:9], start=True, stop=True),
           wait=[(s_dv, mrv)], inc=s_mm)
    gv = p.cnt(s_mm)
    p.emit("vector", lambda E: E.tensor_copy(gmean[0:1, D:D + 1],
                                             p_tf[0:1, 0:1]),
           wait=[(s_mm, gv)], inc=s_dv)
    p.emit("vector", lambda E: E.drain())
    gv = p.cnt(s_mm)
    for hh in range(2):
        p.emit("vector", lambda E, d=gmean[0:1, hh * 512:(hh + 1) * 512],
               sp=ring[hh][0:1, 0:512]:
               E.tensor_scalar(d, sp, gmean[0:1, D:D + 1], 1.0 / S,
                               ALU.subtract, ALU.mult),
               wait=[(s_mm, gv)], inc=s_dv)
    p.fence("vector", s_dv)
    p.barrier()

    gc8 = row_to_cols(p, com, pref + "_gc8",
                      lambda c: gmean[0:1, c * 128:(c + 1) * 128], 8,
                      tring.banks[0][:, 0:8], s_mm, s_dv, dt=BF16)
    p.barrier()
    for k in range(8):
        p.emit("tensor", lambda E, o=p_tf[0:1, 0:512],
               l=gc8[:, k:k + 1], r=g1sb[:, k * 512:(k + 1) * 512],
               st=(k == 0), sp=(k == 7):
               E.matmul(o, l, r, start=st, stop=sp), inc=s_mm)
    gv = p.cnt(s_mm)
    p.emit("vector", lambda E: E.tensor_add(grow[0:1, 0:512], p_tf[0:1, 0:512],
                                            brow[0:1, OG1:OG1 + 512]),
           wait=[(s_mm, gv)], inc=s_dv)
    fv = p.fence("vector", s_dv)
    p.emit("scalar", lambda E: E.activation(grow[0:1, 0:512], grow[0:1, 0:512],
                                            AF.Gelu), wait=[(s_dv, fv)], inc=s_ac)
    p.fence("scalar", s_ac)
    p.barrier()
    gc4 = row_to_cols(p, com, pref + "_gc4",
                      lambda c: grow[0:1, c * 128:(c + 1) * 128], 4,
                      tring.banks[0][:, 0:4], s_mm, s_dv, dt=BF16)
    p.barrier()
    for k in range(4):
        p.emit("tensor", lambda E, o=p_tf[0:1, 0:256],
               l=gc4[:, k:k + 1], r=g2sb[:, k * 256:(k + 1) * 256],
               st=(k == 0), sp=(k == 3):
               E.matmul(o, l, r, start=st, stop=sp), inc=s_mm)
    gv = p.cnt(s_mm)
    p.emit("vector", lambda E: E.tensor_add(grow[0:1, 512:768], p_tf[0:1, 0:256],
                                            brow[0:1, OG2:OG2 + 256]),
           wait=[(s_mm, gv)], inc=s_dv)
    fv = p.fence("vector", s_dv)
    p.emit("scalar", lambda E: E.activation(grow[0:1, 512:768], grow[0:1, 512:768],
                                            AF.Gelu), wait=[(s_dv, fv)], inc=s_ac)
    p.fence("scalar", s_ac)
    p.barrier()
    gc2 = row_to_cols(p, com, pref + "_gc2",
                      lambda c: grow[0:1, 512 + c * 128:512 + (c + 1) * 128], 2,
                      tring.banks[0][:, 0:2], s_mm, s_dv, dt=BF16)
    p.barrier()
    for k in range(2):
        p.emit("tensor", lambda E, o=p_tf[0:1, 0:1],
               l=gc2[:, k:k + 1], r=g3sb[:, k:k + 1],
               st=(k == 0), sp=(k == 1):
               E.matmul(o, l, r, start=st, stop=sp), inc=s_mm)
    gv = p.cnt(s_mm)
    p.emit("scalar", lambda E: E.activation(grow[0:1, 768:769], p_tf[0:1, 0:1],
                                            AF.Sigmoid, bias=g3ap),
           wait=[(s_mm, gv)], inc=s_ac)
    fv = p.fence("scalar", s_ac)
    p.emit("vector", lambda E: E.tensor_scalar_mul(scr[0:1, 2:3],
                                                   grow[0:1, 768:769], 0.7),
           wait=[(s_ac, fv)], inc=s_dv)
    p.fence("vector", s_dv)
    p.barrier()
    dbg_dump(p, "l1_gmean", gmean[:], [1, D + 8], F32, [])
    dbg_dump(p, "l1_grow", grow[:], [1, 772], F32, [])
    dbg_dump(p, "l1_w1", w1sb[:], [128, 8 * D], BF16, [])
    dbg_dump(p, "l1_w4", w4sb[:], [128, 2], BF16, [])
    dbg_dump(p, "l1_g3", g3sb[:], [128, 2], BF16, [])
    dbg_dump(p, "l1_b1c", b1c[:], [128, 8], F32, [])
    p.barrier()

    pending_store = [None]

    def flush_store():
        if pending_store[0] is not None:
            fn = pending_store[0]
            pending_store[0] = None
            fn()

    bapply_done = {}
    store_done = {}
    h1mm_last = {}
    twist_dv = {}
    xn_tr_mm = {}
    tf_read = [0]
    cp_read = [0]
    ring_free = {0: 0, 1: 0}

    for b in range(NB):
        bb = b % 2
        for t in range(TPB):
            gi = b * TPB + t
            wl = []
            if b >= 2:
                wl.append((s_dv, bapply_done[b - 2]))
            p.emit("sync", lambda E, d=L.xg[bb][:, t * D:(t + 1) * D],
                   s=x_ap[gi * 128:(gi + 1) * 128, :]: E.dma_start(d, s),
                   wait=wl, inc=s_ld[bb], inck=16)
        ldv = p.cnt(s_ld[bb])
        for t in range(TPB):
            gi = b * TPB + t
            wl = [(s_ld[bb], ldv)]
            if t == 0 and b >= 1:
                wl.append((s_mm, xn_tr_mm[b - 1]))
            p.emit("vector", lambda E, d=L.xn_tok[:, t * D:(t + 1) * D],
                   x=L.xg[bb][:, t * D:(t + 1) * D],
                   m=L.stats_m[:, gi:gi + 1], r=L.stats_r[:, gi:gi + 1]:
                   E.tensor_scalar(d, x, m, r, ALU.subtract, ALU.mult),
                   wait=wl, inc=s_dv)
        apply_done = p.fence("vector", s_dv)
        first = True
        for c in range(8):
            i, bank, freev = tring.next()
            for j in range(TPB):
                wlt = []
                if j == 0:
                    wlt = [(s_dv, max(freev, apply_done))]
                    if first and b >= 1:
                        wlt.append((s_mm, h1mm_last[(b - 1, 2)]))
                        wlt.append((s_dv, twist_dv[(b - 1, 2)]))
                p.emit("tensor", lambda E, o=bank[:, j * 128:(j + 1) * 128],
                       i_=L.xn_tok[:, j * D + c * 128:j * D + (c + 1) * 128],
                       idn=com.ident_f[:]: E.transpose(o, i_, idn),
                       wait=wlt, inc=s_mm)
            mmv = p.cnt(s_mm)
            p.emit("vector", lambda E, d=outT[0][:, c * TB:(c + 1) * TB],
                   sp=bank[:, 0:TB]: E.tensor_copy(d, sp),
                   wait=[(s_mm, mmv)], inc=s_dv)
            p.emit("vector", lambda E, d=outB[0][:, c * TB:(c + 1) * TB],
                   sp=bank[:, 0:TB]: E.tensor_copy(d, sp), inc=s_dv)
            tring.free[i] = p.cnt(s_dv)
            first = False
        xn_tr_mm[b] = p.cnt(s_mm)
        y0v = p.fence("vector", s_dv)
        flush_store()

        for cyc in range(3):
            cur = cyc % 2
            nxt = 1 - cur
            curT, curB = outT[cur], outB[cur]
            nxtT, nxtB = outT[nxt], outB[nxt]
            data_ready = y0v if cyc == 0 else twist_dv[(b, cyc - 1)]

            def hchain(wsb, rhs, rhs_ev, kc, oc_n, bias_cols, dst,
                       data_wait):
                evs = {}
                for oc in range(oc_n):
                    i = oc % 2
                    pr = ring[i]
                    for k in range(kc):
                        wl2 = []
                        if oc == 0 and rhs_ev is not None:
                            wl2.append((s_ac, rhs_ev[k]))
                        if k == 0:
                            if oc == 0 and data_wait:
                                wl2.append((s_dv, data_wait))
                            if ring_free[i] > 0:
                                wl2.append((s_ac, ring_free[i]))
                        p.emit("tensor", lambda E, o=pr[:, 0:TB],
                               l=wsb[:, k * (oc_n * 128) + oc * 128:
                                     k * (oc_n * 128) + (oc + 1) * 128],
                               r=rhs[:, k * TB:(k + 1) * TB],
                               st=(k == 0), sp=(k == kc - 1):
                               E.matmul(o, l, r, start=st, stop=sp),
                               wait=wl2, inc=s_mm)
                    gv = p.cnt(s_mm)
                    p.emit("scalar", lambda E, d=dst[:, oc * TB:(oc + 1) * TB],
                           sp=pr[:, 0:TB], bc=bias_cols[:, oc:oc + 1]:
                           E.activation(d, sp, AF.Gelu, bias=bc),
                           wait=[(s_mm, gv)], inc=s_ac)
                    ring_free[i] = p.fence("scalar", s_ac)
                    evs[oc] = ring_free[i]
                return evs

            h1ev = hchain(w1sb, curB, None, 8, 8, b1c, h1b, data_ready)
            h1mm_last[(b, cyc)] = p.cnt(s_mm)
            h2ev = hchain(w2sb, h1b, h1ev, 8, 4, b2c, h2b, 0)
            h3ev = hchain(w3sb, h2b, h2ev, 4, 2, b3c, h3b, 0)
            for k in range(2):
                wl2 = [(s_ac, h3ev[k])] if True else []
                if k == 0 and (b > 0 or cyc > 0):
                    wl2.append((s_ac, tf_read[0]))
                p.emit("tensor", lambda E, o=p_tf[0:1, 0:TB],
                       l=w4sb[:, k:k + 1], r=h3b[:, k * TB:(k + 1) * TB],
                       st=(k == 0), sp=(k == 1):
                       E.matmul(o, l, r, start=st, stop=sp), wait=wl2, inc=s_mm)
            gv = p.cnt(s_mm)
            p.emit("scalar", lambda E: E.activation(trow[0:1, 0:TB],
                                                    p_tf[0:1, 0:TB],
                                                    AF.Sigmoid, bias=b4ap),
                   wait=[(s_mm, gv)], inc=s_ac)
            tf_read[0] = p.fence("scalar", s_ac)
            p.emit("vector", lambda E: E.tensor_scalar(
                trow[0:1, TB:2 * TB], trow[0:1, 0:TB], 0.3, scr[0:1, 2:3],
                ALU.mult, ALU.add), wait=[(s_ac, tf_read[0])], inc=s_dv)
            p.emit("vector", lambda E: E.drain())
            p.emit("vector", lambda E: E.tensor_scalar(
                trow[0:1, 0:TB], trow[0:1, TB:2 * TB],
                scr[0:1, 0:1], scr[0:1, 1:2], ALU.mult, ALU.add), inc=s_dv)
            p.emit("vector", lambda E: E.drain())
            p.emit("vector", lambda E: E.tensor_copy(crow_b[0:1, 0:TB],
                                                     trow[0:1, 0:TB]),
                   inc=s_dv)
            p.emit("vector", lambda E: E.drain())
            p.emit("vector", lambda E: E.tensor_copy(crow_f[0:1, 0:TB],
                                                     crow_b[0:1, 0:TB]), inc=s_dv)
            p.emit("vector", lambda E: E.drain())
            p.emit("vector", lambda E: E.tensor_sub(crow_f[0:1, 0:TB],
                                                    trow[0:1, 0:TB],
                                                    crow_f[0:1, 0:TB]), inc=s_dv)
            p.emit("vector", lambda E: E.drain())
            p.emit("vector", lambda E: E.tensor_copy(crow_b[0:1, TB:2 * TB],
                                                     crow_f[0:1, 0:TB]), inc=s_dv)
            cbv = p.fence("vector", s_dv)
            wl2 = [(s_dv, cbv)]
            if (b, cyc) != (0, 0):
                wl2.append((s_dv, cp_read[0]))
            p.emit("tensor", lambda E: E.matmul(p_cp[:, 0:TB], ones128b[:],
                                                crow_b[0:1, 0:TB],
                                                start=True, stop=False),
                   wait=wl2, inc=s_mm)
            p.emit("tensor", lambda E: E.matmul(p_cp[:, 0:TB], ones128b[:],
                                                crow_b[0:1, TB:2 * TB],
                                                start=False, stop=True), inc=s_mm)
            bcv = p.fence("tensor", s_mm)
            for c8 in range(8):
                sc_, sg = TWIST_SRC[c8], TWIST_SGN[c8]
                p.emit("vector", lambda E, i0=curT[:, sc_ * TB:(sc_ + 1) * TB],
                       sg=sg: E.scalar_tensor_tensor(
                           tw_scr[:], i0, sg, p_cp[:, 0:TB], ALU.mult, ALU.mult),
                       wait=[(s_mm, bcv)] if c8 == 0 else (), inc=s_dv)
                p.emit("vector", lambda E, d=nxtT[:, c8 * TB:(c8 + 1) * TB],
                       i0=curT[:, c8 * TB:(c8 + 1) * TB]:
                       E.tensor_add(d, i0, tw_scr[:]),
                       wait=[(s_dv, p.cnt(s_dv))], inc=s_dv)
                p.emit("vector", lambda E, d=nxtB[:, c8 * TB:(c8 + 1) * TB],
                       sp=nxtT[:, c8 * TB:(c8 + 1) * TB]:
                       E.tensor_copy(d, sp),
                       wait=[(s_dv, p.cnt(s_dv))], inc=s_dv)
            twist_dv[(b, cyc)] = p.fence("vector", s_dv)
            cp_read[0] = twist_dv[(b, cyc)]

        if b == 0:
            dbg_dump(p, "l1_y0", outT[0][:], [128, 8 * TB], F32,
                     [(s_dv, y0v)])
            dbg_dump(p, "l1_y0b", outB[0][:], [128, 8 * TB], BF16,
                     [(s_dv, y0v)])
            dbg_dump(p, "l1_h1", h1b[:], [128, 8 * TB], BF16,
                     [(s_ac, ring_free[1])])
            dbg_dump(p, "l1_trow", trow[:], [1, 2 * TB], F32,
                     [(s_dv, twist_dv[(b, 0)])])
            dbg_dump(p, "l1_o1", outT[1][:], [128, 8 * TB], F32,
                     [(s_dv, twist_dv[(b, 0)])])
        fT = outT[1]
        for t in range(TPB):
            for half in range(2):
                srcs = [fT[:, (half * 4 + cc) * TB + t * 128:
                           (half * 4 + cc) * TB + (t + 1) * 128]
                        for cc in range(4)]
                transpose_group(p, com, tring, srcs,
                                ft[:, t * D + half * 512:t * D + (half + 1) * 512],
                                s_mm, s_dv,
                                dwait=[(s_dv, twist_dv[(b, 2)])],
                                fence=(t == TPB - 1 and half == 1))
        tr_done = p.cnt(s_dv)

        for t in range(TPB):
            gi = b * TPB + t
            ftt = ft[:, t * D:(t + 1) * D]
            p.emit("vector", lambda E, x=ftt, c=L.fstats_q[:, gi:gi + 1]
                   if False else L.fstats_s[:, gi:gi + 1]:
                   E.tensor_reduce(c, x, AX.X, ALU.add),
                   wait=[(s_dv, tr_done)], inc=s_dv)
            p.emit("scalar", lambda E, x=ftt, c=L.fstats_q[:, gi:gi + 1]:
                   E.activation(L.sqd[:], x, AF.Square, accum_out=c),
                   wait=[(s_dv, tr_done), (s_ac, p.cnt(s_ac))], inc=s_ac)
        p.fence("vector", s_dv)
        p.fence("scalar", s_ac)
        ffv = ln_finalize(p, L.fstats_s, L.fstats_q, L.fstats_m, L.fstats_r,
                          b * TPB, (b + 1) * TPB,
                          [(s_dv, p.cnt(s_dv)), (s_ac, p.cnt(s_ac))],
                          s_dv, s_ac)
        for t in range(TPB):
            gi = b * TPB + t
            wl = [(s_dv, ffv)]
            if t == 0 and b >= 2:
                wl.append((s_st[bb], store_done[b - 2]))
            p.emit("vector", lambda E, d=L.xo[bb][:, t * D:(t + 1) * D],
                   f=ft[:, t * D:(t + 1) * D],
                   m=L.fstats_m[:, gi:gi + 1], r=L.fstats_r[:, gi:gi + 1]:
                   E.tensor_scalar(d, f, m, r, ALU.subtract, ALU.mult),
                   wait=wl, inc=s_dv)
            p.emit("vector", lambda E, d=L.xo[bb][:, t * D:(t + 1) * D],
                   x=L.xg[bb][:, t * D:(t + 1) * D]:
                   E.tensor_add(d, d, x), wait=[(s_dv, p.cnt(s_dv))], inc=s_dv)
        bapply_done[b] = p.fence("vector", s_dv)

        def mk_store(bv=bapply_done[b], b=b, bb=bb):
            settle = p.cnt(s_dv)
            for t in range(TPB):
                gi = b * TPB + t
                p.emit("sync", lambda E, d=y_ap[gi * 128:(gi + 1) * 128, :],
                       s=L.xo[bb][:, t * D:(t + 1) * D]:
                       E.dma_start(d, s),
                       wait=[(s_dv, max(bv, settle))] if t == 0 else (),
                       inc=s_st[bb], inck=16)
            store_done[b] = p.cnt(s_st[bb])
        pending_store[0] = mk_store
        store_done[b] = p.cnt(s_st[bb]) + 16 * TPB

    flush_store()
    p.emit("sync", lambda E: E.nop(), wait=[(s_st[0], p.cnt(s_st[0]))])
    p.emit("sync", lambda E: E.nop(), wait=[(s_st[1], p.cnt(s_st[1]))])
    p.barrier()


def build_l1_ffn_program():
    nc = bass.Bass("TRN2", target_bir_lowering=False)
    x = nc.dram_tensor("x", [S, D], F32, kind="ExternalInput").ap()
    y = nc.dram_tensor("y", [S, D], F32, kind="ExternalOutput").ap()
    shapes = {"cn_w1": [D, D], "cn_w2": [D, 512], "cn_w3": [512, 256],
              "cn_w4": [256, 1], "gc_w1": [D, 512], "gc_w2": [512, 256],
              "gc_w3": [256, 1], "cn_b1": [1, D], "cn_b2": [1, 512],
              "cn_b3": [1, 256], "cn_b4": [1, 1], "gc_b1": [1, 512],
              "gc_b2": [1, 256], "gc_b3": [1, 1], "ar": [1, 1]}
    w = {k: nc.dram_tensor(k, v, F32, kind="ExternalInput").ap()
         for k, v in shapes.items()}
    capz = _declare_const_inputs(nc, ["ident"])
    with ExitStack() as stack:
        p = P(nc, stack)
        com = Common(p, capz["ident"])
        with p.scope():
            build_l1_ffn(p, com, x, y, w)
        p.flush()
    return nc


def run_l1_ffn(x_full, w_np):
    nc = build_l1_ffn_program()
    in_maps = []
    for c in range(NCORES):
        m = {"x": np.ascontiguousarray(x_full[c])}
        for k, v in w_np.items():
            m[k] = np.ascontiguousarray(v.reshape(1, -1) if v.ndim == 1 else v)
        m.update(_const_in_map(["ident"]))
        in_maps.append(m)
    res = run_bass_kernel_spmd(nc, in_maps, core_ids=list(range(NCORES)))
    return np.stack([r["y"] for r in res.results])


def build_l1_attn(p, com, x_ap, y_ap, w, pref="a1"):
    s_ld = [p.sem(pref + "_ld0"), p.sem(pref + "_ld1")]
    s_st = [p.sem(pref + "_st0"), p.sem(pref + "_st1")]
    s_dv = p.sem(pref + "_dv")
    s_ac = p.sem(pref + "_ac")
    s_mm = p.sem(pref + "_mm")
    s_wld = p.sem(pref + "_wld")
    s_wl2 = [p.sem(pref + "_wl0"), p.sem(pref + "_wl1")]
    s_wcv = p.sem(pref + "_wcv")
    EPS = 1e-8

    tring = TRing(p, pref)
    ring = [p.ps(pref + "_pa", [128, 512], F32),
            p.ps(pref + "_pb", [128, 512], F32)]

    def dvc(fn, wait=(), inc=s_dv):
        p.emit("vector", fn, wait=list(wait) + [(s_dv, p.cnt(s_dv))], inc=inc)

    def acc(fn, wait=()):
        p.emit("scalar", fn, wait=list(wait) + [(s_ac, p.cnt(s_ac))], inc=s_ac)

    wstack = ExitStack()
    brow = wstack.enter_context(p.nc.sbuf_tensor(pref + "_brow", [1, 3 * D],
                                                 F32, side="right"))
    for j, key in enumerate(("bq", "bk", "bv")):
        p.emit("sync", lambda E, d=brow[0:1, j * D:(j + 1) * D], sr=w[key]:
               E.dma_start(d, sr), inc=s_wld, inck=16)
    p.dma_barrier(s_wld)
    p.barrier()

    AB = {}
    dcols = {}
    stg = wstack.enter_context(p.nc.sbuf_tensor(pref + "_stg", [128, 2 * D],
                                                F32, side="right"))
    dfb = wstack.enter_context(p.nc.sbuf_tensor(pref + "_dfb", [128, 8 * D],
                                                BF16, side="right"))
    wb = wstack.enter_context(p.nc.sbuf_tensor(pref + "_wb", [128, 8 * D],
                                               BF16, side="right"))
    base_cv = p.cnt(s_wcv)
    for k in range(8):
        bpar = k % 2
        wait = [(s_wcv, base_cv + k - 1)] if k >= 2 else [(s_wcv, base_cv)]
        p.emit("sync", lambda E, d=stg[:, bpar * D:(bpar + 1) * D],
               sr=w["dft_f"][k * 128:(k + 1) * 128, :]: E.dma_start(d, sr),
               wait=wait, inc=s_wl2[bpar], inck=16)
        p.emit("vector", lambda E, d=dfb[:, k * D:(k + 1) * D],
               sr=stg[:, bpar * D:(bpar + 1) * D]: E.tensor_copy(d, sr),
               wait=[(s_wl2[bpar], p.cnt(s_wl2[bpar]))], inc=s_wcv)
    dinv = load_weight_bf16(p, pref + "_dinv", w["dft_i"], D, D, s_wl2,
                            s_wcv, stg)
    p.barrier()
    bqc = row_to_cols(p, com, pref + "_bqc",
                      lambda c: brow[0:1, c * 128:(c + 1) * 128], 8,
                      tring.banks[0][:, 0:8], s_mm, s_dv, dt=BF16)
    p.barrier()
    bkc = row_to_cols(p, com, pref + "_bkc",
                      lambda c: brow[0:1, D + c * 128:D + (c + 1) * 128], 8,
                      tring.banks[0][:, 0:8], s_mm, s_dv, dt=BF16)
    p.barrier()
    bvc = row_to_cols(p, com, pref + "_bvc",
                      lambda c: brow[0:1, 2 * D + c * 128:2 * D + (c + 1) * 128],
                      8, tring.banks[0][:, 0:8], s_mm, s_dv, dt=BF16)
    p.barrier()
    for nm, bc in (("q", bqc), ("k", bkc), ("v", bvc)):
        base_cv = p.cnt(s_wcv)
        for k in range(8):
            bpar = k % 2
            wait = [(s_wcv, base_cv + k - 1)] if k >= 2 else [(s_wcv, base_cv)]
            p.emit("sync", lambda E, d=stg[:, bpar * D:(bpar + 1) * D],
                   sr=w["W" + nm][k * 128:(k + 1) * 128, :]: E.dma_start(d, sr),
                   wait=wait, inc=s_wl2[bpar], inck=16)
            p.emit("vector", lambda E, d=wb[:, k * D:(k + 1) * D],
                   sr=stg[:, bpar * D:(bpar + 1) * D]: E.tensor_copy(d, sr),
                   wait=[(s_wl2[bpar], p.cnt(s_wl2[bpar]))], inc=s_wcv)
        p.fence("vector", s_wcv)
        p.barrier()
        ab = p.sb(pref + "_A" + nm, [128, 8 * D], BF16)
        dc = p.sb(pref + "_d" + nm, [128, 8], F32)
        for frc in range(8):
            for half in range(2):
                pr = ring[half]
                for ec in range(8):
                    p.emit("tensor", lambda E, o=pr[:, 0:512],
                           l=wb[:, ec * D + frc * 128:ec * D + (frc + 1) * 128],
                           r=dfb[:, ec * D + half * 512:ec * D + (half + 1) * 512],
                           st=(ec == 0), sp=(ec == 7):
                           E.matmul(o, l, r, start=st, stop=sp),
                           wait=[(s_dv, p.cnt(s_dv))] if ec == 0 else (),
                           inc=s_mm)
                gv = p.cnt(s_mm)
                p.emit("vector", lambda E,
                       d=ab[:, frc * D + half * 512:frc * D + (half + 1) * 512],
                       sp=pr[:, 0:512]: E.tensor_copy(d, sp),
                       wait=[(s_mm, gv)], inc=s_dv)
            for ec in range(8):
                p.emit("tensor", lambda E, o=tring.banks[0][:, 0:1],
                       l=dfb[:, ec * D + frc * 128:ec * D + (frc + 1) * 128],
                       r=bc[:, ec:ec + 1], st=(ec == 0), sp=(ec == 7):
                       E.matmul(o, l, r, start=st, stop=sp), inc=s_mm)
            gv = p.cnt(s_mm)
            p.emit("vector", lambda E, d=dc[:, frc:frc + 1],
                   sp=tring.banks[0][:, 0:1]: E.tensor_copy(d, sp),
                   wait=[(s_mm, gv)], inc=s_dv)
        p.fence("vector", s_dv)
        p.barrier()
        AB[nm] = ab
        dcols[nm] = dc
    wstack.close()

    L = LNBufs(p, pref, sqd=False)
    xnT = p.sb(pref + "_xnT", [128, 8 * TB], BF16)
    bufA = p.sb(pref + "_bufA", [128, 8 * TB], F32)
    bufB = p.sb(pref + "_bufB", [128, 8 * TB], F32)
    nsc1 = p.sb(pref + "_n1", [128, 4 * TB], F32)
    nsc2 = p.sb(pref + "_n2", [128, 4 * TB], F32)
    zb = xnT
    ccar = p.sb(pref + "_ccar", [128, 8], F32)
    ft = L.xn_tok
    otf = bufA
    HT_ = 4 * TB

    p.emit("gpsimd", lambda E: E.memset(ccar[:], 0.0))
    p.barrier()

    def project(dst, ab, dc, evw):
        for frc in range(8):
            pr = ring[frc % 2]
            for ec in range(8):
                wl = []
                if ec == 0:
                    wl = [(s_dv, evxnT)]
                    if frc < 2:
                        wl += list(evw)
                        wl.append((s_dv, proj_ev.get(6 + frc, 0)))
                    else:
                        wl.append((s_dv, proj_ev[frc - 2]))
                p.emit("tensor", lambda E, o=pr[:, 0:TB],
                       l=ab[:, ec * D + frc * 128:ec * D + (frc + 1) * 128],
                       r=xnT[:, ec * TB:(ec + 1) * TB],
                       st=(ec == 0), sp=(ec == 7):
                       E.matmul(o, l, r, start=st, stop=sp), wait=wl, inc=s_mm)
            gv = p.cnt(s_mm)
            p.emit("vector", lambda E, d=dst[:, frc * TB:(frc + 1) * TB],
                   sp=pr[:, 0:TB], bcl=dc[:, frc:frc + 1]:
                   E.tensor_scalar_add(d, sp, bcl),
                   wait=[(s_mm, gv), (s_dv, p.cnt(s_dv))], inc=s_dv)
            proj_ev[frc] = p.cnt(s_dv)
        return p.fence("vector", s_dv)

    def normalize(buf):
        top = buf[:, 0:HT_]
        bot = buf[:, HT_:2 * HT_]
        acc(lambda E: E.activation(nsc1[:], top, AF.Square),
            wait=[(s_dv, p.cnt(s_dv))])
        acc(lambda E: E.activation(nsc2[:], bot, AF.Square))
        af = p.fence("scalar", s_ac)
        dvc(lambda E: E.tensor_add(nsc1[:], nsc1[:], nsc2[:]), wait=[(s_ac, af)])
        df_ = p.fence("vector", s_dv)
        acc(lambda E: E.activation(nsc1[:], nsc1[:], AF.Sqrt), wait=[(s_dv, df_)])
        acc(lambda E: E.activation(nsc1[:], nsc1[:], AF.Ln, bias=epsc[:]))
        acc(lambda E: E.activation(nsc1[:], nsc1[:], AF.Exp, scale=-1.0))
        af = p.fence("scalar", s_ac)
        for j, off in ((0, 0), (1, HT_)):
            acc(lambda E, o=off, j=j: E.activation(
                nsc2[0:1, j * TB:(j + 1) * TB], buf[0:1, o:o + TB], AF.Abs),
                wait=[(s_dv, p.cnt(s_dv))] if j == 0 else ())
            acc(lambda E, j=j: E.activation(
                nsc2[0:1, j * TB:(j + 1) * TB], nsc2[0:1, j * TB:(j + 1) * TB],
                AF.Ln, bias=epsc[0:1, :]))
            acc(lambda E, j=j: E.activation(
                nsc2[0:1, j * TB:(j + 1) * TB], nsc2[0:1, j * TB:(j + 1) * TB],
                AF.Exp, scale=-1.0))
        af2 = p.fence("scalar", s_ac)
        dvc(lambda E: E.tensor_mul(nsc2[0:1, 2 * TB:3 * TB], buf[0:1, 0:TB],
                                   nsc2[0:1, 0:TB]), wait=[(s_ac, af2)])
        dvc(lambda E: E.tensor_mul(nsc2[0:1, 3 * TB:4 * TB],
                                   buf[0:1, HT_:HT_ + TB], nsc2[0:1, TB:2 * TB]))
        dvc(lambda E: E.tensor_mul(top, top, nsc1[:]))
        dvc(lambda E: E.tensor_mul(bot, bot, nsc1[:]))
        dvc(lambda E: E.tensor_copy(buf[0:1, 0:TB], nsc2[0:1, 2 * TB:3 * TB]))
        dvc(lambda E: E.tensor_copy(buf[0:1, HT_:HT_ + TB],
                                    nsc2[0:1, 3 * TB:4 * TB]))
        return p.fence("vector", s_dv)

    epsc = p.sb(pref + "_epsc", [128, 1], F32)
    p.emit("gpsimd", lambda E: E.memset(epsc[:], EPS))
    p.barrier()

    pending_store = [None]

    def flush_store():
        if pending_store[0] is not None:
            fn = pending_store[0]
            pending_store[0] = None
            fn()

    bapply_done = {}
    store_done = {}
    xn_tr_mm = {}
    proj_ev = {}
    invread = [0]
    prev_bufB_free = [0]
    xn_consumed = 0
    rstash = p.sb(pref + "_rstash", [1, TB], F32)

    for b in range(NB):
        bb = b % 2
        for t in range(TPB):
            gi = b * TPB + t
            wl = []
            if b >= 2:
                wl.append((s_dv, bapply_done[b - 2]))
            p.emit("sync", lambda E, d=L.xg[bb][:, t * D:(t + 1) * D],
                   sr=x_ap[gi * 128:(gi + 1) * 128, :]: E.dma_start(d, sr),
                   wait=wl, inc=s_ld[bb], inck=16)
        ldv = p.cnt(s_ld[bb])
        for t in range(TPB):
            gi = b * TPB + t
            xt = L.xg[bb][:, t * D:(t + 1) * D]
            p.emit("vector", lambda E, x=xt, c=L.stats_s[:, gi:gi + 1]:
                   E.tensor_reduce(c, x, AX.X, ALU.add),
                   wait=[(s_ld[bb], ldv)], inc=s_dv)
            p.emit("scalar", lambda E, x=xt, c=L.stats_q[:, gi:gi + 1]:
                   E.activation(zb[:, 0:D], x, AF.Square, accum_out=c),
                   wait=[(s_ld[bb], ldv), (s_ac, p.cnt(s_ac))], inc=s_ac)
        p.fence("vector", s_dv)
        p.fence("scalar", s_ac)
        finv = ln_finalize(p, L.stats_s, L.stats_q, L.stats_m, L.stats_r,
                           b * TPB, (b + 1) * TPB,
                           [(s_dv, p.cnt(s_dv)), (s_ac, p.cnt(s_ac))],
                           s_dv, s_ac)
        for t in range(TPB):
            gi = b * TPB + t
            wl = [(s_dv, finv)]
            if t == 0 and b >= 1:
                wl.append((s_mm, xn_tr_mm[b - 1]))
            p.emit("vector", lambda E, d=L.xn_tok[:, t * D:(t + 1) * D],
                   x=L.xg[bb][:, t * D:(t + 1) * D],
                   m=L.stats_m[:, gi:gi + 1], r=L.stats_r[:, gi:gi + 1]:
                   E.tensor_scalar(d, x, m, r, ALU.subtract, ALU.mult),
                   wait=wl, inc=s_dv)
        apply_done = p.fence("vector", s_dv)
        first = True
        for c in range(8):
            srcs = [L.xn_tok[:, t * D + c * 128:t * D + (c + 1) * 128]
                    for t in range(TPB)]
            dw = [(s_dv, apply_done)] if first else ()
            if first and b >= 1:
                dw = list(dw) + [(s_mm, xn_consumed)]
            transpose_group(p, com, tring, srcs,
                            xnT[:, c * TB:(c + 1) * TB], s_mm, s_dv,
                            dwait=dw, fence=(c == 7))
            first = False
        xn_tr_mm[b] = p.cnt(s_mm)
        evxnT = p.cnt(s_dv)
        flush_store()

        project(bufA, AB["k"], dcols["k"], [(s_mm, invread[0])])
        project(bufB, AB["v"], dcols["v"], [(s_dv, prev_bufB_free[0])])
        normalize(bufA)
        normalize(bufB)
        top = lambda buf: buf[:, 0:HT_]
        bot = lambda buf: buf[:, HT_:2 * HT_]
        dvc(lambda E: E.tensor_mul(nsc1[:], top(bufA), top(bufB)))
        dvc(lambda E: E.tensor_mul(nsc2[:], bot(bufA), bot(bufB)))
        dvc(lambda E: E.tensor_copy(rstash[0:1, 0:TB], nsc1[0:1, 0:TB]))
        dvc(lambda E: E.tensor_sub(nsc1[:], nsc1[:], nsc2[:]))
        dvc(lambda E: E.tensor_copy(nsc1[0:1, 0:TB], rstash[0:1, 0:TB]))
        dvc(lambda E: E.tensor_mul(nsc2[:], top(bufA), bot(bufB)))
        dvc(lambda E: E.tensor_mul(top(bufB), bot(bufA), top(bufB)))
        dvc(lambda E: E.tensor_add(top(bufB), nsc2[:], top(bufB)))
        dvc(lambda E: E.tensor_mul(top(bufB)[0:1, 0:TB],
                                   bot(bufA)[0:1, 0:TB],
                                   bot(bufB)[0:1, 0:TB]))
        for c in range(8):
            src = nsc1[:, c * TB:(c + 1) * TB] if c < 4 else                 bufB[:, (c - 4) * TB:(c - 3) * TB]
            dst = bufA[:, c * TB:(c + 1) * TB]
            dvc(lambda E, s_=src, d_=dst, cc=ccar[:, c:c + 1]:
                E.tensor_tensor_scan(d_, s_, s_, cc, ALU.add, ALU.bypass))
            dvc(lambda E, d_=ccar[:, c:c + 1],
                s_=bufA[:, c * TB + TB - 1:c * TB + TB]:
                E.tensor_copy(d_, s_))
        scan_done = p.fence("vector", s_dv)
        evq = project(bufB, AB["q"], dcols["q"], [(s_dv, scan_done)])
        xn_consumed = p.cnt(s_mm)
        normalize(bufB)
        dvc(lambda E: E.tensor_mul(nsc1[:], top(bufA), top(bufB)),
            wait=[(s_mm, xn_consumed)])
        dvc(lambda E: E.tensor_mul(nsc2[:], bot(bufA), bot(bufB)))
        dvc(lambda E: E.tensor_add(nsc1[:], nsc1[:], nsc2[:]))
        dvc(lambda E: E.tensor_copy(zb[:, 0:HT_], nsc1[:]))
        dvc(lambda E: E.tensor_mul(nsc1[:], bot(bufA), top(bufB)))
        dvc(lambda E: E.tensor_mul(nsc2[:], top(bufA), bot(bufB)))
        dvc(lambda E: E.tensor_sub(nsc1[:], nsc1[:], nsc2[:]))
        dvc(lambda E: E.tensor_copy(zb[:, HT_:2 * HT_], nsc1[:]))
        dvc(lambda E: E.tensor_mul(nsc1[0:1, 0:TB], top(bufA)[0:1, 0:TB],
                                   top(bufB)[0:1, 0:TB]))
        dvc(lambda E: E.tensor_copy(zb[0:1, 0:TB], nsc1[0:1, 0:TB]))
        dvc(lambda E: E.tensor_mul(nsc1[0:1, 0:TB], bot(bufA)[0:1, 0:TB],
                                   bot(bufB)[0:1, 0:TB]))
        dvc(lambda E: E.tensor_copy(zb[0:1, HT_:HT_ + TB], nsc1[0:1, 0:TB]))
        zdone = p.fence("vector", s_dv)
        prev_bufB_free = [p.cnt(s_dv)]
        inv_ev = {}
        for tc in range(8):
            pr = ring[tc % 2]
            for frc in range(8):
                wl = []
                if frc == 0:
                    wl = [(s_dv, zdone)]
                    if tc >= 2:
                        wl.append((s_dv, inv_ev[tc - 2]))
                p.emit("tensor", lambda E, o=pr[:, 0:TB],
                       l=dinv[:, frc * D + tc * 128:frc * D + (tc + 1) * 128],
                       r=zb[:, frc * TB:(frc + 1) * TB],
                       st=(frc == 0), sp=(frc == 7):
                       E.matmul(o, l, r, start=st, stop=sp), wait=wl, inc=s_mm)
            gv = p.cnt(s_mm)
            p.emit("vector", lambda E, d=otf[:, tc * TB:(tc + 1) * TB],
                   sp=pr[:, 0:TB]: E.tensor_copy(d, sp),
                   wait=[(s_mm, gv), (s_dv, p.cnt(s_dv))], inc=s_dv)
            inv_ev[tc] = p.cnt(s_dv)
        inv_done = p.fence("vector", s_dv)

        for t in range(TPB):
            for half in range(2):
                srcs = [otf[:, (half * 4 + cc) * TB + t * 128:
                           (half * 4 + cc) * TB + (t + 1) * 128]
                        for cc in range(4)]
                transpose_group(p, com, tring, srcs,
                                ft[:, t * D + half * 512:t * D + (half + 1) * 512],
                                s_mm, s_dv, dwait=[(s_dv, inv_done)],
                                fence=(t == TPB - 1 and half == 1))
        tr_done = p.cnt(s_dv)
        invread[0] = p.cnt(s_mm)

        for t in range(TPB):
            gi = b * TPB + t
            ftt = ft[:, t * D:(t + 1) * D]
            p.emit("vector", lambda E, x=ftt, c=L.fstats_s[:, gi:gi + 1]:
                   E.tensor_reduce(c, x, AX.X, ALU.add),
                   wait=[(s_dv, tr_done)], inc=s_dv)
            p.emit("scalar", lambda E, x=ftt, c=L.fstats_q[:, gi:gi + 1]:
                   E.activation(zb[:, 0:D], x, AF.Square, accum_out=c),
                   wait=[(s_dv, tr_done), (s_ac, p.cnt(s_ac))], inc=s_ac)
        p.fence("vector", s_dv)
        p.fence("scalar", s_ac)
        ffv = ln_finalize(p, L.fstats_s, L.fstats_q, L.fstats_m, L.fstats_r,
                          b * TPB, (b + 1) * TPB,
                          [(s_dv, p.cnt(s_dv)), (s_ac, p.cnt(s_ac))],
                          s_dv, s_ac)
        for t in range(TPB):
            gi = b * TPB + t
            wl = [(s_dv, ffv)]
            if t == 0 and b >= 2:
                wl.append((s_st[bb], store_done[b - 2]))
            p.emit("vector", lambda E, d=L.xo[bb][:, t * D:(t + 1) * D],
                   f=ft[:, t * D:(t + 1) * D],
                   m=L.fstats_m[:, gi:gi + 1], r=L.fstats_r[:, gi:gi + 1]:
                   E.tensor_scalar(d, f, m, r, ALU.subtract, ALU.mult),
                   wait=wl, inc=s_dv)
            p.emit("vector", lambda E, d=L.xo[bb][:, t * D:(t + 1) * D],
                   x=L.xg[bb][:, t * D:(t + 1) * D]:
                   E.tensor_add(d, d, x), wait=[(s_dv, p.cnt(s_dv))], inc=s_dv)
        bapply_done[b] = p.fence("vector", s_dv)

        def mk_store(bv=bapply_done[b], b=b, bb=bb):
            settle = p.cnt(s_dv)
            for t in range(TPB):
                gi = b * TPB + t
                p.emit("sync", lambda E, d=y_ap[gi * 128:(gi + 1) * 128, :],
                       sr=L.xo[bb][:, t * D:(t + 1) * D]:
                       E.dma_start(d, sr),
                       wait=[(s_dv, max(bv, settle))] if t == 0 else (),
                       inc=s_st[bb], inck=16)
            store_done[b] = p.cnt(s_st[bb])
        pending_store[0] = mk_store
        store_done[b] = p.cnt(s_st[bb]) + 16 * TPB

    flush_store()
    p.emit("sync", lambda E: E.nop(), wait=[(s_st[0], p.cnt(s_st[0]))])
    p.emit("sync", lambda E: E.nop(), wait=[(s_st[1], p.cnt(s_st[1]))])
    p.barrier()


def build_l1_attn_program():
    nc = bass.Bass("TRN2", target_bir_lowering=False)
    x = nc.dram_tensor("x", [S, D], F32, kind="ExternalInput").ap()
    y = nc.dram_tensor("y", [S, D], F32, kind="ExternalOutput").ap()
    shapes = {"Wq": [D, D], "Wk": [D, D], "Wv": [D, D],
              "bq": [1, D], "bk": [1, D], "bv": [1, D]}
    w = {k: nc.dram_tensor(k, v, F32, kind="ExternalInput").ap()
         for k, v in shapes.items()}
    capz = _declare_const_inputs(nc, ["ident", "dft_f", "dft_i"])
    w["dft_f"] = capz["dft_f"]
    w["dft_i"] = capz["dft_i"]
    with ExitStack() as stack:
        p = P(nc, stack)
        com = Common(p, capz["ident"])
        with p.scope():
            build_l1_attn(p, com, x, y, w)
        p.flush()
    return nc


def run_l1_attn(x_full, w_np):
    nc = build_l1_attn_program()
    in_maps = []
    for c in range(NCORES):
        m = {"x": np.ascontiguousarray(x_full[c])}
        for k in ("Wq", "Wk", "Wv"):
            m[k] = np.ascontiguousarray(w_np[k])
        for k in ("bq", "bk", "bv"):
            m[k] = np.ascontiguousarray(w_np[k].reshape(1, -1))
        m.update(_const_in_map(["ident", "dft_f", "dft_i"]))
        in_maps.append(m)
    res = run_bass_kernel_spmd(nc, in_maps, core_ids=list(range(NCORES)))
    return np.stack([r["y"] for r in res.results])


def build_l0_attn(p, com, x_ap, y_ap, w, xnq_dram, ctx_dram, pref="t0"):
    s_ld = [p.sem(pref + "_ld0"), p.sem(pref + "_ld1")]
    s_st = [p.sem(pref + "_st0"), p.sem(pref + "_st1")]
    s_cx = p.sem(pref + "_cx")
    s_xq = [p.sem(pref + "_xq0"), p.sem(pref + "_xq1")]
    s_xw = [p.sem(pref + "_xw0"), p.sem(pref + "_xw1")]
    s_dv = p.sem(pref + "_dv")
    s_ac = p.sem(pref + "_ac")
    s_mm = p.sem(pref + "_mm")
    s_wld = p.sem(pref + "_wld")
    s_wl2 = [p.sem(pref + "_wl0"), p.sem(pref + "_wl1")]
    s_wcv = p.sem(pref + "_wcv")

    def dvc(fn, wait=(), inc=s_dv):
        p.emit("vector", fn, wait=list(wait) + [(s_dv, p.cnt(s_dv))], inc=inc)

    tring = TRing(p, pref)
    ps_sc = p.ps(pref + "_sc", [128, 2048], F32)
    ps_cxA = p.ps(pref + "_cxA", [128, 512], F32)
    ps_cxB = p.ps(pref + "_cxB", [128, 512], F32)

    wstack = ExitStack()
    brow = wstack.enter_context(p.nc.sbuf_tensor(pref + "_brow", [1, 3 * D],
                                                 F32, side="right"))
    for j, key in enumerate(("bq", "bk", "bv")):
        p.emit("sync", lambda E, d=brow[0:1, j * D:(j + 1) * D], sr=w[key]:
               E.dma_start(d, sr), inc=s_wld, inck=16)
    p.dma_barrier(s_wld)
    p.barrier()
    bqc = row_to_cols(p, com, pref + "_bqc",
                      lambda c: brow[0:1, c * 128:(c + 1) * 128], 8,
                      tring.banks[0][:, 0:8], s_mm, s_dv)
    p.barrier()
    bkc = row_to_cols(p, com, pref + "_bkc",
                      lambda c: brow[0:1, D + c * 128:D + (c + 1) * 128], 8,
                      tring.banks[0][:, 0:8], s_mm, s_dv)
    p.barrier()
    bvc = row_to_cols(p, com, pref + "_bvc",
                      lambda c: brow[0:1, 2 * D + c * 128:2 * D + (c + 1) * 128],
                      8, tring.banks[0][:, 0:8], s_mm, s_dv)
    p.barrier()
    wstack.close()

    L = LNBufs(p, pref, sqd=False, xg=False, xo=False)
    sqdum = p.sb(pref + "_sqdum", [128, D], BF16)
    HW_ = 65

    with p.scope():
        L.xg = [p.sb(pref + f"_xga{i}", [128, TPB * D], F32) for i in range(2)]
        blkw = [p.sb(pref + f"_blkw{i}", [128, 8 * TB], BF16) for i in range(2)]
        grp_apply = {}
        blk_st = {}
        for g in range(8):
            gb = g % 2
            for t in range(TPB):
                gi = g * TPB + t
                wl = []
                if g >= 2:
                    wl.append((s_dv, grp_apply[g - 2]))
                p.emit("sync", lambda E, d=L.xg[gb][:, t * D:(t + 1) * D],
                       sr=x_ap[gi * 128:(gi + 1) * 128, :]: E.dma_start(d, sr),
                       wait=wl, inc=s_ld[gb], inck=16)
            ldv = p.cnt(s_ld[gb])
            for t in range(TPB):
                gi = g * TPB + t
                xt = L.xg[gb][:, t * D:(t + 1) * D]
                p.emit("vector", lambda E, x=xt, c=L.stats_s[:, gi:gi + 1]:
                       E.tensor_reduce(c, x, AX.X, ALU.add),
                       wait=[(s_ld[gb], ldv)], inc=s_dv)
                p.emit("scalar", lambda E, x=xt, c=L.stats_q[:, gi:gi + 1]:
                       E.activation(sqdum[:], x, AF.Square, accum_out=c),
                       wait=[(s_ld[gb], ldv), (s_ac, p.cnt(s_ac))], inc=s_ac)
            p.fence("vector", s_dv)
            p.fence("scalar", s_ac)
            finv = ln_finalize(p, L.stats_s, L.stats_q, L.stats_m, L.stats_r,
                               g * TPB, (g + 1) * TPB,
                               [(s_dv, p.cnt(s_dv)), (s_ac, p.cnt(s_ac))],
                               s_dv, s_ac)
            for t in range(TPB):
                gi = g * TPB + t
                dvc(lambda E, xt=L.xg[gb][:, t * D:(t + 1) * D],
                    m=L.stats_m[:, gi:gi + 1], r=L.stats_r[:, gi:gi + 1]:
                    E.tensor_scalar(xt, xt, m, r, ALU.subtract, ALU.mult),
                    wait=[(s_dv, finv)])
            apply_done = p.fence("vector", s_dv)
            for c in range(8):
                srcs = [L.xg[gb][:, t * D + c * 128:t * D + (c + 1) * 128]
                        for t in range(TPB)]
                dw = [(s_dv, apply_done)] if c == 0 else []
                if c == 0 and g >= 2:
                    dw.append((s_xw[gb], blk_st.get(g - 2, 0)))
                transpose_group(p, com, tring, srcs,
                                blkw[gb][:, c * TB:(c + 1) * TB], s_mm, s_dv,
                                dwait=dw, fence=(c == 7))
            grp_apply[g] = p.cnt(s_dv)
            p.emit("sync", lambda E, g=g, gb=gb: E.dma_start(
                xnq_dram[:, :, g * TB:(g + 1) * TB].rearrange(
                    "c q w -> q c w"),
                blkw[gb].ap().rearrange("q (c w) -> q c w", w=TB)),
                   wait=[(s_dv, grp_apply[g])], inc=s_xw[gb], inck=16)
            blk_st[g] = p.cnt(s_xw[gb])
        p.emit("sync", lambda E: E.nop(), wait=[(s_xw[0], p.cnt(s_xw[0]))])
        p.emit("sync", lambda E: E.nop(), wait=[(s_xw[1], p.cnt(s_xw[1]))])
        p.barrier()

    with p.scope():
        blk = [p.sb(pref + f"_blk{i}", [128, 8 * TB], BF16) for i in range(2)]
        vw = p.sb(pref + "_vw", [128, 32 * 16 * HW_], BF16)
        vtc = p.sb(pref + "_vtc", [128, TB], F32)
        wstack = ExitStack()
        stg = wstack.enter_context(p.nc.sbuf_tensor(pref + "_stg", [128, 2 * D],
                                                    F32, side="right"))
        wvb = wstack.enter_context(p.nc.sbuf_tensor(pref + "_wvb", [128, 8 * D],
                                                    BF16, side="right"))
        base_cv = p.cnt(s_wcv)
        for k in range(8):
            bpar = k % 2
            wait = [(s_wcv, base_cv + k - 1)] if k >= 2 else [(s_wcv, base_cv)]
            p.emit("sync", lambda E, d=stg[:, bpar * D:(bpar + 1) * D],
                   sr=w["wv"][k * 128:(k + 1) * 128, :]: E.dma_start(d, sr),
                   wait=wait, inc=s_wl2[bpar], inck=16)
            p.emit("vector", lambda E, d=wvb[:, k * D:(k + 1) * D],
                   sr=stg[:, bpar * D:(bpar + 1) * D]: E.tensor_copy(d, sr),
                   wait=[(s_wl2[bpar], p.cnt(s_wl2[bpar]))], inc=s_wcv)
        p.fence("vector", s_wcv)
        p.emit("gpsimd", lambda E: E.memset(vw[:], 1.0))
        p.barrier()
        vt_mm = [0]
        vtc_ac = [0]
        for tb in range(8):
            gb = tb % 2
            wl = [(s_mm, vt_mm[0])] if tb >= 2 else []
            p.emit("sync", lambda E, gb=gb, tb=tb: E.dma_start(
                blk[gb].ap().rearrange("q (c w) -> q c w", w=TB),
                xnq_dram[:, :, tb * TB:(tb + 1) * TB].rearrange(
                    "c q w -> q c w")),
                   wait=wl, inc=s_xq[gb], inck=16)
            ldq = p.cnt(s_xq[gb])
            for dc_ in range(8):
                psm = ps_sc[:, (dc_ % 2) * 512:(dc_ % 2) * 512 + 512]
                for c in range(8):
                    wl = []
                    if c == 0:
                        wl = [(s_xq[gb], ldq), (s_ac, vtc_ac[0])]
                    p.emit("tensor", lambda E, o=psm,
                           l=wvb[:, c * D + dc_ * 128:c * D + (dc_ + 1) * 128],
                           r=blk[gb][:, c * TB:(c + 1) * TB],
                           st=(c == 0), sp=(c == 7):
                           E.matmul(o, l, r, start=st, stop=sp),
                           wait=wl, inc=s_mm)
                gv = p.cnt(s_mm)
                p.emit("scalar", lambda E, psm=psm, dc_=dc_:
                       E.activation(vtc[:], psm, AF.Identity,
                                    bias=bvc[:, dc_:dc_ + 1]),
                       wait=[(s_mm, gv), (s_ac, p.cnt(s_ac))], inc=s_ac)
                vtc_ac[0] = p.fence("scalar", s_ac)
                for t4 in range(4):
                    i, bank, freev = tring.next()
                    p.emit("tensor", lambda E, bank=bank, t4=t4:
                           E.transpose(bank[:, 0:128],
                                       vtc[:, t4 * 128:(t4 + 1) * 128],
                                       com.ident_f[:]),
                           wait=[(s_ac, vtc_ac[0]), (s_dv, freev)], inc=s_mm)
                    gv2 = p.cnt(s_mm)
                    kt = tb * 4 + t4
                    p.emit("vector", lambda E, kt=kt, dc_=dc_, bank=bank:
                           E.tensor_copy(
                               vw.ap().rearrange("q (t h e) -> q t h e",
                                                 h=16, e=HW_)
                               [:, kt, dc_ * 2:dc_ * 2 + 2, 0:64],
                               bank[:, 0:128].rearrange("q (h e) -> q h e",
                                                        e=64)),
                           wait=[(s_mm, gv2), (s_dv, p.cnt(s_dv))], inc=s_dv)
                    tring.free[i] = p.cnt(s_dv)
            vt_mm[0] = p.cnt(s_mm)
        p.fence("vector", s_dv)
        p.barrier()
        wstack.close()

        ropeC = p.sb(pref + "_ropec", [128, S], BF16)
        ropeS = p.sb(pref + "_ropes", [128, S], BF16)
        wstack = ExitStack()
        stg = wstack.enter_context(p.nc.sbuf_tensor(pref + "_stg3",
                                                    [128, S], F32,
                                                    side="right"))
        for j, (key, dst) in enumerate((("rope_c", ropeC), ("rope_s", ropeS))):
            wl = [] if j == 0 else [(s_wcv, p.cnt(s_wcv))]
            p.emit("sync", lambda E, d=stg[:], sr=w[key]:
                   E.dma_start(d, sr), wait=wl, inc=s_wl2[j], inck=16)
            p.emit("vector", lambda E, d=dst[:], sr=stg[:]:
                   E.tensor_copy(d, sr), wait=[(s_wl2[j], p.cnt(s_wl2[j]))],
                   inc=s_wcv)
            p.fence("vector", s_wcv)
        p.barrier()
        wstack.close()

        wqp = p.sb(pref + "_wqp", [128, 8 * 128], BF16)
        wkp = p.sb(pref + "_wkp", [128, 8 * 128], BF16)
        qt = p.sb(pref + "_qt", [128, S], BF16)
        kt_ = p.sb(pref + "_kt", [128, S], BF16)
        swb = p.sb(pref + "_swb", [128, S], BF16)
        pt = p.sb(pref + "_pt", [128, 2048], BF16)
        cstage = p.sb(pref + "_cstage", [128, 1024], F32)
        ctout = p.sb(pref + "_ctout", [128, TPB * 128], F32)
        rcl = p.sb(pref + "_rcl", [128, 2], F32)
        stgp = p.sb(pref + "_stgp", [128, 256], F32)

        last_exp = [0]
        last_ctev = [0]
        last_projmm = [vt_mm[0]]
        last_ctx_st = [0]
        rope_dv = [0]

        for pair in range(8):
            base_cv = p.cnt(s_wcv)
            for j, (wkey, dst) in enumerate((("wq", wqp), ("wk", wkp))):
                for k in range(8):
                    bpar = k % 2
                    idx = j * 8 + k
                    wait = [(s_wcv, base_cv + idx - 1)] if idx >= 2 else \
                        [(s_wcv, base_cv)]
                    p.emit("sync", lambda E,
                           d=stgp[:, bpar * 128:(bpar + 1) * 128],
                           sr=w[wkey][k * 128:(k + 1) * 128,
                                      pair * 128:(pair + 1) * 128]:
                           E.dma_start(d, sr), wait=wait, inc=s_wl2[bpar],
                           inck=16)
                    p.emit("vector", lambda E,
                           d=dst[:, k * 128:(k + 1) * 128],
                           sr=stgp[:, bpar * 128:(bpar + 1) * 128]:
                           E.tensor_copy(d, sr),
                           wait=[(s_wl2[bpar], p.cnt(s_wl2[bpar])),
                                 (s_mm, last_projmm[0]),
                                 (s_dv, p.cnt(s_dv))], inc=s_wcv)
            wpair_done = p.fence("vector", s_wcv)
            qk_ev = {}
            for tb in range(8):
                gb = tb % 2
                wl = [(s_mm, qk_ev.get(("mm", tb - 2), last_projmm[0]))]
                p.emit("sync", lambda E, gb=gb, tb=tb: E.dma_start(
                    blk[gb].ap().rearrange("q (c w) -> q c w", w=TB),
                    xnq_dram[:, :, tb * TB:(tb + 1) * TB].rearrange(
                        "c q w -> q c w")),
                       wait=wl, inc=s_xq[gb], inck=16)
                ldq = p.cnt(s_xq[gb])
                for j, (wsb_, dst, bcl) in enumerate(((wqp, qt, bqc),
                                                      (wkp, kt_, bkc))):
                    pr = tring.banks[j]
                    for c in range(8):
                        wl = []
                        if c == 0:
                            wl = [(s_xq[gb], ldq), (s_wcv, wpair_done),
                                  (s_dv, qk_ev.get((j, tb - 1),
                                                   tring.free[j]))]
                        p.emit("tensor", lambda E, o=pr[:, 0:TB],
                               l=wsb_[:, c * 128:(c + 1) * 128],
                               r=blk[gb][:, c * TB:(c + 1) * TB],
                               st=(c == 0), sp=(c == 7):
                               E.matmul(o, l, r, start=st, stop=sp),
                               wait=wl, inc=s_mm)
                    gv = p.cnt(s_mm)
                    p.emit("vector", lambda E, d=dst[:, tb * TB:(tb + 1) * TB],
                           sp=pr[:, 0:TB], bcl=bcl[:, pair:pair + 1]:
                           E.tensor_scalar_add(d, sp, bcl),
                           wait=[(s_mm, gv), (s_dv, p.cnt(s_dv))], inc=s_dv)
                    qk_ev[(j, tb)] = p.cnt(s_dv)
                qk_ev[("mm", tb)] = p.cnt(s_mm)
            last_projmm[0] = p.cnt(s_mm)
            tring.free[0] = qk_ev[(0, 7)]
            tring.free[1] = qk_ev[(1, 7)]
            for dst in (qt, kt_):
                for (o_, i_) in ((0, 32), (32, 0), (64, 96), (96, 64)):
                    dvc(lambda E, o_=o_, i_=i_, dst=dst:
                        E.tensor_copy(swb[o_:o_ + 32, :], dst[i_:i_ + 32, :]))
                dvc(lambda E: E.tensor_mul(swb[:], swb[:], ropeS[:]))
                dvc(lambda E, dst=dst: E.tensor_mul(dst[:], dst[:], ropeC[:]))
                dvc(lambda E, dst=dst: E.tensor_add(dst[:], dst[:], swb[:]))
            rope_dv[0] = p.fence("vector", s_dv)

            for qb in range(8):
                for kc2 in range(16):
                    for sub in range(2):
                        kc = kc2 * 2 + sub
                        for hh, base in enumerate((0, 64)):
                            wl = []
                            if hh == 0 and sub == 0:
                                wl = [(s_ac, last_exp[0]),
                                      (s_dv, rope_dv[0])]
                            p.emit("tensor", lambda E,
                                   o=ps_sc[:, hh * 1024 + sub * 512:
                                           hh * 1024 + (sub + 1) * 512],
                                   l=kt_[base:base + 64,
                                         kc * 128:(kc + 1) * 128],
                                   r=qt[base:base + 64,
                                        qb * TB:(qb + 1) * TB],
                                   tp=(base, 0):
                                   E.matmul(o, l, r, start=True, stop=True,
                                            tile_position=tp),
                                   wait=wl, inc=s_mm)
                    gv = p.cnt(s_mm)
                    p.emit("scalar", lambda E: E.activation(
                        pt[:], ps_sc[:, 0:2048], AF.Exp, scale=0.125),
                           wait=[(s_mm, gv), (s_ac, p.cnt(s_ac))], inc=s_ac)
                    last_exp[0] = p.cnt(s_ac)
                    for sub in range(2):
                        kc = kc2 * 2 + sub
                        for hh, ps_cx in enumerate((ps_cxA, ps_cxB)):
                            h = pair * 2 + hh
                            wl = []
                            if hh == 0 and sub == 0:
                                wl = [(s_ac, last_exp[0])]
                                if kc2 == 0:
                                    wl.append((s_dv, last_ctev[0]))
                            p.emit("tensor", lambda E, o=ps_cx[0:65, 0:TB],
                                   l=vw[:, kc * 1040 + h * HW_:
                                        kc * 1040 + (h + 1) * HW_],
                                   r=pt[:, hh * 1024 + sub * 512:
                                        hh * 1024 + (sub + 1) * 512],
                                   st=(kc == 0), sp=(kc == 31):
                                   E.matmul(o, l, r, start=st, stop=sp),
                                   wait=wl, inc=s_mm)
                pv_gv = p.cnt(s_mm)
                for hh, ps_cx in enumerate((ps_cxA, ps_cxB)):
                    p.emit("vector", lambda E, hh=hh, ps_cx=ps_cx:
                           E.tensor_copy(cstage[0:65, hh * 512:hh * 512 + 512],
                                         ps_cx[0:65, 0:TB]),
                           wait=[(s_mm, pv_gv), (s_dv, p.cnt(s_dv)),
                                 (s_cx, last_ctx_st[0])], inc=s_dv)
                cs_dv = p.fence("vector", s_dv)
                last_ctev[0] = cs_dv
                for t4 in range(TPB):
                    i, bank, freev = tring.next()
                    for hh in range(2):
                        p.emit("tensor", lambda E, hh=hh, t4=t4, bank=bank:
                               E.transpose(bank[:, hh * 65:hh * 65 + 65],
                                           cstage[0:65,
                                                  hh * 512 + t4 * 128:
                                                  hh * 512 + (t4 + 1) * 128],
                                           com.ident_f[0:65, 0:65]),
                               wait=[(s_dv, max(cs_dv, freev))] if hh == 0
                               else [], inc=s_mm)
                    gv2 = p.cnt(s_mm)
                    dvc(lambda E, bank=bank: E.reciprocal(
                        rcl[:, 0:1], bank[:, 64:65]), wait=[(s_mm, gv2)])
                    dvc(lambda E, bank=bank: E.reciprocal(
                        rcl[:, 1:2], bank[:, 129:130]))
                    for hh in range(2):
                        dvc(lambda E, hh=hh, t4=t4, bank=bank:
                            E.tensor_scalar_mul(
                                ctout[:, t4 * 128 + hh * 64:
                                      t4 * 128 + (hh + 1) * 64],
                                bank[:, hh * 65:hh * 65 + 64],
                                rcl[:, hh:hh + 1]))
                    tring.free[i] = p.cnt(s_dv)
                ct_fence = p.fence("vector", s_dv)
                dst = ctx_dram[qb * TB:(qb + 1) * TB,
                               pair * 128:(pair + 1) * 128]
                p.emit("sync", lambda E, d=dst: E.dma_start(
                    d.rearrange("(t q) c -> q t c", q=128),
                    ctout.ap().rearrange("q (t c) -> q t c", c=128)),
                       wait=[(s_dv, ct_fence)], inc=s_cx, inck=16)
                last_ctx_st[0] = p.cnt(s_cx)
        p.emit("sync", lambda E: E.nop(), wait=[(s_cx, p.cnt(s_cx))])
        p.barrier()

    L.xg = [p.sb(pref + f"_xgb{i}", [128, TPB * D], F32) for i in range(2)]
    cg = [p.sb(pref + f"_cg{i}", [128, TPB * D], F32) for i in range(2)]
    xo = [p.sb(pref + f"_xo{i}", [128, TPB * D], F32) for i in range(2)]
    bapply = {}
    st_done = {}
    for g in range(8):
        gb = g % 2
        for t in range(TPB):
            gi = g * TPB + t
            wl = []
            if g >= 2:
                wl.append((s_dv, bapply[g - 2]))
            p.emit("sync", lambda E, d=L.xg[gb][:, t * D:(t + 1) * D],
                   sr=x_ap[gi * 128:(gi + 1) * 128, :]: E.dma_start(d, sr),
                   wait=wl, inc=s_ld[gb], inck=16)
            p.emit("sync", lambda E, d=cg[gb][:, t * D:(t + 1) * D],
                   sr=ctx_dram[gi * 128:(gi + 1) * 128, :]: E.dma_start(d, sr),
                   wait=[], inc=s_ld[gb], inck=16)
        ldv = p.cnt(s_ld[gb])
        for t in range(TPB):
            gi = g * TPB + t
            ct = cg[gb][:, t * D:(t + 1) * D]
            p.emit("vector", lambda E, x=ct, c=L.fstats_s[:, gi:gi + 1]:
                   E.tensor_reduce(c, x, AX.X, ALU.add),
                   wait=[(s_ld[gb], ldv)], inc=s_dv)
            p.emit("scalar", lambda E, x=ct, c=L.fstats_q[:, gi:gi + 1]:
                   E.activation(sqdum[:], x, AF.Square, accum_out=c),
                   wait=[(s_ld[gb], ldv), (s_ac, p.cnt(s_ac))], inc=s_ac)
        p.fence("vector", s_dv)
        p.fence("scalar", s_ac)
        ffv = ln_finalize(p, L.fstats_s, L.fstats_q, L.fstats_m, L.fstats_r,
                          g * TPB, (g + 1) * TPB,
                          [(s_dv, p.cnt(s_dv)), (s_ac, p.cnt(s_ac))],
                          s_dv, s_ac)
        for t in range(TPB):
            gi = g * TPB + t
            wl = [(s_dv, ffv)]
            if t == 0 and g >= 2:
                wl.append((s_st[gb], st_done[g - 2]))
            p.emit("vector", lambda E, d=xo[gb][:, t * D:(t + 1) * D],
                   f=cg[gb][:, t * D:(t + 1) * D],
                   m=L.fstats_m[:, gi:gi + 1], r=L.fstats_r[:, gi:gi + 1]:
                   E.tensor_scalar(d, f, m, r, ALU.subtract, ALU.mult),
                   wait=wl, inc=s_dv)
            p.emit("vector", lambda E, d=xo[gb][:, t * D:(t + 1) * D],
                   x=L.xg[gb][:, t * D:(t + 1) * D]:
                   E.tensor_add(d, d, x), wait=[(s_dv, p.cnt(s_dv))], inc=s_dv)
        bapply[g] = p.fence("vector", s_dv)
        for t in range(TPB):
            gi = g * TPB + t
            p.emit("sync", lambda E, d=y_ap[gi * 128:(gi + 1) * 128, :],
                   sr=xo[gb][:, t * D:(t + 1) * D]: E.dma_start(d, sr),
                   wait=[(s_dv, bapply[g])] if t == 0 else (),
                   inc=s_st[gb], inck=16)
        st_done[g] = p.cnt(s_st[gb])
    p.emit("sync", lambda E: E.nop(), wait=[(s_st[0], p.cnt(s_st[0]))])
    p.emit("sync", lambda E: E.nop(), wait=[(s_st[1], p.cnt(s_st[1]))])
    p.barrier()


def build_l0_attn_program():
    nc = bass.Bass("TRN2", target_bir_lowering=False)
    x = nc.dram_tensor("x", [S, D], F32, kind="ExternalInput").ap()
    y = nc.dram_tensor("y", [S, D], F32, kind="ExternalOutput").ap()
    shapes = {"wq": [D, D], "wk": [D, D], "wv": [D, D],
              "bq": [1, D], "bk": [1, D], "bv": [1, D]}
    w = {k: nc.dram_tensor(k, v, F32, kind="ExternalInput").ap()
         for k, v in shapes.items()}
    capz = _declare_const_inputs(nc, ["ident", "rope_c", "rope_s"])
    w["rope_c"] = capz["rope_c"]
    w["rope_s"] = capz["rope_s"]
    xnq = nc.dram_tensor("xnq", [8, 128, S], BF16, kind="Internal").ap()
    ctxd = nc.dram_tensor("ctxd", [S, D], F32, kind="Internal").ap()
    with ExitStack() as stack:
        p = P(nc, stack)
        com = Common(p, capz["ident"])
        with p.scope():
            build_l0_attn(p, com, x, y, w, xnq, ctxd)
        p.flush()
    return nc


def run_l0_attn(x_full, w_np):
    nc = build_l0_attn_program()
    perm = rope_perm()
    in_maps = []
    for c in range(NCORES):
        m = {"x": np.ascontiguousarray(x_full[c]),
             "wq": np.ascontiguousarray(w_np["Wq"][:, perm]),
             "wk": np.ascontiguousarray(w_np["Wk"][:, perm]),
             "wv": np.ascontiguousarray(w_np["Wv"]),
             "bq": np.ascontiguousarray(w_np["bq"][perm].reshape(1, -1)),
             "bk": np.ascontiguousarray(w_np["bk"][perm].reshape(1, -1)),
             "bv": np.ascontiguousarray(w_np["bv"].reshape(1, -1))}
        m.update(_const_in_map(["ident", "rope_c", "rope_s"]))
        in_maps.append(m)
    res = run_bass_kernel_spmd(nc, in_maps, core_ids=list(range(NCORES)))
    return np.stack([r["y"] for r in res.results])


_CONSTS = host_constants()


def _declare_const_inputs(nc, names):
    aps = {}
    for n in names:
        arr = _CONSTS[n]
        aps[n] = nc.dram_tensor("c_" + n, list(arr.shape), F32,
                                kind="ExternalInput").ap()
    return aps


def _const_in_map(names):
    return {"c_" + n: _CONSTS[n] for n in names}


def build_l0_ffn_program():
    nc = bass.Bass("TRN2", target_bir_lowering=False)
    x = nc.dram_tensor("x", [S, D], F32, kind="ExternalInput").ap()
    y = nc.dram_tensor("y", [S, D], F32, kind="ExternalOutput").ap()
    w = {
        "U1": nc.dram_tensor("u1", [D, 256], F32, kind="ExternalInput").ap(),
        "V1": nc.dram_tensor("v1", [256, 4096], F32, kind="ExternalInput").ap(),
        "b1": nc.dram_tensor("b1", [1, 4096], F32, kind="ExternalInput").ap(),
        "U2": nc.dram_tensor("u2", [4096, 256], F32, kind="ExternalInput").ap(),
        "V2": nc.dram_tensor("v2", [256, 1024], F32, kind="ExternalInput").ap(),
        "b2": nc.dram_tensor("b2", [1, 1024], F32, kind="ExternalInput").ap(),
    }
    capz = _declare_const_inputs(nc, ["ident"])
    with ExitStack() as stack:
        p = P(nc, stack)
        com = Common(p, capz["ident"])
        with p.scope():
            build_l0_ffn(p, com, x, y, w)
        p.flush()
    return nc


def run_l0_ffn(x_full, w_np):
    nc = build_l0_ffn_program()
    in_maps = []
    for c in range(NCORES):
        m = {"x": np.ascontiguousarray(x_full[c]),
             "u1": w_np["U1"], "v1": w_np["V1"],
             "b1": w_np["b1"].reshape(1, -1),
             "u2": w_np["U2"], "v2": w_np["V2"],
             "b2": w_np["b2"].reshape(1, -1)}
        m.update(_const_in_map(["ident"]))
        in_maps.append(m)
    res = run_bass_kernel_spmd(nc, in_maps, core_ids=list(range(NCORES)))
    return np.stack([r["y"] for r in res.results])




def build_full_program():
    nc = bass.Bass("TRN2", target_bir_lowering=False)
    x = nc.dram_tensor("x", [S, D], F32, kind="ExternalInput").ap()
    y = nc.dram_tensor("y", [S, D], F32, kind="ExternalOutput").ap()
    sc1 = nc.dram_tensor("sc1", [S, D], F32, kind="Internal").ap()
    sc2 = nc.dram_tensor("sc2", [S, D], F32, kind="Internal").ap()
    sc3 = nc.dram_tensor("sc3", [S, D], F32, kind="Internal").ap()
    xnq = nc.dram_tensor("xnq", [8, 128, S], BF16, kind="Internal").ap()
    ctxd = nc.dram_tensor("ctxd", [S, D], F32, kind="Internal").ap()

    wA = {k: nc.dram_tensor("A" + k, v, F32, kind="ExternalInput").ap()
          for k, v in (("wq", [D, D]), ("wk", [D, D]), ("wv", [D, D]),
                       ("bq", [1, D]), ("bk", [1, D]), ("bv", [1, D]))}
    wF = {k: nc.dram_tensor("F" + k, v, F32, kind="ExternalInput").ap()
          for k, v in (("U1", [D, 256]), ("V1", [256, 4096]),
                       ("b1", [1, 4096]), ("U2", [4096, 256]),
                       ("V2", [256, D]), ("b2", [1, D]))}
    wB = {k: nc.dram_tensor("B" + k, v, F32, kind="ExternalInput").ap()
          for k, v in (("Wq", [D, D]), ("Wk", [D, D]), ("Wv", [D, D]),
                       ("bq", [1, D]), ("bk", [1, D]), ("bv", [1, D]))}
    wG = {k: nc.dram_tensor("G" + k, v, F32, kind="ExternalInput").ap()
          for k, v in (("cn_w1", [D, D]), ("cn_w2", [D, 512]),
                       ("cn_w3", [512, 256]), ("cn_w4", [256, 1]),
                       ("gc_w1", [D, 512]), ("gc_w2", [512, 256]),
                       ("gc_w3", [256, 1]), ("cn_b1", [1, D]),
                       ("cn_b2", [1, 512]), ("cn_b3", [1, 256]),
                       ("cn_b4", [1, 1]), ("gc_b1", [1, 512]),
                       ("gc_b2", [1, 256]), ("gc_b3", [1, 1]),
                       ("ar", [1, 1]))}
    capz = _declare_const_inputs(nc, ["ident", "rope_c", "rope_s",
                                      "dft_f", "dft_i"])
    wA["rope_c"] = capz["rope_c"]
    wA["rope_s"] = capz["rope_s"]
    wB["dft_f"] = capz["dft_f"]
    wB["dft_i"] = capz["dft_i"]

    with ExitStack() as stack:
        p = P(nc, stack)
        com = Common(p, capz["ident"])
        with p.scope():
            build_l0_attn(p, com, x, sc1, wA, xnq, ctxd)
        with p.scope():
            build_l0_ffn(p, com, sc1, sc2, wF)
        with p.scope():
            build_l1_attn(p, com, sc2, sc3, wB)
        with p.scope():
            build_l1_ffn(p, com, sc3, y, wG)
        p.flush()
    return nc


_FUSED_NC = [None]


def kernel_fused(**inputs):
    if _FUSED_NC[0] is None:
        _FUSED_NC[0] = build_full_program()
    nc = _FUSED_NC[0]
    perm = rope_perm()
    f32 = lambda a: np.ascontiguousarray(np.asarray(a, np.float32))
    base = {}
    base["Awq"] = f32(inputs["L0_attn_Wq"])[:, perm]
    base["Awk"] = f32(inputs["L0_attn_Wk"])[:, perm]
    base["Awv"] = f32(inputs["L0_attn_Wv"])
    base["Abq"] = f32(inputs["L0_attn_bq"])[perm].reshape(1, -1)
    base["Abk"] = f32(inputs["L0_attn_bk"])[perm].reshape(1, -1)
    base["Abv"] = f32(inputs["L0_attn_bv"]).reshape(1, -1)
    for k in ("U1", "V1", "b1", "U2", "V2", "b2"):
        v = f32(inputs["L0_ffn_" + k])
        base["F" + k] = v.reshape(1, -1) if v.ndim == 1 else v
    for k in ("Wq", "Wk", "Wv", "bq", "bk", "bv"):
        v = f32(inputs["L1_attn_" + k])
        base["B" + k] = v.reshape(1, -1) if v.ndim == 1 else v
    for k in ("cn_w1", "cn_w2", "cn_w3", "cn_w4", "gc_w1", "gc_w2", "gc_w3",
              "cn_b1", "cn_b2", "cn_b3", "cn_b4", "gc_b1", "gc_b2", "gc_b3"):
        v = f32(inputs["L1_ffn_" + k])
        base["G" + k] = v.reshape(1, -1) if v.ndim == 1 else v
    base["Gar"] = f32(inputs["L1_ffn_adaptive_range"]).reshape(1, 1)
    for k, v in _CONSTS.items():
        base["c_" + k] = v
    xin = f32(inputs["L0_attn_hidden_states"])
    in_maps = []
    for c in range(NCORES):
        m = dict(base)
        m["x"] = np.ascontiguousarray(xin[c])
        in_maps.append(m)
    res = run_bass_kernel_spmd(nc, in_maps, core_ids=list(range(NCORES)))
    return np.ascontiguousarray(
        np.stack([r["y"] for r in res.results]).astype(np.float32))


def kernel(**inputs):
    return kernel_fused(**inputs)


def kernel_multi(**inputs):
    x = np.ascontiguousarray(np.asarray(inputs["L0_attn_hidden_states"],
                                        np.float32))
    wa = {"Wq": inputs["L0_attn_Wq"], "Wk": inputs["L0_attn_Wk"],
          "Wv": inputs["L0_attn_Wv"], "bq": inputs["L0_attn_bq"],
          "bk": inputs["L0_attn_bk"], "bv": inputs["L0_attn_bv"]}
    wa = {k: np.asarray(v, np.float32) for k, v in wa.items()}
    x = run_l0_attn(x, wa)

    wf = {k: np.asarray(inputs["L0_ffn_" + k], np.float32)
          for k in ("U1", "V1", "b1", "U2", "V2", "b2")}
    x = run_l0_ffn(x, wf)

    wb = {k: np.asarray(inputs["L1_attn_" + k], np.float32)
          for k in ("Wq", "Wk", "Wv", "bq", "bk", "bv")}
    x = run_l1_attn(x, wb)

    wg = {k: np.asarray(inputs["L1_ffn_" + k], np.float32)
          for k in ("cn_w1", "cn_b1", "cn_w2", "cn_b2", "cn_w3", "cn_b3",
                    "cn_w4", "cn_b4", "gc_w1", "gc_b1", "gc_w2", "gc_b2",
                    "gc_w3", "gc_b3")}
    wg["ar"] = np.asarray(inputs["L1_ffn_adaptive_range"], np.float32)
    x = run_l1_ffn(x, wg)
    return np.ascontiguousarray(x.astype(np.float32))
